# version 07
# baseline (speedup 1.0000x reference)
import numpy as np
import ml_dtypes
import concourse.bass as bass
import concourse.mybir as mybir
from concourse.bass_utils import run_bass_kernel_spmd
from contextlib import ExitStack

F32 = mybir.dt.float32
BF16 = mybir.dt.bfloat16
AF = mybir.ActivationFunctionType
ALU = mybir.AluOpType
BF = ml_dtypes.bfloat16

D = 2048
INTER = 4096
NH, HD = 64, 64
NG, NS = 8, 128
GN = NG * NS
CONV = INTER + 2 * GN
KC = 4
DH = 8192
T = 4096
Q = 128
NCH = 32
TC = 512
NTC = 8
NKD = 16
NPI = 32
NPX = 48
NPD = 16
NPH = 64
EPS = 1e-5
NEGBIG = -1e30
NBLK = 1 + NPI + NG + NG + NPI


class Ctr:
    def __init__(self):
        self.c = {}

    def inc(self, instr, sem, n=1):
        instr.then_inc(sem, n)
        self.c[sem.name] = self.c.get(sem.name, 0) + n
        return self.c[sem.name]

    def v(self, sem):
        return self.c.get(sem.name, 0)


def build_program(debug_outs=()):
    nc = bass.Bass("TRN2", target_bir_lowering=False)

    def din(name, shape, dt=F32):
        return nc.dram_tensor(name, shape, dt, kind="ExternalInput").ap()

    def dscr(name, shape, dt):
        kind = "ExternalOutput" if name in debug_outs else "Internal"
        return nc.dram_tensor(name, shape, dt, kind=kind).ap()

    xT = din("xT", [D, T])
    xTh = din("xTh", [D, 3])
    wproj = din("wproj", [NBLK, NKD, 128, 128])
    wout = din("wout", [NPD, NPI, 128, 128])
    wup = din("wup", [NPH, NKD, 128, 128])
    wdn = din("wdn", [NPD, NPH, 128, 128])
    convw = din("convw", [128, NPX * KC])
    convb = din("convb", [128, NPX])
    dtb = din("dtb", [64, 1])
    alog = din("alog", [64, 1])
    dexp = din("dexp", [128, NPI])
    nw_i = din("nw", [128, NPI])
    upsc = din("upsc", [128, NPH])
    dnsc = din("dnsc", [128, NPD])
    oscr = din("oscr", [128, 1])
    id_bf = din("id_bf", [128, 128], BF16)
    id_f32 = din("id_f32", [128, 128])
    onec_bf = din("onec_bf", [128, 1], BF16)
    oner_bf = din("oner_bf", [1, 128], BF16)
    oner_f32 = din("oner_f32", [1, 128])
    negbigI = din("negbigI", [128, 128], BF16)
    triL = din("triL", [128, 128], BF16)
    zcol = din("zcol", [128, 1])
    epscol = din("epscol", [128, 1])
    chain = din("chain", [1, 9])

    outT = nc.dram_tensor("outT", [D, T], F32, kind="ExternalOutput").ap()
    sc_xnT = dscr("sc_xnT", [D, T], BF16)
    sc_gate = dscr("sc_gate", [INTER, T], BF16)
    sc_xbc = dscr("sc_xbc", [CONV, T], BF16)
    sc_hid = dscr("sc_hid", [DH, T], BF16)
    sc_g = dscr("sc_g", [D, T], BF16)
    cc_in = nc.dram_tensor("cc_in", [128, T + 1], F32).ap()
    cc_out = nc.dram_tensor("cc_out", [4 * 128, T + 1], F32).ap()
    d_flat = nc.dram_tensor("d_flat", [64, 32], F32).ap()

    ctr = Ctr()

    with ExitStack() as ctx:
        E = ctx.enter_context

        sx = E(nc.semaphore("sx"))
        sw = E(nc.semaphore("sw"))
        st = E(nc.semaphore("st"))
        pe = E(nc.semaphore("pe"))
        dv = E(nc.semaphore("dv"))
        ac = E(nc.semaphore("ac"))
        pl = E(nc.semaphore("pl"))
        cc = E(nc.semaphore("cc"))
        swp = [E(nc.semaphore(f"swp{i}")) for i in range(2)]
        sxp = [E(nc.semaphore(f"sxp{i}")) for i in range(2)]
        hlp = [E(nc.semaphore(f"hlp{i}")) for i in range(2)]
        gsp = [E(nc.semaphore(f"gsp{i}")) for i in range(2)]
        osp = [E(nc.semaphore(f"osp{i}")) for i in range(2)]
        gbp = [E(nc.semaphore(f"gbp{i}")) for i in range(2)]
        fnp = [E(nc.semaphore(f"fnp{i}")) for i in range(2)]
        sxb = E(nc.semaphore("sxb"))
        bcp = [E(nc.semaphore(f"bcp{i}")) for i in range(2)]
        slp = E(nc.semaphore("slp"))
        sdr = E(nc.semaphore("sdr"))

        k_id_bf = E(nc.sbuf_tensor("k_id_bf", [128, 128], BF16))
        k_id_f32 = E(nc.sbuf_tensor("k_id_f32", [128, 128], F32))
        k_onec = E(nc.sbuf_tensor("k_onec", [128, 1], BF16))
        k_oner = E(nc.sbuf_tensor("k_oner", [1, 128], BF16))
        k_oner32 = E(nc.sbuf_tensor("k_oner32", [1, 128], F32))
        k_negbigI = E(nc.sbuf_tensor("k_negbigI", [128, 128], BF16))
        k_triL = E(nc.sbuf_tensor("k_triL", [128, 128], BF16))
        k_zcol = E(nc.sbuf_tensor("k_zcol", [128, 1], F32))
        k_eps = E(nc.sbuf_tensor("k_eps", [128, 1], F32))
        k_convw = E(nc.sbuf_tensor("k_convw", [128, NPX * KC], F32))
        k_convb = E(nc.sbuf_tensor("k_convb", [128, NPX], F32))
        k_dtb = E(nc.sbuf_tensor("k_dtb", [64, 1], F32))
        k_alog = E(nc.sbuf_tensor("k_alog", [64, 1], F32))
        k_dexp = E(nc.sbuf_tensor("k_dexp", [128, NPI], F32))
        k_nw = E(nc.sbuf_tensor("k_nw", [128, NPI], F32))
        k_upsc = E(nc.sbuf_tensor("k_upsc", [128, NPH], F32))
        k_dnsc = E(nc.sbuf_tensor("k_dnsc", [128, NPD], F32))
        k_oscr = E(nc.sbuf_tensor("k_oscr", [128, 1], F32))
        k_chain = E(nc.sbuf_tensor("k_chain", [1, 9], F32))
        p_dtc = E(nc.sbuf_tensor("p_dtc", [128, T + 8], F32))
        p_dt = p_dtc[0:64, 0:T]
        p_cdtp = p_dtc[64:128, 0:T + 8]
        p_ach = E(nc.sbuf_tensor("p_ach", [64, 32], F32))
        p_expA = E(nc.sbuf_tensor("p_expA", [64, 1], F32))
        p_delt = E(nc.sbuf_tensor("p_delt", [64, 1], F32))
        p_dsco = E(nc.sbuf_tensor("p_dsco", [128, NPD], F32))
        xnTh = E(nc.sbuf_tensor("xnTh", [128, NKD * 3], BF16))

        nc.const_aps.register(F32, EPS, k_eps[:, :])
        blk = E(nc.Block())

        def bar(e, sem, instr):
            e.wait_ge(sem, ctr.inc(instr, sem, 1))

        def stage_barrier():
            allsems = [sx, sw, st, pe, dv, ac, pl] + swp + sxp + hlp + gsp + osp \
                + gbp + fnp + [sxb, slp, sdr] + bcp
            snap = [(s_, ctr.v(s_)) for s_ in allsems if ctr.v(s_) > 0]
            for eng in (blk.sync, blk.gpsimd, blk.tensor, blk.vector, blk.scalar):
                @eng
                def _(e, snap=snap):
                    for s_, v_ in snap:
                        e.wait_ge(s_, v_)

        @blk.sync
        def _(s):
            for dst, src in [
                (k_id_bf[:, :], id_bf), (k_id_f32[:, :], id_f32),
                (k_onec[:, :], onec_bf), (k_oner[:, :], oner_bf),
                (k_oner32[:, :], oner_f32),
                (k_negbigI[:, :], negbigI), (k_triL[:, :], triL),
                (k_zcol[:, :], zcol),
                (k_eps[:, :], epscol),
                (k_convw[:, :], convw),
                (k_convb[:, :], convb),
                (k_dtb[:, :], dtb), (k_alog[:, :], alog),
                (k_dexp[:, :], dexp),
                (k_nw[:, :], nw_i),
                (k_upsc[:, :], upsc),
                (k_dnsc[:, :], dnsc),
                (k_oscr[:, :], oscr), (k_chain[:, :], chain),
            ]:
                ctr.inc(s.dma_start(out=dst, in_=src), sx, 16)
        c0 = ctr.v(sx)

        arA_ctx = ExitStack()
        arA = arA_ctx.enter_context(nc.sbuf_tensor("arA", [128, NKD * T], BF16))
        TCP0 = 256
        NP0 = T // TCP0
        with ExitStack() as sc_:
            SE = sc_.enter_context
            xbf = SE(nc.sbuf_tensor("xbf", [128, NKD * TCP0], BF16))
            xsq = SE(nc.sbuf_tensor("xsq", [128, NKD * TCP0], BF16))
            rows = SE(nc.sbuf_tensor("rows", [1, TCP0 * 6], F32))
            rowb = SE(nc.sbuf_tensor("rowb", [1, TCP0 * 4], BF16))
            rh32 = SE(nc.sbuf_tensor("rh32", [1, TCP0 * 2], F32))
            mrb = SE(nc.sbuf_tensor("mrb", [128, TCP0 * 2], BF16))
            pS = [SE(nc.psum_tensor(f"pS{i}", [128, TCP0], F32)) for i in range(4)]

            prev_done = {"dv": None, "pe_stats": None, "ac_bce": None}

            def p0_chunk(tcols, src_ap, dst_fn, first):
                @blk.gpsimd
                def _(g):
                    if first:
                        g.wait_ge(sx, c0)
                    if prev_done["dv"] is not None:
                        g.wait_ge(dv, prev_done["dv"])
                    ctr.inc(g.dma_start(
                        out=xbf[:, 0:NKD * tcols].rearrange("p (kt t) -> p kt t", t=tcols),
                        in_=src_ap), sw, 16)
                ld = ctr.v(sw)

                @blk.scalar
                def _(s):
                    s.wait_ge(sw, ld)
                    for kt in range(NKD):
                        a = s.activation(xsq[:, kt * tcols:(kt + 1) * tcols],
                                         xbf[:, kt * tcols:(kt + 1) * tcols], AF.Square)
                    ctr.inc(a, ac, 1)
                sqd = ctr.v(ac)

                @blk.tensor
                def _(t):
                    t.wait_ge(ac, sqd)
                    if prev_done["dv"] is not None:
                        t.wait_ge(dv, prev_done["dv"])
                    for kt in range(NKD):
                        t.matmul(pS[0][0:1, 0:tcols], k_onec[:, :],
                                 xbf[:, kt * tcols:(kt + 1) * tcols],
                                 start=(kt == 0), stop=(kt == NKD - 1))
                    for kt in range(NKD):
                        m = t.matmul(pS[1][0:1, 0:tcols], k_onec[:, :],
                                     xsq[:, kt * tcols:(kt + 1) * tcols],
                                     start=(kt == 0), stop=(kt == NKD - 1))
                    ctr.inc(m, pe, 1)
                std = ctr.v(pe)

                @blk.vector
                def _(v):
                    v.wait_ge(pe, std)
                    TP = TCP0
                    m_ = rows[0:1, 0:tcols]
                    e2 = rows[0:1, TP:TP + tcols]
                    var = rows[0:1, 2 * TP:2 * TP + tcols]
                    v.tensor_scalar_mul(m_, pS[0][0:1, 0:tcols], 1.0 / D)
                    bar(v, dv, v.tensor_scalar_mul(e2, pS[1][0:1, 0:tcols], 1.0 / D))
                    bar(v, dv, v.tensor_tensor(var, m_, m_, ALU.mult))
                    a = v.tensor_tensor(var, e2, var, ALU.subtract)
                    ctr.inc(a, dv, 1)
                vd = ctr.v(dv)

                @blk.scalar
                def _(s):
                    s.wait_ge(dv, vd)
                    TP = TCP0
                    bar(s, ac, s.activation(rows[0:1, 3 * TP:3 * TP + tcols],
                                            rows[0:1, 2 * TP:2 * TP + tcols], AF.Ln,
                                            bias=EPS))
                    a = s.activation(rows[0:1, 4 * TP:4 * TP + tcols],
                                     rows[0:1, 3 * TP:3 * TP + tcols], AF.Exp, scale=-0.5)
                    ctr.inc(a, ac, 1)
                sq2 = ctr.v(ac)

                @blk.vector
                def _(v):
                    v.wait_ge(ac, sq2)
                    TP = TCP0
                    rstd = rows[0:1, 4 * TP:4 * TP + tcols]
                    mr = rows[0:1, 5 * TP:5 * TP + tcols]
                    mh, ml = rowb[0:1, 0:tcols], rowb[0:1, TP:TP + tcols]
                    rh, rl = rowb[0:1, 2 * TP:2 * TP + tcols], rowb[0:1, 3 * TP:3 * TP + tcols]
                    v.tensor_tensor(mr, rows[0:1, 0:tcols], rstd, ALU.mult)
                    bar(v, dv, v.tensor_copy(rh, rstd))
                    v.tensor_copy(mh, mr)
                    bar(v, dv, v.tensor_copy(rh32[0:1, TP:TP + tcols], rh))
                    v.tensor_copy(rh32[0:1, 0:tcols], mh)
                    bar(v, dv, v.tensor_tensor(rl, rstd, rh32[0:1, TP:TP + tcols], ALU.subtract))
                    a = v.tensor_tensor(ml, mr, rh32[0:1, 0:tcols], ALU.subtract)
                    ctr.inc(a, dv, 1)
                rd = ctr.v(dv)

                @blk.tensor
                def _(t):
                    t.wait_ge(dv, rd)
                    TP = TCP0
                    if prev_done["ac_bce"] is not None:
                        t.wait_ge(ac, prev_done["ac_bce"])
                    t.matmul(pS[2][:, 0:tcols], k_oner[:, :], rowb[0:1, 0:tcols],
                             start=True, stop=False)
                    t.matmul(pS[2][:, 0:tcols], k_oner[:, :], rowb[0:1, TP:TP + tcols],
                             start=False, stop=True)
                    t.matmul(pS[3][:, 0:tcols], k_oner[:, :], rowb[0:1, 2 * TP:2 * TP + tcols],
                             start=True, stop=False)
                    m = t.matmul(pS[3][:, 0:tcols], k_oner[:, :], rowb[0:1, 3 * TP:3 * TP + tcols],
                                 start=False, stop=True)
                    ctr.inc(m, pe, 1)
                bcd = ctr.v(pe)

                @blk.scalar
                def _(s):
                    s.wait_ge(pe, bcd)
                    s.copy(mrb[:, 0:tcols], pS[2][:, 0:tcols])
                    a = s.copy(mrb[:, TCP0:TCP0 + tcols], pS[3][:, 0:tcols])
                    ctr.inc(a, ac, 1)
                prev_done["ac_bce"] = ctr.v(ac)
                bce = ctr.v(ac)

                @blk.vector
                def _(v):
                    v.wait_ge(ac, bce)
                    for kt in range(NKD):
                        xs = xbf[:, kt * tcols:(kt + 1) * tcols]
                        a = v.tensor_tensor(xsq[:, kt * tcols:(kt + 1) * tcols], xs,
                                            mrb[:, TCP0:TCP0 + tcols], ALU.mult)
                    bar(v, dv, a)
                    for kt in range(NKD):
                        a = v.tensor_tensor(dst_fn(kt),
                                            xsq[:, kt * tcols:(kt + 1) * tcols],
                                            mrb[:, 0:tcols], ALU.subtract)
                    ctr.inc(a, dv, 1)
                prev_done["dv"] = ctr.v(dv)
                return ctr.v(dv)

            src_h = xTh.rearrange("(kt p) t -> p kt t", p=128)
            p0_chunk(3, src_h, lambda kt: xnTh[:, kt * 3:(kt + 1) * 3], True)

            for tcc in range(NP0):
                src = xT.rearrange("(kt p) t -> p kt t", p=128)[:, :, tcc * TCP0:(tcc + 1) * TCP0]
                done = p0_chunk(
                    TCP0, src,
                    lambda kt, tcc=tcc: arA[:, kt * T + tcc * TCP0: kt * T + (tcc + 1) * TCP0],
                    False)

                @blk.scalar
                def _(s, tcc=tcc, done=done):
                    s.wait_ge(dv, done)
                    dst = sc_xnT.rearrange("(kt p) t -> p kt t", p=128)[
                        :, :, tcc * TCP0:(tcc + 1) * TCP0]
                    src2 = arA.rearrange("p (kt t) -> p kt t", t=T)[
                        :, :, tcc * TCP0:(tcc + 1) * TCP0]
                    ctr.inc(s.dma_start(out=dst, in_=src2), st, 16)
            p0_st = ctr.v(st)
            p0_dv = ctr.v(dv)

        with ExitStack() as sc_:
            SE = sc_.enter_context
            wbuf = [SE(nc.sbuf_tensor(f"wbuf{i}", [128, NKD * 128], BF16)) for i in range(2)]
            xbcr = SE(nc.sbuf_tensor("xbcr", [128, T + 3], F32))
            accb = SE(nc.sbuf_tensor("accb", [128, T], F32))
            ostg = [SE(nc.sbuf_tensor(f"ostg{i}", [128, T], BF16)) for i in range(2)]
            pM = [SE(nc.psum_tensor(f"pM{i}", [128, TC], F32)) for i in range(2)]
            pHh = SE(nc.psum_tensor("pHh", [128, 8], F32))
            sp1 = accb[0:64, :]
            sp2 = xbcr[0:64, 0:T]

            def btype(b):
                if b == 0:
                    return "dt"
                if b <= NPI:
                    return "x"
                if b <= NPI + NG:
                    return "B"
                if b <= NPI + 2 * NG:
                    return "C"
                return "gate"

            wld = {}
            mm_ms = {}
            evac_ms = {}
            accinit_ms = {}
            conv_ms = {}
            silu_ms = {}
            spill_ms = {}
            dtmath_dv = [None]

            @blk.gpsimd
            def _(g):
                g.wait_ge(dv, p0_dv)
                for b in range(2):
                    ctr.inc(g.dma_start(
                        out=wbuf[b][:, :].rearrange("p (kt m) -> p kt m", m=128),
                        in_=wproj[b].transpose([1, 0, 2])), swp[b % 2], 16)
                    wld[b] = ctr.v(swp[b % 2])

            for b in range(NBLK):
                bt = btype(b)

                if bt in ("x", "B", "C"):
                    @blk.tensor
                    def _(t, b=b):
                        t.wait_ge(swp[b % 2], wld[b])
                        if (b - 1, 0) in evac_ms:
                            t.wait_ge(ac, evac_ms[(b - 1, 0)])
                        for kt in range(NKD):
                            t.matmul(pHh[:, 0:3], wbuf[b % 2][:, kt * 128:(kt + 1) * 128],
                                     xnTh[:, kt * 3:(kt + 1) * 3],
                                     start=(kt == 0), stop=(kt == NKD - 1))

                ac_base = ctr.v(ac)
                dv_base = ctr.v(dv)
                for tcc in range(NTC):
                    if bt == "dt":
                        evac_ms[(b, tcc)] = ac_base + 3 * (tcc + 1)
                    else:
                        evac_ms[(b, tcc)] = ac_base + tcc + 1
                for tcc in range(NTC):
                    @blk.tensor
                    def _(t, b=b, tcc=tcc):
                        if tcc == 0:
                            t.wait_ge(swp[b % 2], wld[b])
                        it = b * NTC + tcc
                        if it >= 2:
                            kk = ((b, tcc - 2) if tcc >= 2
                                  else (b - 1, NTC - 2 + tcc))
                            t.wait_ge(ac, evac_ms[kk])
                            if kk in accinit_ms:
                                t.wait_ge(dv, accinit_ms[kk])
                        for kt in range(NKD):
                            mm = t.matmul(pM[tcc % 2][:, :],
                                          wbuf[b % 2][:, kt * 128:(kt + 1) * 128],
                                          arA[:, kt * T + tcc * TC: kt * T + (tcc + 1) * TC],
                                          start=(kt == 0), stop=(kt == NKD - 1))
                        ctr.inc(mm, pe, 1)
                        mm_ms[(b, tcc)] = ctr.v(pe)
                    if bt == "dt":
                        @blk.scalar
                        def _(s, b=b, bt=bt, tcc=tcc):
                            if tcc == 0:
                                if bt in ("x", "B", "C"):
                                    if b in (1, 2) and dtmath_dv[0] is not None:
                                        s.wait_ge(dv, dtmath_dv[0])
                                    if (b - 1) in silu_ms:
                                        s.wait_ge(ac, silu_ms[b - 1])
                                    if (b - 2) in spill_ms:
                                        s.wait_ge(sxp[b % 2], spill_ms[b - 2])
                                if bt == "gate" and (b - 2) in spill_ms:
                                    s.wait_ge(sxp[b % 2], spill_ms[b - 2])
                            s.wait_ge(pe, mm_ms[(b, tcc)])
                            p = pM[tcc % 2]
                            c0_, c1_ = tcc * TC, (tcc + 1) * TC
                            if bt == "gate":
                                a = s.activation(ostg[b % 2][:, c0_:c1_], p[:, :], AF.Silu)
                            elif bt == "dt":
                                bar(s, ac, s.activation(sp1[:, c0_:c1_], p[0:64, :],
                                                        AF.Abs, bias=k_dtb[:, :]))
                                bar(s, ac, s.activation(sp1[:, c0_:c1_], sp1[:, c0_:c1_],
                                                        AF.Exp, scale=-1.0))
                                s.activation(sp1[:, c0_:c1_], sp1[:, c0_:c1_], AF.Ln, bias=1.0)
                                a = s.activation(sp2[:, c0_:c1_], p[0:64, :], AF.Relu,
                                                 bias=k_dtb[:, :])
                            else:
                                if tcc == 0:
                                    s.copy(xbcr[:, 0:3], pHh[:, 0:3])
                                a = s.copy(xbcr[:, 3 + c0_:3 + c1_], p[:, :])
                            ctr.inc(a, ac, 1)
                            evac_ms[(b, tcc)] = ctr.v(ac)

                if bt != "dt":
                    @blk.scalar
                    def _(s, b=b, bt=bt):
                        if bt in ("x", "B", "C"):
                            if b in (1, 2) and dtmath_dv[0] is not None:
                                s.wait_ge(dv, dtmath_dv[0])
                            if (b - 1) in conv_ms:
                                s.wait_ge(conv_ms[b - 1][0], conv_ms[b - 1][1])
                        if bt == "gate" and (b - 2) in spill_ms:
                            s.wait_ge(sxp[b % 2], spill_ms[b - 2])
                        for tcc in range(NTC):
                            s.wait_ge(pe, mm_ms[(b, tcc)])
                            p = pM[tcc % 2]
                            c0_, c1_ = tcc * TC, (tcc + 1) * TC
                            if bt == "gate":
                                a = s.activation(ostg[b % 2][:, c0_:c1_], p[:, :], AF.Silu)
                            elif bt == "dt":
                                bar(s, ac, s.activation(sp1[:, c0_:c1_], p[0:64, :],
                                                        AF.Abs, bias=k_dtb[:, :]))
                                bar(s, ac, s.activation(sp1[:, c0_:c1_], sp1[:, c0_:c1_],
                                                        AF.Exp, scale=-1.0))
                                s.activation(sp1[:, c0_:c1_], sp1[:, c0_:c1_], AF.Ln, bias=1.0)
                                a = s.activation(sp2[:, c0_:c1_], p[0:64, :], AF.Relu,
                                                 bias=k_dtb[:, :])
                            else:
                                if tcc == 0:
                                    s.copy(xbcr[:, 0:3], pHh[:, 0:3])
                                a = s.copy(xbcr[:, 3 + c0_:3 + c1_], p[:, :])
                            ctr.inc(a, ac, 1)
                            assert ctr.v(ac) == evac_ms[(b, tcc)], (b, tcc, ctr.v(ac))

                if b + 2 < NBLK:
                    @blk.gpsimd
                    def _(g, b=b):
                        g.wait_ge(pe, mm_ms[(b, NTC - 1)])
                        ctr.inc(g.dma_start(
                            out=wbuf[b % 2][:, :].rearrange("p (kt m) -> p kt m", m=128),
                            in_=wproj[b + 2].transpose([1, 0, 2])), swp[b % 2], 16)
                        wld[b + 2] = ctr.v(swp[b % 2])

                if bt in ("x", "B", "C"):
                    use_pool = False
                    eng = blk.gpsimd if use_pool else blk.vector
                    sem = pl if use_pool else dv

                    @eng
                    def _(e, b=b, use_pool=use_pool):
                        e.wait_ge(ac, evac_ms[(b, NTC - 1)])
                        if (b - 1) in silu_ms:
                            e.wait_ge(ac, silu_ms[b - 1])
                        bar(e, sem, e.tensor_scalar(
                            accb[:, :], xbcr[:, 3:3 + T],
                            k_convw[:, (b - 1) * KC + 3:(b - 1) * KC + 4],
                            k_convb[:, b - 1:b], ALU.mult, ALU.add))
                        for k in range(3):
                            a = e.scalar_tensor_tensor(
                                accb[:, :], xbcr[:, k:k + T],
                                k_convw[:, (b - 1) * KC + k:(b - 1) * KC + k + 1],
                                accb[:, :], ALU.mult, ALU.add)
                            if k < 2:
                                bar(e, sem, a)
                        ctr.inc(a, sem, 1)
                    conv_ms[b] = (sem, ctr.v(sem))

                    @blk.scalar
                    def _(s, b=b):
                        s.wait_ge(conv_ms[b][0], conv_ms[b][1])
                        if (b - 2) in spill_ms:
                            s.wait_ge(sxp[b % 2], spill_ms[b - 2])
                        a = s.activation(ostg[b % 2][:, :], accb[:, :], AF.Silu)
                        ctr.inc(a, ac, 1)
                    silu_ms[b] = ctr.v(ac)

                    @blk.sync
                    def _(s, b=b):
                        s.wait_ge(ac, silu_ms[b])
                        ctr.inc(s.dma_start(out=sc_xbc[(b - 1) * 128:b * 128, :],
                                            in_=ostg[b % 2][:, :]), sxp[b % 2], 16)
                    spill_ms[b] = ctr.v(sxp[b % 2])
                elif bt == "gate":
                    gi = b - (1 + NPI + 2 * NG)

                    @blk.sync
                    def _(s, b=b, gi=gi):
                        s.wait_ge(ac, evac_ms[(b, NTC - 1)])
                        ctr.inc(s.dma_start(out=sc_gate[gi * 128:(gi + 1) * 128, :],
                                            in_=ostg[b % 2][:, :]), sxp[b % 2], 16)
                    spill_ms[b] = ctr.v(sxp[b % 2])
                else:
                    @blk.vector
                    def _(v):
                        v.wait_ge(ac, evac_ms[(0, NTC - 1)])
                        bar(v, dv, v.tensor_tensor(p_dt[:, :], sp1[:, 0:T], sp2[:, 0:T],
                                                   ALU.add))
                        v.tensor_copy(p_cdtp[:, 0:1], k_zcol[0:64, :])
                        a = v.tensor_tensor_scan(
                            p_cdtp[:, 1:T + 1], p_dt[:, :],
                            k_zcol[0:64, :].broadcast_to([64, T]), 0.0, ALU.add, ALU.add)
                        ctr.inc(a, dv, 1)
                    c1 = ctr.v(dv)

                    @blk.scalar
                    def _(s, c1=c1):
                        s.wait_ge(dv, c1)
                        a = s.activation(p_expA[:, :], k_alog[:, :], AF.Exp)
                        ctr.inc(a, ac, 1)
                    c2 = ctr.v(ac)

                    @blk.vector
                    def _(v, c2=c2):
                        v.wait_ge(ac, c2)
                        ce = p_cdtp[:, 128:T + 1:128]
                        cs = p_cdtp[:, 0:T:128]
                        bar(v, dv, v.tensor_tensor(p_ach[:, :], ce, cs, ALU.subtract))
                        v.tensor_scalar(p_ach[:, :], p_ach[:, :], p_expA[:, :], -1.0,
                                        ALU.mult, ALU.mult)
                        a = v.tensor_scalar(p_delt[:, :], p_cdtp[:, T:T + 1],
                                            p_expA[:, :], -1.0, ALU.mult, ALU.mult)
                        ctr.inc(a, dv, 1)
                    c3 = ctr.v(dv)

                    @blk.scalar
                    def _(s, c3=c3):
                        s.wait_ge(dv, c3)
                        s.activation(p_ach[:, :], p_ach[:, :], AF.Exp)
                        a = s.activation(p_delt[:, :], p_delt[:, :], AF.Exp)
                        ctr.inc(a, ac, 1)
                    c4 = ctr.v(ac)
                    dtmath_dv[0] = c3
                    dtmath_ac = c4

                    dtmath_dv[0] = c1

            p1_sxp = (ctr.v(sxp[0]), ctr.v(sxp[1]))
            p1_pe = ctr.v(pe)
        arA_ctx.close()

        stage_barrier()
        with ExitStack() as sc_:
            SE = sc_.enter_context
            Xa = SE(nc.sbuf_tensor("Xa", [128, NPI * TC], BF16))
            Ba = SE(nc.sbuf_tensor("Ba", [128, NG * TC], BF16))
            XTa = SE(nc.sbuf_tensor("XTa", [128, 4 * INTER], BF16))
            BTa = SE(nc.sbuf_tensor("BTa", [128, 4 * GN], BF16))
            XwTa = SE(nc.sbuf_tensor("XwTa", [128, 4 * INTER], BF16))
            wcha = SE(nc.sbuf_tensor("wcha", [64, 4 * 128], F32))
            wTba = SE(nc.sbuf_tensor("wTba", [128, 4 * 64], BF16))
            lam = SE(nc.sbuf_tensor("lam", [128, INTER], F32))
            pTa = [SE(nc.psum_tensor(f"pTa{i}", [128, 1024], BF16)) for i in range(2)]
            pWa = SE(nc.psum_tensor("pWa", [128, 512], F32))
            pLa = [SE(nc.psum_tensor(f"pLa{i}", [128, TC], F32)) for i in range(4)]

            sa_tr_cons = None
            sa_lam = None
            for scc in range(8):
                @blk.sync
                def _(s, scc=scc):
                    if scc == 0:
                        s.wait_ge(sxp[0], p1_sxp[0])
                        s.wait_ge(sxp[1], p1_sxp[1])
                    else:
                        s.wait_ge(pe, sa_trd)
                    rr = sc_xbc.rearrange("(pt p) t -> p pt t", p=128)
                    sl = slice(scc * TC, (scc + 1) * TC)
                    ctr.inc(s.dma_start(out=Xa.rearrange("p (pt t) -> p pt t", t=TC),
                                        in_=rr[:, 0:NPI, sl]), sx, 16)
                    ctr.inc(s.dma_start(out=Ba.rearrange("p (pt t) -> p pt t", t=TC),
                                        in_=rr[:, NPI:NPI + NG, sl]), sx, 16)
                lda = ctr.v(sx)

                @blk.vector
                def _(v, scc=scc):
                    if scc == 0:
                        v.wait_ge(ac, dtmath_ac)
                    for c in range(4):
                        cg = scc * 4 + c
                        a = v.tensor_scalar(wcha[:, c * 128:(c + 1) * 128],
                                            p_cdtp[:, cg * 128 + 1:cg * 128 + 129],
                                            p_cdtp[:, T:T + 1], p_expA[:, :],
                                            ALU.subtract, ALU.mult)
                    ctr.inc(a, dv, 1)
                w1 = ctr.v(dv)

                @blk.scalar
                def _(s, w1=w1):
                    s.wait_ge(dv, w1)
                    a = s.activation(wcha[:, 0:512], wcha[:, 0:512], AF.Exp)
                    ctr.inc(a, ac, 1)
                w2 = ctr.v(ac)

                @blk.vector
                def _(v, scc=scc, w2=w2):
                    v.wait_ge(ac, w2)
                    for c in range(4):
                        cg = scc * 4 + c
                        a = v.tensor_tensor(wcha[:, c * 128:(c + 1) * 128],
                                            wcha[:, c * 128:(c + 1) * 128],
                                            p_dt[:, cg * 128:(cg + 1) * 128], ALU.mult)
                    ctr.inc(a, dv, 1)
                w3 = ctr.v(dv)

                tiles = []
                for c in range(4):
                    for pt in range(NPI):
                        tiles.append(("X", c, pt))
                    for g_ in range(NG):
                        tiles.append(("B", c, g_))
                waves = [tiles[i:i + 8] for i in range(0, len(tiles), 8)]
                pTa_ev = pTa_ev if scc > 0 else {0: None, 1: None}
                last_wave_ev = None
                for wv, wtiles in enumerate(waves):
                    @blk.tensor
                    def _(t, wv=wv, wtiles=wtiles, scc=scc, lda=lda, w3=w3):
                        if wv == 0:
                            t.wait_ge(sx, lda)
                            t.wait_ge(dv, w3)
                            if sa_tr_cons is not None:
                                t.wait_ge(dv, sa_tr_cons)
                        if pTa_ev[wv % 2] is not None:
                            t.wait_ge(dv, pTa_ev[wv % 2])
                        p = pTa[wv % 2]
                        for i, (kind, c, j) in enumerate(wtiles):
                            src = (Xa[:, j * TC + c * 128: j * TC + c * 128 + 128]
                                   if kind == "X" else
                                   Ba[:, j * TC + c * 128: j * TC + c * 128 + 128])
                            mm = t.transpose(p[:, i * 128:(i + 1) * 128], src, k_id_bf[:, :])
                        ctr.inc(mm, pe, 1)
                    twd = ctr.v(pe)

                    @blk.vector
                    def _(v, wv=wv, wtiles=wtiles, twd=twd):
                        v.wait_ge(pe, twd)
                        p = pTa[wv % 2]
                        for i, (kind, c, j) in enumerate(wtiles):
                            dst = (XTa[:, c * INTER + j * 128: c * INTER + (j + 1) * 128]
                                   if kind == "X" else
                                   BTa[:, c * GN + j * 128: c * GN + (j + 1) * 128])
                            a = v.tensor_copy(dst, p[:, i * 128:(i + 1) * 128])
                        ctr.inc(a, dv, 1)
                    pTa_ev[wv % 2] = ctr.v(dv)
                    last_wave_ev = ctr.v(dv)
                sa_trd = ctr.v(pe)

                @blk.tensor
                def _(t, w3=w3):
                    t.wait_ge(dv, w3)
                    for c in range(4):
                        mm = t.transpose(pWa[:, c * 64:(c + 1) * 64],
                                         wcha[:, c * 128:(c + 1) * 128],
                                         k_id_f32[0:64, 0:64])
                    ctr.inc(mm, pe, 1)
                wtd = ctr.v(pe)

                @blk.vector
                def _(v, wtd=wtd):
                    v.wait_ge(pe, wtd)
                    a = v.tensor_copy(wTba[:, :], pWa[:, 0:256])
                    ctr.inc(a, dv, 1)
                wte = ctr.v(dv)

                @blk.vector
                def _(v, wte=wte, lw=last_wave_ev):
                    v.wait_ge(dv, lw)
                    v.wait_ge(dv, wte)
                    for c in range(4):
                        a = v.tensor_tensor(
                            XwTa[:, c * INTER:(c + 1) * INTER].rearrange(
                                "p (h d) -> p h d", d=64),
                            XTa[:, c * INTER:(c + 1) * INTER].rearrange(
                                "p (h d) -> p h d", d=64),
                            wTba[:, c * 64:(c + 1) * 64].unsqueeze(2).broadcast_to(
                                [128, 64, 64]),
                            ALU.mult)
                    ctr.inc(a, dv, 1)
                xwd = ctr.v(dv)

                for half in range(2):
                    @blk.tensor
                    def _(t, half=half, xwd=xwd, scc=scc):
                        t.wait_ge(dv, xwd)
                        if sa_lam is not None:
                            t.wait_ge(dv, sa_lam)
                        for gi in range(4):
                            g_ = half * 4 + gi
                            for c in range(4):
                                mm = t.matmul(
                                    pLa[gi][:, :],
                                    BTa[:, c * GN + g_ * 128: c * GN + (g_ + 1) * 128],
                                    XwTa[:, c * INTER + g_ * 512: c * INTER + g_ * 512 + 512],
                                    start=(c == 0), stop=(c == 3))
                        ctr.inc(mm, pe, 1)
                    cbd = ctr.v(pe)

                    @blk.vector
                    def _(v, half=half, cbd=cbd, scc=scc, prev=sa_lam):
                        v.wait_ge(pe, cbd)
                        if prev is not None:
                            v.wait_ge(dv, prev)
                        for gi in range(4):
                            g_ = half * 4 + gi
                            if scc == 0:
                                a = v.tensor_copy(lam[:, g_ * 512:(g_ + 1) * 512],
                                                  pLa[gi][:, :])
                            else:
                                a = v.tensor_tensor(lam[:, g_ * 512:(g_ + 1) * 512],
                                                    lam[:, g_ * 512:(g_ + 1) * 512],
                                                    pLa[gi][:, :], ALU.add)
                        ctr.inc(a, dv, 1)
                    sa_lam = ctr.v(dv)
                sa_tr_cons = xwd
                sa_tr_cons = sa_lam

            @blk.scalar
            def _(s):
                s.wait_ge(dv, sa_lam)
                s.wait_ge(ac, dtmath_ac)
                ctr.inc(s.dma_start(out=cc_in[0:128, 0:T], in_=lam[:, :]), st, 16)
                ctr.inc(s.dma_start(out=cc_in[:, T:T + 1], in_=k_zcol[:, :],
                                    allow_slow_non_contiguous=True), st, 16)
                s.wait_ge(st, ctr.v(st))
                ctr.inc(s.dma_start(out=cc_in[0:64, T:T + 1], in_=p_delt[:, :],
                                    allow_slow_non_contiguous=True), st, 16)
            payd = ctr.v(st)

            @blk.gpsimd
            def _(g):
                g.wait_ge(st, payd)
                a = g.collective_compute(
                    "AllGather", ALU.bypass,
                    replica_groups=[[0, 1, 2, 3], [4, 5, 6, 7]],
                    ins=[cc_in.opt()], outs=[cc_out.opt()])
                ctr.inc(a, cc, 1)
            ccd = ctr.v(cc)

        stage_barrier()
        with ExitStack() as sc_:
            SE = sc_.enter_context
            arA = SE(nc.sbuf_tensor("arB", [128, NKD * T], BF16))
            wb2 = [SE(nc.sbuf_tensor(f"wb2{i}", [128, NKD * 128], BF16)) for i in range(2)]
            ws2 = [SE(nc.sbuf_tensor(f"ws2{i}", [128, NKD * 128], BF16)) for i in range(2)]
            hst = [SE(nc.sbuf_tensor(f"hst{i}", [128, T], BF16)) for i in range(2)]
            pF = [SE(nc.psum_tensor(f"pF{i}", [128, TC], F32)) for i in range(2)]

            @blk.sync
            def _(s):
                s.wait_ge(pe, p1_pe)
                s.wait_ge(st, p0_st)
                for kt in range(NKD):
                    ctr.inc(s.dma_start(out=arA[:, kt * T:(kt + 1) * T],
                                        in_=sc_xnT[kt * 128:(kt + 1) * 128, :]), sx, 16)
            f1xn = ctr.v(sx)

            f1_wld = {}
            f1_sgn = {}
            f1_mm = {}
            f1_ev = {}
            f1_sp = {}

            @blk.gpsimd
            def _(g):
                for hb in range(2):
                    ctr.inc(g.dma_start(
                        out=wb2[hb][:, :].rearrange("p (kt m) -> p kt m", m=128),
                        in_=wup[hb].transpose([1, 0, 2])), swp[hb % 2], 16)
                    f1_wld[hb] = ctr.v(swp[hb % 2])

            for hb in range(NPH):
                @blk.scalar
                def _(s, hb=hb):
                    s.wait_ge(swp[hb % 2], f1_wld[hb])
                    if hb >= 2:
                        s.wait_ge(pe, f1_mm[(hb - 2, NTC - 1)])
                    a = s.activation(ws2[hb % 2][:, :], wb2[hb % 2][:, :], AF.Sign)
                    ctr.inc(a, ac, 1)
                f1_sgn[hb] = ctr.v(ac)

                if hb + 2 < NPH:
                    @blk.gpsimd
                    def _(g, hb=hb):
                        g.wait_ge(ac, f1_sgn[hb])
                        ctr.inc(g.dma_start(
                            out=wb2[hb % 2][:, :].rearrange("p (kt m) -> p kt m", m=128),
                            in_=wup[hb + 2].transpose([1, 0, 2])), swp[hb % 2], 16)
                        f1_wld[hb + 2] = ctr.v(swp[hb % 2])

                ac_base_f1 = ctr.v(ac)
                for tcc in range(NTC):
                    f1_ev[hb * NTC + tcc] = ac_base_f1 + tcc + 1

                for tcc in range(NTC):
                    @blk.tensor
                    def _(t, hb=hb, tcc=tcc):
                        if tcc == 0:
                            if hb == 0:
                                t.wait_ge(sx, f1xn)
                            t.wait_ge(ac, f1_sgn[hb])
                        it = hb * NTC + tcc
                        if it >= 2:
                            t.wait_ge(ac, f1_ev[it - 2])
                        for kt in range(NKD):
                            mm = t.matmul(pF[tcc % 2][:, :],
                                          ws2[hb % 2][:, kt * 128:(kt + 1) * 128],
                                          arA[:, kt * T + tcc * TC: kt * T + (tcc + 1) * TC],
                                          start=(kt == 0), stop=(kt == NKD - 1))
                        ctr.inc(mm, pe, 1)
                        f1_mm[(hb, tcc)] = ctr.v(pe)

                @blk.scalar
                def _(s, hb=hb):
                    if hb >= 2:
                        s.wait_ge(sxp[hb % 2], f1_sp[hb - 2])
                    for tcc in range(NTC):
                        s.wait_ge(pe, f1_mm[(hb, tcc)])
                        a = s.activation(hst[hb % 2][:, tcc * TC:(tcc + 1) * TC],
                                         pF[tcc % 2][:, :], AF.Relu,
                                         scale=k_upsc[:, hb:hb + 1])
                        ctr.inc(a, ac, 1)
                        assert ctr.v(ac) == f1_ev[hb * NTC + tcc]

                @blk.sync
                def _(s, hb=hb):
                    s.wait_ge(ac, f1_ev[hb * NTC + NTC - 1])
                    ctr.inc(s.dma_start(out=sc_hid[hb * 128:(hb + 1) * 128, :],
                                        in_=hst[hb % 2][:, :]), sxp[hb % 2], 16)
                f1_sp[hb] = ctr.v(sxp[hb % 2])
            f1_pe = ctr.v(pe)
            f1_sxp = (ctr.v(sxp[0]), ctr.v(sxp[1]))

        stage_barrier()
        with ExitStack() as sc_:
            SE = sc_.enter_context
            TC2 = 256
            NTC2 = 16
            GRP = [(0, 6), (6, 6), (12, 4)]
            dnw = SE(nc.sbuf_tensor("dnw", [128, 6 * NPH * 128], BF16))
            hb2 = [SE(nc.sbuf_tensor(f"hb2{i}", [128, NPH * TC2], BF16)) for i in range(2)]
            gstg = [SE(nc.sbuf_tensor(f"gstg{i}", [128, TC2], BF16)) for i in range(2)]
            pF2 = [SE(nc.psum_tensor(f"pF2{i}", [128, TC2], F32)) for i in range(2)]

            @blk.vector
            def _(v):
                v.wait_ge(sx, c0)
                a = v.tensor_scalar(p_dsco[:, :], k_dnsc[:, :], k_oscr[:, :], None, ALU.mult)
                ctr.inc(a, dv, 1)
            dscd = ctr.v(dv)

            f2_mm = {}
            f2_ev = {}
            f2_hl = {}
            f2_stms = {}
            for G, (goff, gn) in enumerate(GRP):
                @blk.gpsimd
                def _(g, G=G, goff=goff, gn=gn):
                    if G == 0:
                        g.wait_ge(pe, f1_pe)
                    else:
                        g.wait_ge(pe, f2_mm[(G - 1, NTC2 - 1, GRP[G - 1][1] - 1)])
                    for mt in range(gn):
                        ctr.inc(g.dma_start(
                            out=dnw[:, mt * 8192:(mt + 1) * 8192].rearrange(
                                "p (kt m) -> p kt m", m=128),
                            in_=wdn[goff + mt].transpose([1, 0, 2])), sw, 16)
                ldh = ctr.v(sw)

                @blk.scalar
                def _(s, ldh=ldh, gn=gn):
                    s.wait_ge(sw, ldh)
                    for q in range(2 * gn):
                        a = s.activation(dnw[:, q * 4096:(q + 1) * 4096],
                                         dnw[:, q * 4096:(q + 1) * 4096], AF.Sign)
                    ctr.inc(a, ac, 1)
                sgn2 = ctr.v(ac)

                for tcc in range(NTC2):
                    @blk.sync
                    def _(s, G=G, tcc=tcc, gn=gn):
                        if G == 0 and tcc == 0:
                            s.wait_ge(sxp[0], f1_sxp[0])
                            s.wait_ge(sxp[1], f1_sxp[1])
                        if tcc >= 2:
                            s.wait_ge(pe, f2_mm[(G, tcc - 2, gn - 1)])
                        elif G > 0 and tcc < 2:
                            s.wait_ge(pe, f2_mm[(G - 1, NTC2 - 2 + tcc, GRP[G - 1][1] - 1)])
                        src = sc_hid.rearrange("(pt p) t -> p pt t", p=128)[
                            :, :, tcc * TC2:(tcc + 1) * TC2]
                        ctr.inc(s.dma_start(
                            out=hb2[tcc % 2].rearrange("p (pt t) -> p pt t", t=TC2),
                            in_=src), hlp[tcc % 2], 16)
                    f2_hl[(G, tcc)] = ctr.v(hlp[tcc % 2])

                    ac_base_f2 = ctr.v(ac)
                    for mt in range(gn):
                        f2_ev[(G, tcc, mt)] = ac_base_f2 + mt + 1

                    for mt in range(gn):
                        @blk.tensor
                        def _(t, G=G, tcc=tcc, mt=mt, sgn2=sgn2, gn=gn):
                            if mt == 0:
                                if tcc == 0:
                                    t.wait_ge(ac, sgn2)
                                t.wait_ge(hlp[tcc % 2], f2_hl[(G, tcc)])
                            it = tcc * gn + mt
                            if it >= 2:
                                k = (tcc, mt - 2) if mt >= 2 else (tcc - 1, gn - 2 + mt)
                                t.wait_ge(ac, f2_ev[(G,) + k])
                            for kt in range(NPH):
                                mm = t.matmul(pF2[mt % 2][:, :],
                                              dnw[:, mt * 8192 + kt * 128:
                                                  mt * 8192 + (kt + 1) * 128],
                                              hb2[tcc % 2][:, kt * TC2:(kt + 1) * TC2],
                                              start=(kt == 0), stop=(kt == NPH - 1))
                            ctr.inc(mm, pe, 1)
                            f2_mm[(G, tcc, mt)] = ctr.v(pe)

                    @blk.scalar
                    def _(s, G=G, goff=goff, tcc=tcc, gn=gn):
                        if G == 0 and tcc == 0:
                            s.wait_ge(dv, dscd)
                        for mt in range(gn):
                            s.wait_ge(pe, f2_mm[(G, tcc, mt)])
                            prev = ((G, tcc, mt - 2) if mt >= 2 else
                                    (G, tcc - 1, gn - 2 + mt) if tcc >= 1 else
                                    (G - 1, NTC2 - 1, GRP[G - 1][1] - 2 + mt) if G >= 1
                                    else None)
                            if prev is not None and prev in f2_stms:
                                s.wait_ge(gsp[mt % 2], f2_stms[prev])
                            a = s.activation(gstg[mt % 2][:, :], pF2[mt % 2][:, :],
                                             AF.Silu,
                                             scale=p_dsco[:, goff + mt:goff + mt + 1])
                            ctr.inc(a, ac, 1)
                            assert ctr.v(ac) == f2_ev[(G, tcc, mt)]
                            s.wait_ge(ac, f2_ev[(G, tcc, mt)])
                            ctr.inc(s.dma_start(
                                out=sc_g[(goff + mt) * 128:(goff + mt + 1) * 128,
                                         tcc * TC2:(tcc + 1) * TC2],
                                in_=gstg[mt % 2][:, :]), gsp[mt % 2], 16)
                            f2_stms[(G, tcc, mt)] = ctr.v(gsp[mt % 2])
            f2_gsp = (ctr.v(gsp[0]), ctr.v(gsp[1]))
            f2_pe = ctr.v(pe)
        stage_barrier()
        with ExitStack() as sc_:
            SE = sc_.enter_context
            Xb = SE(nc.sbuf_tensor("Xb", [128, NPI * TC], BF16))
            Bbc = [SE(nc.sbuf_tensor(f"Bbc{i}", [128, GN], BF16)) for i in range(2)]
            Cbc = [SE(nc.sbuf_tensor(f"Cbc{i}", [128, GN], BF16)) for i in range(2)]
            S_ = SE(nc.sbuf_tensor("S_", [128, INTER], F32))
            stp = [SE(nc.sbuf_tensor(f"stp{i}", [128, 512], F32)) for i in range(2)]
            XT2 = SE(nc.sbuf_tensor("XT2", [128, INTER], BF16))
            BT2 = SE(nc.sbuf_tensor("BT2", [128, GN], BF16))
            sml = SE(nc.sbuf_tensor("sml", [64, 512], F32))
            smT = SE(nc.sbuf_tensor("smT", [128, 256], F32))
            wT2 = SE(nc.sbuf_tensor("wT2", [128, 64], BF16))
            alr = SE(nc.sbuf_tensor("alr", [128, 64], F32))
            Eb = SE(nc.sbuf_tensor("Eb", [128, NH * 128], BF16))
            Gb = SE(nc.sbuf_tensor("Gb", [128, NG * 128], BF16))
            Mp = SE(nc.sbuf_tensor("Mp", [128, NH * 128], BF16))
            XwT2 = SE(nc.sbuf_tensor("XwT2", [128, INTER], BF16))
            scout = SE(nc.sbuf_tensor("scout", [128, NPI * TC], BF16))
            drow = SE(nc.sbuf_tensor("drow", [1, 3 * 64], F32))
            crow = SE(nc.sbuf_tensor("crow", [1, 6 * 64], F32))
            crep = SE(nc.sbuf_tensor("crep", [128, 3 * 64], F32))
            rmsr = SE(nc.sbuf_tensor("rmsr", [1, TC * 2], F32))
            rmsb = SE(nc.sbuf_tensor("rmsb", [1, TC * 2], BF16))
            rstdb = SE(nc.sbuf_tensor("rstdb", [128, TC], BF16))
            wb3 = [SE(nc.sbuf_tensor(f"wb3{i}", [128, 16 * 128], BF16)) for i in range(2)]
            gabp = [SE(nc.sbuf_tensor(f"gabp{i}", [128, TC], BF16)) for i in range(2)]
            gfbp = [SE(nc.sbuf_tensor(f"gfbp{i}", [128, TC], BF16)) for i in range(2)]
            xfbp = [SE(nc.sbuf_tensor(f"xfbp{i}", [128, TC], F32)) for i in range(2)]
            sstg = [SE(nc.sbuf_tensor(f"sstg{i}", [128, TC], BF16)) for i in range(2)]
            ostg2 = [SE(nc.sbuf_tensor(f"ostg2{i}", [128, TC], F32)) for i in range(2)]
            S_bf = SE(nc.sbuf_tensor("S_bf", [128, INTER], BF16))
            pA = [SE(nc.psum_tensor(f"pA{i}", [128, 512], F32)) for i in range(2)]
            pB = [SE(nc.psum_tensor(f"pB{i}", [128, 512], F32)) for i in range(2)]
            pCx = SE(nc.psum_tensor("pCx", [128, 512], F32))
            pE2 = SE(nc.psum_tensor("pE2", [128, 512], F32))
            pT2 = [SE(nc.psum_tensor(f"pT2{i}", [128, 1024], BF16)) for i in range(2)]
            Ych = Eb[:, 0:INTER]

            @blk.sync
            def _(s):
                s.wait_ge(cc, ccd)
                for k in range(3):
                    ctr.inc(s.dma_start(
                        out=drow[0:1, k * 64:(k + 1) * 64],
                        in_=cc_out[k * 128:k * 128 + 64, T:T + 1].transpose([1, 0]),
                        allow_slow_non_contiguous=True), sdr, 16)
            drd = ctr.v(sdr)

            @blk.vector
            def _(v):
                v.wait_ge(sdr, drd)
                ch = k_chain
                t01 = crow[0:1, 192:256]
                t02 = crow[0:1, 256:320]
                t12 = crow[0:1, 320:384]
                v.scalar_tensor_tensor(t01, drow[0:1, 64:128], ch[0:1, 3:4],
                                       ch[0:1, 4:5].broadcast_to([1, 64]), ALU.mult, ALU.add)
                v.scalar_tensor_tensor(t02, drow[0:1, 128:192], ch[0:1, 5:6],
                                       ch[0:1, 6:7].broadcast_to([1, 64]), ALU.mult, ALU.add)
                bar(v, dv, v.scalar_tensor_tensor(t12, drow[0:1, 128:192], ch[0:1, 7:8],
                                                  ch[0:1, 8:9].broadcast_to([1, 64]),
                                                  ALU.mult, ALU.add))
                bar(v, dv, v.tensor_tensor(crow[0:1, 0:64], t01, t02, ALU.mult))
                v.tensor_single_scalar(crow[0:1, 0:64], crow[0:1, 0:64], ch[0:1, 0:1],
                                       ALU.mult)
                v.tensor_single_scalar(crow[0:1, 64:128], t12, ch[0:1, 1:2], ALU.mult)
                a = v.tensor_single_scalar(crow[0:1, 128:192], k_oner32[0:1, 0:64],
                                           ch[0:1, 2:3], ALU.mult)
                ctr.inc(a, dv, 1)
            cfd = ctr.v(dv)

            @blk.tensor
            def _(t):
                t.wait_ge(dv, cfd)
                for k in range(3):
                    mm = t.matmul(pE2[:, k * 64:(k + 1) * 64], k_oner32[:, :],
                                  crow[0:1, k * 64:(k + 1) * 64], start=True, stop=True)
                ctr.inc(mm, pe, 1)
            crm = ctr.v(pe)

            @blk.scalar
            def _(s):
                s.wait_ge(pe, crm)
                a = s.copy(crep[:, :], pE2[:, 0:192])
                ctr.inc(a, ac, 1)
            crd = ctr.v(ac)

            for pc in range(8):
                for k in range(3):
                    lp_guard = ctr.v(dv)

                    @blk.sync
                    def _(s, pc=pc, k=k, lp_guard=lp_guard):
                        if pc == 0 and k == 0:
                            s.wait_ge(ac, crd)
                        else:
                            s.wait_ge(dv, lp_guard)
                        ctr.inc(s.dma_start(
                            out=stp[1][:, :],
                            in_=cc_out[k * 128:(k + 1) * 128, pc * 512:(pc + 1) * 512]),
                            slp, 16)
                    lpd = ctr.v(slp)

                    @blk.vector
                    def _(v, pc=pc, k=k, lpd=lpd):
                        v.wait_ge(slp, lpd)
                        cap = crep[:, k * 64:(k + 1) * 64]
                        capg = cap[:, pc * 8:(pc + 1) * 8].unsqueeze(2).broadcast_to(
                            [128, 8, 64])
                        if k == 0:
                            a = v.tensor_tensor(
                                S_[:, pc * 512:(pc + 1) * 512].rearrange(
                                    "p (hh d) -> p hh d", d=64),
                                stp[1][:, :].rearrange("p (hh d) -> p hh d", d=64),
                                capg, ALU.mult)
                        else:
                            bar(v, dv, v.tensor_tensor(
                                stp[0][:, :].rearrange("p (hh d) -> p hh d", d=64),
                                stp[1][:, :].rearrange("p (hh d) -> p hh d", d=64),
                                capg, ALU.mult))
                            a = v.tensor_tensor(S_[:, pc * 512:(pc + 1) * 512],
                                                S_[:, pc * 512:(pc + 1) * 512],
                                                stp[0][:, :], ALU.add)
                        ctr.inc(a, dv, 1)
            @blk.vector
            def _(v, prev=ctr.v(dv)):
                v.wait_ge(dv, prev)
                a = v.tensor_copy(S_bf[:, :], S_[:, :])
                ctr.inc(a, dv, 1)
            sind = ctr.v(dv)

            sb_free = {}
            gev = {}
            eexp_ms = {}
            ostm_prev = None
            for scc in range(8):
                @blk.sync
                def _(s, scc=scc):
                    if scc == 0:
                        s.wait_ge(gsp[0], f2_gsp[0])
                        s.wait_ge(gsp[1], f2_gsp[1])
                        s.wait_ge(dv, sind)
                    if scc >= 1:
                        s.wait_ge(dv, sb_free[scc - 1])
                    rr = sc_xbc.rearrange("(pt p) t -> p pt t", p=128)
                    sl = slice(scc * TC, (scc + 1) * TC)
                    ctr.inc(s.dma_start(out=Xb.rearrange("p (pt t) -> p pt t", t=TC),
                                        in_=rr[:, 0:NPI, sl]), sxb, 16)
                sbl = ctr.v(sxb)

                for c in range(4):
                    cg = scc * 4 + c
                    t0 = c * 128

                    @blk.sync
                    def _(s, scc=scc, cg=cg, t0=t0):
                        if cg >= 2:
                            s.wait_ge(pe, bc_free[cg - 2])
                        rr = sc_xbc.rearrange("(pt p) t -> p pt t", p=128)
                        csl = slice(scc * TC + t0, scc * TC + t0 + 128)
                        ctr.inc(s.dma_start(
                            out=Bbc[cg % 2].rearrange("p (g t) -> p g t", t=128),
                            in_=rr[:, NPI:NPI + NG, csl]), bcp[cg % 2], 16)
                        ctr.inc(s.dma_start(
                            out=Cbc[cg % 2].rearrange("p (g t) -> p g t", t=128),
                            in_=rr[:, NPI + NG:NPX, csl]), bcp[cg % 2], 16)
                    bcl = ctr.v(bcp[cg % 2])
                    bc_free = bc_free if cg > 0 else {}

                    @blk.vector
                    def _(v, cg=cg, scc=scc):
                        if cg == 0:
                            v.wait_ge(dv, sind)
                        v.tensor_scalar(sml[:, 0:128],
                                        p_cdtp[:, cg * 128 + 1:cg * 128 + 129],
                                        p_cdtp[:, (cg + 1) * 128:(cg + 1) * 128 + 1],
                                        p_expA[:, :], ALU.subtract, ALU.mult)
                        v.tensor_scalar(sml[:, 128:256],
                                        p_cdtp[:, cg * 128 + 1:cg * 128 + 129],
                                        p_cdtp[:, cg * 128:cg * 128 + 1],
                                        p_expA[:, :], ALU.subtract, ALU.mult)
                        a = v.tensor_scalar(sml[:, 256:384],
                                            p_cdtp[:, cg * 128 + 1:cg * 128 + 129],
                                            p_expA[:, :], -1.0, ALU.mult, ALU.mult)
                        ctr.inc(a, dv, 1)
                    sm1 = ctr.v(dv)

                    @blk.scalar
                    def _(s, cg=cg, sm1=sm1):
                        s.wait_ge(dv, sm1)
                        s.activation(sml[:, 0:128], sml[:, 0:128], AF.Exp)
                        s.activation(sml[:, 128:256], sml[:, 128:256], AF.Exp, scale=-1.0)
                        a = s.activation(sml[:, 384:512], p_dt[:, cg * 128:(cg + 1) * 128],
                                         AF.Ln)
                        ctr.inc(a, ac, 1)
                    sm2 = ctr.v(ac)

                    @blk.vector
                    def _(v, cg=cg, sm2=sm2):
                        v.wait_ge(ac, sm2)
                        v.tensor_tensor(sml[:, 0:128], sml[:, 0:128],
                                        p_dt[:, cg * 128:(cg + 1) * 128], ALU.mult)
                        a = v.tensor_tensor(sml[:, 384:512], sml[:, 384:512],
                                            sml[:, 256:384], ALU.subtract)
                        ctr.inc(a, dv, 1)
                    sm3 = ctr.v(dv)

                    pt2_ev = pt2_ev if cg > 0 else {0: None, 1: None}
                    for wv in range(5):
                        @blk.tensor
                        def _(t, wv=wv, c=c, cg=cg, sbl=sbl, sm3=sm3, bcl=bcl):
                            if wv == 0:
                                t.wait_ge(sxb, sbl)
                                t.wait_ge(bcp[cg % 2], bcl)
                                if cg > 0:
                                    t.wait_ge(dv, xt_cons)
                            if pt2_ev[wv % 2] is not None:
                                t.wait_ge(dv, pt2_ev[wv % 2])
                            p = pT2[wv % 2]
                            for i in range(8):
                                n = wv * 8 + i
                                if n < 32:
                                    src = Xb[:, n * TC + t0: n * TC + t0 + 128]
                                else:
                                    src = Bbc[cg % 2][:, (n - 32) * 128:(n - 31) * 128]
                                mm = t.transpose(p[:, i * 128:(i + 1) * 128], src,
                                                 k_id_bf[:, :])
                            ctr.inc(mm, pe, 1)
                        twd = ctr.v(pe)

                        @blk.vector
                        def _(v, wv=wv, twd=twd):
                            v.wait_ge(pe, twd)
                            p = pT2[wv % 2]
                            for i in range(8):
                                n = wv * 8 + i
                                dst = (XT2[:, n * 128:(n + 1) * 128] if n < 32
                                       else BT2[:, (n - 32) * 128:(n - 31) * 128])
                                a = v.tensor_copy(dst, p[:, i * 128:(i + 1) * 128])
                            ctr.inc(a, dv, 1)
                        pt2_ev[wv % 2] = ctr.v(dv)
                        if wv == 4:
                            xbt_done = ctr.v(dv)

                    @blk.tensor
                    def _(t, cg=cg, sm3=sm3):
                        t.wait_ge(dv, sm3)
                        if cg > 0:
                            t.wait_ge(dv, gev[(cg - 1, 7)])
                        id64 = k_id_f32[0:64, 0:64]
                        t.transpose(pE2[:, 0:64], sml[:, 0:128], id64)
                        t.transpose(pE2[:, 64:128], sml[:, 128:256], id64)
                        t.transpose(pE2[:, 128:192], sml[:, 256:384], id64)
                        mm = t.transpose(pE2[:, 192:256], sml[:, 384:512], id64)
                        ctr.inc(mm, pe, 1)
                    smtd = ctr.v(pe)

                    acb = ctr.v(ac)
                    smev = acb + 1
                    for blkh in range(16):
                        eexp_ms[(cg, blkh)] = acb + 2 + blkh
                    gevac = acb + 18
                    ald = acb + 19

                    ebd_list = []

                    @blk.tensor
                    def _(t, cg=cg, smev=smev):
                        t.wait_ge(ac, smev)
                        if cg > 0:
                            t.wait_ge(dv, gev[(cg - 1, 7)])
                            t.wait_ge(dv, gev[(cg - 1, 6)])
                        for blkh in range(16):
                            if blkh >= 2:
                                t.wait_ge(ac, eexp_ms[(cg, blkh - 2)])
                            p = pA[blkh % 2]
                            for hh in range(4):
                                h = blkh * 4 + hh
                                t.matmul(p[:, hh * 128:(hh + 1) * 128],
                                         smT[:, 128 + h:129 + h].broadcast_to([128, 128]),
                                         k_id_f32[:, :], is_transpose=True,
                                         start=(hh == 0), stop=False)
                            mm = t.matmul(p[:, :], k_negbigI[:, :],
                                          k_triL[:, :].unsqueeze(1).broadcast_to(
                                              [128, 4, 128]),
                                          start=False, stop=True)
                            ctr.inc(mm, pe, 1)
                            ebd_list.append(ctr.v(pe))

                    @blk.tensor
                    def _(t, c=c, t0=t0, cg=cg):
                        if cg > 0:
                            t.wait_ge(dv, gev[(cg - 1, 7)])
                        for g_ in range(NG):
                            p = pB[g_ // 4]
                            mm = t.matmul(p[:, (g_ % 4) * 128:(g_ % 4) * 128 + 128],
                                          Bbc[cg % 2][:, g_ * 128:(g_ + 1) * 128],
                                          Cbc[cg % 2][:, g_ * 128:(g_ + 1) * 128],
                                          start=True, stop=True)
                        ctr.inc(mm, pe, 1)
                    gmd = ctr.v(pe)

                    @blk.vector
                    def _(v, smev=smev):
                        v.wait_ge(ac, smev)
                        a = v.tensor_copy(wT2[:, :], smT[:, 0:64])
                        ctr.inc(a, dv, 1)
                    wtc = ctr.v(dv)

                    @blk.vector
                    def _(v, gevac=gevac, cg=cg, wtc=wtc, xbt_done=xbt_done):
                        v.wait_ge(ac, gevac)
                        v.wait_ge(ac, eexp_ms[(cg, 15)])
                        v.wait_ge(dv, xbt_done)
                        v.wait_ge(dv, wtc)
                        v.tensor_tensor(
                            Mp[:, :].rearrange("p (g hh i) -> p g hh i", g=NG, hh=8),
                            Eb[:, :].rearrange("p (g hh i) -> p g hh i", g=NG, hh=8),
                            Gb[:, :].rearrange("p (g i) -> p g i", g=NG).unsqueeze(2)
                            .broadcast_to([128, NG, 8, 128]), ALU.mult)
                        a = v.tensor_tensor(
                            XwT2[:, :].rearrange("p (h d) -> p h d", d=64),
                            XT2[:, :].rearrange("p (h d) -> p h d", d=64),
                            wT2[:, :].unsqueeze(2).broadcast_to([128, 64, 64]), ALU.mult)
                        ctr.inc(a, dv, 1)
                    mpd = ctr.v(dv)

                    @blk.tensor
                    def _(t, cg=cg, mpd=mpd):
                        t.wait_ge(dv, mpd)
                        t.wait_ge(ac, dtmath_ac)
                        mm = t.transpose(pE2[:, 256:320],
                                         p_ach[:, cg:cg + 1].broadcast_to([64, 128]),
                                         k_id_f32[0:64, 0:64])
                        ctr.inc(mm, pe, 1)
                    alm = ctr.v(pe)

                    @blk.scalar
                    def _(s, cg=cg, smtd=smtd, gmd=gmd, alm=alm, ebd_list=ebd_list):
                        s.wait_ge(pe, smtd)
                        ctr.inc(s.copy(smT[:, :], pE2[:, 0:256]), ac, 1)
                        for blkh in range(16):
                            s.wait_ge(pe, ebd_list[blkh])
                            p = pA[blkh % 2]
                            for hh in range(4):
                                h = blkh * 4 + hh
                                a = s.activation(Eb[:, h * 128:(h + 1) * 128],
                                                 p[:, hh * 128:(hh + 1) * 128], AF.Exp,
                                                 bias=smT[:, 192 + h:193 + h])
                            ctr.inc(a, ac, 1)
                        s.wait_ge(pe, gmd)
                        s.copy(Gb[:, 0:512], pB[0][:, :])
                        ctr.inc(s.copy(Gb[:, 512:1024], pB[1][:, :]), ac, 1)
                        s.wait_ge(pe, alm)
                        ctr.inc(s.copy(alr[:, :], pE2[:, 256:320]), ac, 1)
                        assert ctr.v(ac) == cg * 0 + ald, (ctr.v(ac), ald)

                    for g_ in range(NG):
                        pCb = pCx if g_ % 2 == 0 else pE2

                        @blk.tensor
                        def _(t, g_=g_, cg=cg, t0=t0, mpd=mpd, ald=ald):
                            if g_ == 0:
                                t.wait_ge(dv, mpd)
                            if g_ == 1:
                                t.wait_ge(ac, ald)
                            if g_ >= 2:
                                t.wait_ge(dv, gev[(cg, g_ - 2)])
                            par = g_ % 2
                            for hh in range(8):
                                h = g_ * 8 + hh
                                t.matmul(pA[par][:, hh * 64:(hh + 1) * 64],
                                         Mp[:, h * 128:(h + 1) * 128],
                                         XT2[:, h * 64:(h + 1) * 64],
                                         start=True, stop=True)
                            t.matmul(pB[par][:, :],
                                     Cbc[cg % 2][:, g_ * 128:(g_ + 1) * 128],
                                     S_bf[:, g_ * 512:(g_ + 1) * 512], start=True, stop=True)
                            mm = t.matmul((pCx if g_ % 2 == 0 else pE2)[:, :],
                                          BT2[:, g_ * 128:(g_ + 1) * 128],
                                          XwT2[:, g_ * 512:(g_ + 1) * 512],
                                          start=True, stop=True)
                            ctr.inc(mm, pe, 1)
                        gmm = ctr.v(pe)

                        @blk.gpsimd
                        def _(g, g_=g_, gmm=gmm, ald=ald):
                            g.wait_ge(pe, gmm)
                            g.wait_ge(ac, ald)
                            a = g.tensor_tensor(
                                stp[g_ % 2][:, :].rearrange("p (hh d) -> p hh d", d=64),
                                S_[:, g_ * 512:(g_ + 1) * 512].rearrange(
                                    "p (hh d) -> p hh d", d=64),
                                alr[:, g_ * 8:(g_ + 1) * 8].unsqueeze(2).broadcast_to(
                                    [128, 8, 64]), ALU.mult)
                            ctr.inc(a, pl, 1)
                        spd = ctr.v(pl)

                        @blk.vector
                        def _(v, g_=g_, cg=cg, gmm=gmm, spd=spd):
                            v.wait_ge(pe, gmm)
                            par = g_ % 2
                            bar(v, dv, v.tensor_tensor(
                                Ych[:, g_ * 512:(g_ + 1) * 512].rearrange(
                                    "p (hh d) -> p hh d", d=64),
                                pB[par][:, :].rearrange("p (hh d) -> p hh d", d=64),
                                smT[:, 64 + g_ * 8:64 + (g_ + 1) * 8].unsqueeze(2)
                                .broadcast_to([128, 8, 64]), ALU.mult))
                            v.tensor_tensor(Ych[:, g_ * 512:(g_ + 1) * 512],
                                            pA[par][:, :], Ych[:, g_ * 512:(g_ + 1) * 512],
                                            ALU.add)
                            v.wait_ge(pl, spd)
                            bar(v, dv, v.tensor_tensor(S_[:, g_ * 512:(g_ + 1) * 512],
                                                       stp[g_ % 2][:, :],
                                                       (pCx if g_ % 2 == 0 else pE2)[:, :],
                                                       ALU.add))
                            a = v.tensor_copy(S_bf[:, g_ * 512:(g_ + 1) * 512],
                                              S_[:, g_ * 512:(g_ + 1) * 512])
                            ctr.inc(a, dv, 1)
                        gev[(cg, g_)] = ctr.v(dv)
                    xt_cons = gev[(cg, 7)]
                    bc_free[cg] = ctr.v(pe)

                    for wv in range(4):
                        @blk.tensor
                        def _(t, wv=wv, cg=cg):
                            if wv == 0:
                                t.wait_ge(dv, gev[(cg, 7)])
                            if pt2_ev[wv % 2] is not None:
                                t.wait_ge(dv, pt2_ev[wv % 2])
                            p = pT2[wv % 2]
                            for i in range(8):
                                pt = wv * 8 + i
                                mm = t.transpose(p[:, i * 128:(i + 1) * 128],
                                                 Ych[:, pt * 128:(pt + 1) * 128],
                                                 k_id_bf[:, :])
                            ctr.inc(mm, pe, 1)
                        ytd = ctr.v(pe)

                        @blk.vector
                        def _(v, wv=wv, c=c, ytd=ytd):
                            v.wait_ge(pe, ytd)
                            p = pT2[wv % 2]
                            for i in range(8):
                                pt = wv * 8 + i
                                a = v.tensor_copy(
                                    scout[:, pt * TC + c * 128: pt * TC + c * 128 + 128],
                                    p[:, i * 128:(i + 1) * 128])
                            ctr.inc(a, dv, 1)
                        pt2_ev[wv % 2] = ctr.v(dv)
                ychd = ctr.v(dv)

                gp_dv = []
                for pt in range(NPI):
                    @blk.sync
                    def _(s, pt=pt, scc=scc):
                        if pt >= 2:
                            s.wait_ge(dv, gp_dv[pt - 2])
                        ctr.inc(s.dma_start(
                            out=gabp[pt % 2][:, :],
                            in_=sc_gate[pt * 128:(pt + 1) * 128, scc * TC:(scc + 1) * TC]),
                            gbp[pt % 2], 16)
                    gld = ctr.v(gbp[pt % 2])

                    @blk.vector
                    def _(v, pt=pt, gld=gld, ychd=ychd, sbl=sbl):
                        if pt == 0:
                            v.wait_ge(dv, ychd)
                            v.wait_ge(sxb, sbl)
                        v.wait_ge(gbp[pt % 2], gld)
                        bar(v, dv, v.scalar_tensor_tensor(stp[pt % 2][:, :],
                                                          Xb[:, pt * TC:(pt + 1) * TC],
                                                          k_dexp[:, pt:pt + 1],
                                                          scout[:, pt * TC:(pt + 1) * TC],
                                                          ALU.mult, ALU.add))
                        a = v.tensor_tensor(scout[:, pt * TC:(pt + 1) * TC],
                                            stp[pt % 2][:, :], gabp[pt % 2][:, :],
                                            ALU.mult)
                        ctr.inc(a, dv, 1)
                    gp_dv.append(ctr.v(dv))
                gpd = ctr.v(dv)

                acb2 = ctr.v(ac)
                peb2 = ctr.v(pe)
                sq_ms = {pt: peb2 + pt + 1 for pt in range(NPI)}
                sqa_pl = {pt: acb2 + pt + 1 for pt in range(NPI)}

                @blk.scalar
                def _(s, gpd=gpd):
                    s.wait_ge(dv, gpd)
                    for pt in range(NPI):
                        if pt >= 4:
                            s.wait_ge(pe, sq_ms[pt - 4])
                        a = s.activation(XwT2[:, (pt % 4) * 512:(pt % 4 + 1) * 512],
                                         scout[:, pt * TC:(pt + 1) * TC], AF.Square)
                        ctr.inc(a, ac, 1)
                        assert ctr.v(ac) == sqa_pl[pt]

                @blk.tensor
                def _(t, scc=scc):
                    if scc > 0:
                        t.wait_ge(ac, rbe_prev)
                    for pt in range(NPI):
                        t.wait_ge(ac, sqa_pl[pt])
                        mm = t.matmul(pB[0][0:1, 0:TC], k_onec[:, :],
                                      XwT2[:, (pt % 4) * 512:(pt % 4 + 1) * 512],
                                      start=(pt == 0), stop=(pt == NPI - 1))
                        ctr.inc(mm, pe, 1)
                        assert ctr.v(pe) == sq_ms[pt]
                ssqd = ctr.v(pe)

                @blk.vector
                def _(v, ssqd=ssqd):
                    v.wait_ge(pe, ssqd)
                    a = v.tensor_scalar_mul(rmsr[0:1, 0:TC], pB[0][0:1, 0:TC], 1.0 / INTER)
                    ctr.inc(a, dv, 1)
                r1 = ctr.v(dv)

                @blk.scalar
                def _(s, r1=r1):
                    s.wait_ge(dv, r1)
                    bar(s, ac, s.activation(rmsr[0:1, TC:2 * TC], rmsr[0:1, 0:TC],
                                            AF.Ln, bias=EPS))
                    a = s.activation(rmsr[0:1, 0:TC], rmsr[0:1, TC:2 * TC], AF.Exp,
                                     scale=-0.5)
                    ctr.inc(a, ac, 1)
                r2 = ctr.v(ac)

                @blk.vector
                def _(v, r2=r2):
                    v.wait_ge(ac, r2)
                    bar(v, dv, v.tensor_copy(rmsb[0:1, 0:TC], rmsr[0:1, 0:TC]))
                    bar(v, dv, v.tensor_copy(rmsr[0:1, TC:2 * TC], rmsb[0:1, 0:TC]))
                    a = v.tensor_tensor(rmsb[0:1, TC:2 * TC], rmsr[0:1, 0:TC],
                                        rmsr[0:1, TC:2 * TC], ALU.subtract)
                    ctr.inc(a, dv, 1)
                r3 = ctr.v(dv)

                @blk.tensor
                def _(t, r3=r3):
                    t.wait_ge(dv, r3)
                    t.matmul(pB[1][:, 0:TC], k_oner[:, :], rmsb[0:1, 0:TC],
                             start=True, stop=False)
                    mm = t.matmul(pB[1][:, 0:TC], k_oner[:, :], rmsb[0:1, TC:2 * TC],
                                  start=False, stop=True)
                    ctr.inc(mm, pe, 1)
                r4 = ctr.v(pe)

                @blk.scalar
                def _(s, r4=r4):
                    s.wait_ge(pe, r4)
                    a = s.copy(rstdb[:, :], pB[1][:, 0:TC])
                    ctr.inc(a, ac, 1)
                rbe = ctr.v(ac)
                rbe_prev = rbe

                @blk.vector
                def _(v, rbe=rbe):
                    v.wait_ge(ac, rbe)
                    for pt in range(NPI):
                        a = v.scalar_tensor_tensor(Xb[:, pt * TC:(pt + 1) * TC],
                                                   rstdb[:, :], k_nw[:, pt:pt + 1],
                                                   scout[:, pt * TC:(pt + 1) * TC],
                                                   ALU.mult, ALU.mult)
                    ctr.inc(a, dv, 1)
                nrm = ctr.v(dv)

                wo_ld = {}
                wo_mm = {}
                wo_mmh = {}
                wo_ev = {}
                out_dv = {}

                @blk.gpsimd
                def _(g, scc=scc):
                    g.wait_ge(pe, f2_pe)
                    for hf in range(2):
                        if scc > 0:
                            g.wait_ge(pe, wo_mmh_prev[(15, hf)])
                        ctr.inc(g.dma_start(
                            out=wb3[hf][:, :].rearrange("p (kt m) -> p kt m", m=128),
                            in_=wout[0, hf * 16:(hf + 1) * 16].transpose([1, 0, 2])),
                            swp[hf], 16)
                        wo_ld[(0, hf)] = ctr.v(swp[hf])

                acb3 = ctr.v(ac)
                dvb3 = ctr.v(dv)
                fnb = [ctr.v(fnp[0]), ctr.v(fnp[1])]
                osb = [ctr.v(osp[0]), ctr.v(osp[1])]
                fld_pl = {}
                for mt in range(NPD):
                    wo_ev[mt] = acb3 + mt + 1
                    out_dv[mt] = dvb3 + 2 * (mt + 1)
                    fld_pl[mt] = fnb[mt % 2] + 32 * (mt // 2 + 1)
                ostm_new = {mt: osb[mt % 2] + 16 * (mt // 2 + 1) for mt in range(NPD)}

                for mt in range(NPD):
                    for hf in range(2):
                        @blk.tensor
                        def _(t, mt=mt, hf=hf, nrm=nrm, scc=scc):
                            if mt == 0 and hf == 0:
                                t.wait_ge(dv, nrm)
                            t.wait_ge(swp[hf], wo_ld[(mt, hf)])
                            if hf == 0 and mt >= 2:
                                t.wait_ge(ac, wo_ev[mt - 2])
                            for k2 in range(16):
                                kt = hf * 16 + k2
                                mm = t.matmul(pA[mt % 2][:, :],
                                              wb3[hf][:, k2 * 128:(k2 + 1) * 128],
                                              Xb[:, kt * TC:(kt + 1) * TC],
                                              start=(kt == 0), stop=(kt == NPI - 1))
                            ctr.inc(mm, pe, 1)
                            wo_mmh[(mt, hf)] = ctr.v(pe)
                        wo_mm[mt] = ctr.v(pe)

                        nxt = (mt + 1, hf)
                        if nxt[0] < NPD:
                            @blk.gpsimd
                            def _(g, mt=mt, hf=hf, nxt=nxt):
                                g.wait_ge(pe, wo_mmh[(mt, hf)])
                                ctr.inc(g.dma_start(
                                    out=wb3[nxt[1]][:, :].rearrange(
                                        "p (kt m) -> p kt m", m=128),
                                    in_=wout[nxt[0], nxt[1] * 16:(nxt[1] + 1) * 16].transpose([1, 0, 2])),
                                    swp[nxt[1]], 16)
                                wo_ld[nxt] = ctr.v(swp[nxt[1]])

                @blk.sync
                def _(s, scc=scc, nrm=nrm):
                    for mt in range(NPD):
                        if mt >= 2:
                            s.wait_ge(dv, out_dv[mt - 2])
                        elif scc > 0:
                            s.wait_ge(dv, nrm)
                        ctr.inc(s.dma_start(
                            out=gfbp[mt % 2][:, :],
                            in_=sc_g[mt * 128:(mt + 1) * 128, scc * TC:(scc + 1) * TC]),
                            fnp[mt % 2], 16)
                        ctr.inc(s.dma_start(
                            out=xfbp[mt % 2][:, :],
                            in_=xT.rearrange("(pt p) t -> pt p t", p=128)[
                                mt, :, scc * TC:(scc + 1) * TC]), fnp[mt % 2], 16)
                        assert ctr.v(fnp[mt % 2]) == fld_pl[mt]

                @blk.scalar
                def _(s, scc=scc):
                    for mt in range(NPD):
                        s.wait_ge(pe, wo_mm[mt])
                        if mt >= 2:
                            s.wait_ge(dv, out_dv[mt - 2])
                        a = s.copy(sstg[mt % 2][:, :], pA[mt % 2][:, :])
                        ctr.inc(a, ac, 1)
                        assert ctr.v(ac) == wo_ev[mt]
                        if mt >= 2:
                            ctr.inc(s.dma_start(
                                out=outT[(mt - 2) * 128:(mt - 1) * 128,
                                         scc * TC:(scc + 1) * TC],
                                in_=ostg2[mt % 2][:, :]), osp[mt % 2], 16)
                            assert ctr.v(osp[mt % 2]) == ostm_new[mt - 2]
                    for mt in (NPD - 2, NPD - 1):
                        s.wait_ge(dv, out_dv[mt])
                        ctr.inc(s.dma_start(
                            out=outT[mt * 128:(mt + 1) * 128, scc * TC:(scc + 1) * TC],
                            in_=ostg2[mt % 2][:, :]), osp[mt % 2], 16)
                        assert ctr.v(osp[mt % 2]) == ostm_new[mt]

                @blk.vector
                def _(v, scc=scc, ostm_prev=ostm_prev):
                    for mt in range(NPD):
                        v.wait_ge(ac, wo_ev[mt])
                        v.wait_ge(fnp[mt % 2], fld_pl[mt])
                        if mt >= 2:
                            v.wait_ge(osp[mt % 2], ostm_new[mt - 2])
                        elif scc > 0:
                            v.wait_ge(osp[mt % 2], ostm_prev[14 + mt])
                        bar(v, dv, v.tensor_tensor(sstg[mt % 2][:, :], sstg[mt % 2][:, :],
                                                   gfbp[mt % 2][:, :], ALU.mult))
                        a = v.tensor_tensor(ostg2[mt % 2][:, :], xfbp[mt % 2][:, :],
                                            sstg[mt % 2][:, :], ALU.add)
                        ctr.inc(a, dv, 1)
                        assert ctr.v(dv) == out_dv[mt]
                ostm_prev = ostm_new
                wo_mmh_prev = wo_mmh
                sb_free[scc] = out_dv[NPD - 1]

            fin0, fin1 = ctr.v(osp[0]), ctr.v(osp[1])

            @blk.scalar
            def _(s):
                s.wait_ge(osp[0], fin0)
                s.wait_ge(osp[1], fin1)

    return nc


def _prep_weights(i):
    w = {}
    Wdt = np.zeros((D, 128), np.float32)
    Wdt[:, 0:64] = i["L0_attn_Wdt"]
    blocks = [Wdt]
    for m in range(NPI):
        blocks.append(i["L0_attn_Wx"][:, m * 128:(m + 1) * 128])
    for m in range(NG):
        blocks.append(i["L0_attn_WB"][:, m * 128:(m + 1) * 128])
    for m in range(NG):
        blocks.append(i["L0_attn_WC"][:, m * 128:(m + 1) * 128])
    for m in range(NPI):
        blocks.append(i["L0_attn_Wz"][:, m * 128:(m + 1) * 128])
    w["wproj"] = np.ascontiguousarray(
        np.stack([b.reshape(NKD, 128, 128) for b in blocks]))
    w["wout"] = np.ascontiguousarray(
        i["L0_attn_Wout"].reshape(NPI, 128, NPD, 128).transpose(2, 0, 1, 3))
    w["wup"] = np.ascontiguousarray(
        i["L0_ffn_up_weight"].reshape(NPH, 128, NKD, 128).transpose(0, 2, 3, 1))
    w["wdn"] = np.ascontiguousarray(
        i["L0_ffn_down_weight"].reshape(NPD, 128, NPH, 128).transpose(0, 2, 3, 1))
    w["convw"] = np.ascontiguousarray(
        i["L0_attn_conv_w"].reshape(NPX, 128, KC).transpose(1, 0, 2).reshape(128, NPX * KC))
    w["convb"] = np.ascontiguousarray(i["L0_attn_conv_b"].reshape(NPX, 128).T)
    w["dtb"] = np.ascontiguousarray(i["L0_attn_dt_bias"].reshape(64, 1))
    w["alog"] = np.ascontiguousarray(i["L0_attn_A_log"].reshape(64, 1))
    w["dexp"] = np.ascontiguousarray(np.repeat(i["L0_attn_D"], HD).reshape(NPI, 128).T)
    w["nw"] = np.ascontiguousarray(i["L0_attn_norm_w"].reshape(NPI, 128).T)
    w["upsc"] = np.ascontiguousarray(i["L0_ffn_up_scale"].reshape(NPH, 128).T)
    w["dnsc"] = np.ascontiguousarray(i["L0_ffn_down_scale"].reshape(NPD, 128).T)
    w["oscr"] = np.ascontiguousarray(
        np.broadcast_to(i["L0_ffn_output_scale"].reshape(1, 1), (128, 1)))
    w["id_bf"] = np.eye(128, dtype=BF)
    w["id_f32"] = np.eye(128, dtype=np.float32)
    w["onec_bf"] = np.ones((128, 1), BF)
    w["oner_bf"] = np.ones((1, 128), BF)
    w["oner_f32"] = np.ones((1, 128), np.float32)
    w["negbigI"] = (NEGBIG * np.eye(128)).astype(BF)
    w["triL"] = np.tril(np.ones((128, 128)), -1).astype(BF)
    w["zcol"] = np.zeros((128, 1), np.float32)
    w["epscol"] = np.full((128, 1), EPS, np.float32)
    return w


def kernel(**inputs):
    inputs = {k: np.asarray(v, dtype=np.float32) for k, v in inputs.items()}
    x = inputs["L0_attn_hidden_states"]
    w = _prep_weights(inputs)

    nc = build_program()
    in_maps = []
    for c in range(8):
        b, s = c // 4, c % 4
        xTc = np.ascontiguousarray(x[b, s * T:(s + 1) * T, :].T)
        if s == 0:
            halo = np.zeros((D, 3), np.float32)
        else:
            halo = np.ascontiguousarray(x[b, s * T - 3:s * T, :].T)
        cm = [1.0 if k < s else 0.0 for k in range(3)]
        cf01 = 1.0 if s >= 2 else 0.0
        cf02 = 1.0 if s >= 3 else 0.0
        cf12 = 1.0 if s >= 3 else 0.0
        chain = np.array([[cm[0], cm[1], cm[2], cf01, 1 - cf01,
                           cf02, 1 - cf02, cf12, 1 - cf12]], np.float32)
        m = dict(w)
        m["xT"] = xTc
        m["xTh"] = halo
        m["chain"] = chain
        in_maps.append(m)

    res = run_bass_kernel_spmd(nc, in_maps, core_ids=list(range(8)))
    out = np.empty((2, 16384, D), np.float32)
    for c in range(8):
        b, s = c // 4, c % 4
        out[b, s * T:(s + 1) * T, :] = res.results[c]["outT"].T
    return out


# version 08
# speedup vs baseline: 1.0536x; 1.0536x over previous
import numpy as np
import ml_dtypes
import concourse.bass as bass
import concourse.mybir as mybir
from concourse.bass_utils import run_bass_kernel_spmd
from contextlib import ExitStack

F32 = mybir.dt.float32
BF16 = mybir.dt.bfloat16
AF = mybir.ActivationFunctionType
ALU = mybir.AluOpType
BF = ml_dtypes.bfloat16

D = 2048
INTER = 4096
NH, HD = 64, 64
NG, NS = 8, 128
GN = NG * NS
CONV = INTER + 2 * GN
KC = 4
DH = 8192
T = 4096
Q = 128
NCH = 32
TC = 512
NTC = 8
NKD = 16
NPI = 32
NPX = 48
NPD = 16
NPH = 64
EPS = 1e-5
NEGBIG = -1e30
NBLK = 1 + NPI + NG + NG + NPI


class Ctr:
    def __init__(self):
        self.c = {}

    def inc(self, instr, sem, n=1):
        instr.then_inc(sem, n)
        self.c[sem.name] = self.c.get(sem.name, 0) + n
        return self.c[sem.name]

    def v(self, sem):
        return self.c.get(sem.name, 0)


def build_program(debug_outs=()):
    nc = bass.Bass("TRN2", target_bir_lowering=False)

    def din(name, shape, dt=F32):
        return nc.dram_tensor(name, shape, dt, kind="ExternalInput").ap()

    def dscr(name, shape, dt):
        kind = "ExternalOutput" if name in debug_outs else "Internal"
        return nc.dram_tensor(name, shape, dt, kind=kind).ap()

    xT = din("xT", [D, T])
    xTh = din("xTh", [D, 3])
    wproj = din("wproj", [NBLK, NKD, 128, 128])
    wout = din("wout", [NPD, NPI, 128, 128])
    wup = din("wup", [NPH, NKD, 128, 128])
    wdn = din("wdn", [NPD, NPH, 128, 128])
    convw = din("convw", [128, NPX * KC])
    convb = din("convb", [128, NPX])
    dtb = din("dtb", [64, 1])
    alog = din("alog", [64, 1])
    dexp = din("dexp", [128, NPI])
    nw_i = din("nw", [128, NPI])
    upsc = din("upsc", [128, NPH])
    dnsc = din("dnsc", [128, NPD])
    oscr = din("oscr", [128, 1])
    id_bf = din("id_bf", [128, 128], BF16)
    id_f32 = din("id_f32", [128, 128])
    onec_bf = din("onec_bf", [128, 1], BF16)
    oner_bf = din("oner_bf", [1, 128], BF16)
    oner_f32 = din("oner_f32", [1, 128])
    negbigI = din("negbigI", [128, 128], BF16)
    triL = din("triL", [128, 128], BF16)
    zcol = din("zcol", [128, 1])
    epscol = din("epscol", [128, 1])
    chain = din("chain", [1, 9])

    outT = nc.dram_tensor("outT", [D, T], F32, kind="ExternalOutput").ap()
    sc_xnT = dscr("sc_xnT", [D, T], BF16)
    sc_gate = dscr("sc_gate", [INTER, T], BF16)
    sc_xbc = dscr("sc_xbc", [CONV, T], BF16)
    sc_hid = dscr("sc_hid", [DH, T], BF16)
    sc_g = dscr("sc_g", [D, T], BF16)
    cc_in = nc.dram_tensor("cc_in", [128, T + 1], F32).ap()
    cc_out = nc.dram_tensor("cc_out", [4 * 128, T + 1], F32).ap()
    d_flat = nc.dram_tensor("d_flat", [64, 32], F32).ap()

    ctr = Ctr()

    with ExitStack() as ctx:
        E = ctx.enter_context

        sx = E(nc.semaphore("sx"))
        sw = E(nc.semaphore("sw"))
        st = E(nc.semaphore("st"))
        pe = E(nc.semaphore("pe"))
        dv = E(nc.semaphore("dv"))
        ac = E(nc.semaphore("ac"))
        pl = E(nc.semaphore("pl"))
        cc = E(nc.semaphore("cc"))
        swp = [E(nc.semaphore(f"swp{i}")) for i in range(2)]
        sxp = [E(nc.semaphore(f"sxp{i}")) for i in range(2)]
        hlp = [E(nc.semaphore(f"hlp{i}")) for i in range(2)]
        gsp = [E(nc.semaphore(f"gsp{i}")) for i in range(2)]
        osp = [E(nc.semaphore(f"osp{i}")) for i in range(2)]
        gbp = [E(nc.semaphore(f"gbp{i}")) for i in range(2)]
        fnp = [E(nc.semaphore(f"fnp{i}")) for i in range(2)]
        sxb = E(nc.semaphore("sxb"))
        bcp = [E(nc.semaphore(f"bcp{i}")) for i in range(2)]
        slp = E(nc.semaphore("slp"))
        sdr = E(nc.semaphore("sdr"))

        k_id_bf = E(nc.sbuf_tensor("k_id_bf", [128, 128], BF16))
        k_id_f32 = E(nc.sbuf_tensor("k_id_f32", [128, 128], F32))
        k_onec = E(nc.sbuf_tensor("k_onec", [128, 1], BF16))
        k_oner = E(nc.sbuf_tensor("k_oner", [1, 128], BF16))
        k_oner32 = E(nc.sbuf_tensor("k_oner32", [1, 128], F32))
        k_negbigI = E(nc.sbuf_tensor("k_negbigI", [128, 128], BF16))
        k_triL = E(nc.sbuf_tensor("k_triL", [128, 128], BF16))
        k_zcol = E(nc.sbuf_tensor("k_zcol", [128, 1], F32))
        k_eps = E(nc.sbuf_tensor("k_eps", [128, 1], F32))
        k_convw = E(nc.sbuf_tensor("k_convw", [128, NPX * KC], F32))
        k_convb = E(nc.sbuf_tensor("k_convb", [128, NPX], F32))
        k_dtb = E(nc.sbuf_tensor("k_dtb", [64, 1], F32))
        k_alog = E(nc.sbuf_tensor("k_alog", [64, 1], F32))
        k_dexp = E(nc.sbuf_tensor("k_dexp", [128, NPI], F32))
        k_nw = E(nc.sbuf_tensor("k_nw", [128, NPI], F32))
        k_upsc = E(nc.sbuf_tensor("k_upsc", [128, NPH], F32))
        k_dnsc = E(nc.sbuf_tensor("k_dnsc", [128, NPD], F32))
        k_oscr = E(nc.sbuf_tensor("k_oscr", [128, 1], F32))
        k_chain = E(nc.sbuf_tensor("k_chain", [1, 9], F32))
        p_dtc = E(nc.sbuf_tensor("p_dtc", [128, T + 8], F32))
        p_dt = p_dtc[0:64, 0:T]
        p_cdtp = p_dtc[64:128, 0:T + 8]
        p_ach = E(nc.sbuf_tensor("p_ach", [64, 32], F32))
        p_expA = E(nc.sbuf_tensor("p_expA", [64, 1], F32))
        p_delt = E(nc.sbuf_tensor("p_delt", [64, 1], F32))
        p_dsco = E(nc.sbuf_tensor("p_dsco", [128, NPD], F32))
        xnTh = E(nc.sbuf_tensor("xnTh", [128, NKD * 3], BF16))

        nc.const_aps.register(F32, EPS, k_eps[:, :])
        blk = E(nc.Block())

        def bar(e, sem, instr):
            e.wait_ge(sem, ctr.inc(instr, sem, 1))

        def stage_barrier():
            allsems = [sx, sw, st, pe, dv, ac, pl] + swp + sxp + hlp + gsp + osp \
                + gbp + fnp + [sxb, slp, sdr] + bcp
            snap = [(s_, ctr.v(s_)) for s_ in allsems if ctr.v(s_) > 0]
            for eng in (blk.sync, blk.gpsimd, blk.tensor, blk.vector, blk.scalar):
                @eng
                def _(e, snap=snap):
                    for s_, v_ in snap:
                        e.wait_ge(s_, v_)

        @blk.sync
        def _(s):
            for dst, src in [
                (k_id_bf[:, :], id_bf), (k_id_f32[:, :], id_f32),
                (k_onec[:, :], onec_bf), (k_oner[:, :], oner_bf),
                (k_oner32[:, :], oner_f32),
                (k_negbigI[:, :], negbigI), (k_triL[:, :], triL),
                (k_zcol[:, :], zcol),
                (k_eps[:, :], epscol),
                (k_convw[:, :], convw),
                (k_convb[:, :], convb),
                (k_dtb[:, :], dtb), (k_alog[:, :], alog),
                (k_dexp[:, :], dexp),
                (k_nw[:, :], nw_i),
                (k_upsc[:, :], upsc),
                (k_dnsc[:, :], dnsc),
                (k_oscr[:, :], oscr), (k_chain[:, :], chain),
            ]:
                ctr.inc(s.dma_start(out=dst, in_=src), sx, 16)
        c0 = ctr.v(sx)

        arA_ctx = ExitStack()
        arA = arA_ctx.enter_context(nc.sbuf_tensor("arA", [128, NKD * T], BF16))
        TCP0 = 256
        NP0 = T // TCP0
        with ExitStack() as sc_:
            SE = sc_.enter_context
            xbf = SE(nc.sbuf_tensor("xbf", [128, NKD * TCP0], BF16))
            xsq = SE(nc.sbuf_tensor("xsq", [128, NKD * TCP0], BF16))
            rows = SE(nc.sbuf_tensor("rows", [1, TCP0 * 6], F32))
            rowb = SE(nc.sbuf_tensor("rowb", [1, TCP0 * 4], BF16))
            rh32 = SE(nc.sbuf_tensor("rh32", [1, TCP0 * 2], F32))
            mrb = SE(nc.sbuf_tensor("mrb", [128, TCP0 * 2], BF16))
            pS = [SE(nc.psum_tensor(f"pS{i}", [128, TCP0], F32)) for i in range(4)]

            prev_done = {"dv": None, "pe_stats": None, "ac_bce": None}

            def p0_chunk(tcols, src_ap, dst_fn, first):
                @blk.gpsimd
                def _(g):
                    if first:
                        g.wait_ge(sx, c0)
                    if prev_done["dv"] is not None:
                        g.wait_ge(dv, prev_done["dv"])
                    ctr.inc(g.dma_start(
                        out=xbf[:, 0:NKD * tcols].rearrange("p (kt t) -> p kt t", t=tcols),
                        in_=src_ap), sw, 16)
                ld = ctr.v(sw)

                @blk.scalar
                def _(s):
                    s.wait_ge(sw, ld)
                    for kt in range(NKD):
                        a = s.activation(xsq[:, kt * tcols:(kt + 1) * tcols],
                                         xbf[:, kt * tcols:(kt + 1) * tcols], AF.Square)
                    ctr.inc(a, ac, 1)
                sqd = ctr.v(ac)

                @blk.tensor
                def _(t):
                    t.wait_ge(ac, sqd)
                    if prev_done["dv"] is not None:
                        t.wait_ge(dv, prev_done["dv"])
                    for kt in range(NKD):
                        t.matmul(pS[0][0:1, 0:tcols], k_onec[:, :],
                                 xbf[:, kt * tcols:(kt + 1) * tcols],
                                 start=(kt == 0), stop=(kt == NKD - 1))
                    for kt in range(NKD):
                        m = t.matmul(pS[1][0:1, 0:tcols], k_onec[:, :],
                                     xsq[:, kt * tcols:(kt + 1) * tcols],
                                     start=(kt == 0), stop=(kt == NKD - 1))
                    ctr.inc(m, pe, 1)
                std = ctr.v(pe)

                @blk.vector
                def _(v):
                    v.wait_ge(pe, std)
                    TP = TCP0
                    m_ = rows[0:1, 0:tcols]
                    e2 = rows[0:1, TP:TP + tcols]
                    var = rows[0:1, 2 * TP:2 * TP + tcols]
                    v.tensor_scalar_mul(m_, pS[0][0:1, 0:tcols], 1.0 / D)
                    bar(v, dv, v.tensor_scalar_mul(e2, pS[1][0:1, 0:tcols], 1.0 / D))
                    bar(v, dv, v.tensor_tensor(var, m_, m_, ALU.mult))
                    a = v.tensor_tensor(var, e2, var, ALU.subtract)
                    ctr.inc(a, dv, 1)
                vd = ctr.v(dv)

                @blk.scalar
                def _(s):
                    s.wait_ge(dv, vd)
                    TP = TCP0
                    bar(s, ac, s.activation(rows[0:1, 3 * TP:3 * TP + tcols],
                                            rows[0:1, 2 * TP:2 * TP + tcols], AF.Ln,
                                            bias=EPS))
                    a = s.activation(rows[0:1, 4 * TP:4 * TP + tcols],
                                     rows[0:1, 3 * TP:3 * TP + tcols], AF.Exp, scale=-0.5)
                    ctr.inc(a, ac, 1)
                sq2 = ctr.v(ac)

                @blk.vector
                def _(v):
                    v.wait_ge(ac, sq2)
                    TP = TCP0
                    rstd = rows[0:1, 4 * TP:4 * TP + tcols]
                    mr = rows[0:1, 5 * TP:5 * TP + tcols]
                    mh, ml = rowb[0:1, 0:tcols], rowb[0:1, TP:TP + tcols]
                    rh, rl = rowb[0:1, 2 * TP:2 * TP + tcols], rowb[0:1, 3 * TP:3 * TP + tcols]
                    v.tensor_tensor(mr, rows[0:1, 0:tcols], rstd, ALU.mult)
                    bar(v, dv, v.tensor_copy(rh, rstd))
                    v.tensor_copy(mh, mr)
                    bar(v, dv, v.tensor_copy(rh32[0:1, TP:TP + tcols], rh))
                    v.tensor_copy(rh32[0:1, 0:tcols], mh)
                    bar(v, dv, v.tensor_tensor(rl, rstd, rh32[0:1, TP:TP + tcols], ALU.subtract))
                    a = v.tensor_tensor(ml, mr, rh32[0:1, 0:tcols], ALU.subtract)
                    ctr.inc(a, dv, 1)
                rd = ctr.v(dv)

                @blk.tensor
                def _(t):
                    t.wait_ge(dv, rd)
                    TP = TCP0
                    if prev_done["ac_bce"] is not None:
                        t.wait_ge(ac, prev_done["ac_bce"])
                    t.matmul(pS[2][:, 0:tcols], k_oner[:, :], rowb[0:1, 0:tcols],
                             start=True, stop=False)
                    t.matmul(pS[2][:, 0:tcols], k_oner[:, :], rowb[0:1, TP:TP + tcols],
                             start=False, stop=True)
                    t.matmul(pS[3][:, 0:tcols], k_oner[:, :], rowb[0:1, 2 * TP:2 * TP + tcols],
                             start=True, stop=False)
                    m = t.matmul(pS[3][:, 0:tcols], k_oner[:, :], rowb[0:1, 3 * TP:3 * TP + tcols],
                                 start=False, stop=True)
                    ctr.inc(m, pe, 1)
                bcd = ctr.v(pe)

                @blk.scalar
                def _(s):
                    s.wait_ge(pe, bcd)
                    s.copy(mrb[:, 0:tcols], pS[2][:, 0:tcols])
                    a = s.copy(mrb[:, TCP0:TCP0 + tcols], pS[3][:, 0:tcols])
                    ctr.inc(a, ac, 1)
                prev_done["ac_bce"] = ctr.v(ac)
                bce = ctr.v(ac)

                @blk.vector
                def _(v):
                    v.wait_ge(ac, bce)
                    for kt in range(NKD):
                        xs = xbf[:, kt * tcols:(kt + 1) * tcols]
                        a = v.tensor_tensor(xsq[:, kt * tcols:(kt + 1) * tcols], xs,
                                            mrb[:, TCP0:TCP0 + tcols], ALU.mult)
                    bar(v, dv, a)
                    for kt in range(NKD):
                        a = v.tensor_tensor(dst_fn(kt),
                                            xsq[:, kt * tcols:(kt + 1) * tcols],
                                            mrb[:, 0:tcols], ALU.subtract)
                    ctr.inc(a, dv, 1)
                prev_done["dv"] = ctr.v(dv)
                return ctr.v(dv)

            src_h = xTh.rearrange("(kt p) t -> p kt t", p=128)
            p0_chunk(3, src_h, lambda kt: xnTh[:, kt * 3:(kt + 1) * 3], True)

            for tcc in range(NP0):
                src = xT.rearrange("(kt p) t -> p kt t", p=128)[:, :, tcc * TCP0:(tcc + 1) * TCP0]
                done = p0_chunk(
                    TCP0, src,
                    lambda kt, tcc=tcc: arA[:, kt * T + tcc * TCP0: kt * T + (tcc + 1) * TCP0],
                    False)

                @blk.scalar
                def _(s, tcc=tcc, done=done):
                    s.wait_ge(dv, done)
                    dst = sc_xnT.rearrange("(kt p) t -> p kt t", p=128)[
                        :, :, tcc * TCP0:(tcc + 1) * TCP0]
                    src2 = arA.rearrange("p (kt t) -> p kt t", t=T)[
                        :, :, tcc * TCP0:(tcc + 1) * TCP0]
                    ctr.inc(s.dma_start(out=dst, in_=src2), st, 16)
            p0_st = ctr.v(st)
            p0_dv = ctr.v(dv)

        with ExitStack() as sc_:
            SE = sc_.enter_context
            wbuf = [SE(nc.sbuf_tensor(f"wbuf{i}", [128, NKD * 128], BF16)) for i in range(2)]
            xbcr = SE(nc.sbuf_tensor("xbcr", [128, T + 3], F32))
            accb = SE(nc.sbuf_tensor("accb", [128, T], F32))
            ostg = [SE(nc.sbuf_tensor(f"ostg{i}", [128, T], BF16)) for i in range(2)]
            pM = [SE(nc.psum_tensor(f"pM{i}", [128, TC], F32)) for i in range(2)]
            pHh = SE(nc.psum_tensor("pHh", [128, 8], F32))
            sp1 = accb[0:64, :]
            sp2 = xbcr[0:64, 0:T]

            def btype(b):
                if b == 0:
                    return "dt"
                if b <= NPI:
                    return "x"
                if b <= NPI + NG:
                    return "B"
                if b <= NPI + 2 * NG:
                    return "C"
                return "gate"

            wld = {}
            mm_ms = {}
            evac_ms = {}
            accinit_ms = {}
            conv_ms = {}
            convh0_ms = {}
            silu_ms = {}
            spill_ms = {}
            dtmath_dv = [None]

            @blk.gpsimd
            def _(g):
                g.wait_ge(dv, p0_dv)
                for b in range(2):
                    ctr.inc(g.dma_start(
                        out=wbuf[b][:, :].rearrange("p (kt m) -> p kt m", m=128),
                        in_=wproj[b].transpose([1, 0, 2])), swp[b % 2], 16)
                    wld[b] = ctr.v(swp[b % 2])

            for b in range(NBLK):
                bt = btype(b)

                if bt in ("x", "B", "C"):
                    @blk.tensor
                    def _(t, b=b):
                        t.wait_ge(swp[b % 2], wld[b])
                        if (b - 1, 0) in evac_ms:
                            t.wait_ge(ac, evac_ms[(b - 1, 0)])
                        for kt in range(NKD):
                            t.matmul(pHh[:, 0:3], wbuf[b % 2][:, kt * 128:(kt + 1) * 128],
                                     xnTh[:, kt * 3:(kt + 1) * 3],
                                     start=(kt == 0), stop=(kt == NKD - 1))

                ac_base = ctr.v(ac)
                dv_base = ctr.v(dv)
                for tcc in range(NTC):
                    if bt == "dt":
                        evac_ms[(b, tcc)] = ac_base + 3 * (tcc + 1)
                    else:
                        evac_ms[(b, tcc)] = ac_base + tcc + 1
                for tcc in range(NTC):
                    @blk.tensor
                    def _(t, b=b, tcc=tcc):
                        if tcc == 0:
                            t.wait_ge(swp[b % 2], wld[b])
                        it = b * NTC + tcc
                        if it >= 2:
                            kk = ((b, tcc - 2) if tcc >= 2
                                  else (b - 1, NTC - 2 + tcc))
                            t.wait_ge(ac, evac_ms[kk])
                            if kk in accinit_ms:
                                t.wait_ge(dv, accinit_ms[kk])
                        for kt in range(NKD):
                            mm = t.matmul(pM[tcc % 2][:, :],
                                          wbuf[b % 2][:, kt * 128:(kt + 1) * 128],
                                          arA[:, kt * T + tcc * TC: kt * T + (tcc + 1) * TC],
                                          start=(kt == 0), stop=(kt == NKD - 1))
                        ctr.inc(mm, pe, 1)
                        mm_ms[(b, tcc)] = ctr.v(pe)
                    if bt == "dt":
                        @blk.scalar
                        def _(s, b=b, bt=bt, tcc=tcc):
                            if tcc == 0:
                                if bt in ("x", "B", "C"):
                                    if b in (1, 2) and dtmath_dv[0] is not None:
                                        s.wait_ge(dv, dtmath_dv[0])
                                    if (b - 1) in silu_ms:
                                        s.wait_ge(ac, silu_ms[b - 1])
                                    if (b - 2) in spill_ms:
                                        s.wait_ge(sxp[b % 2], spill_ms[b - 2])
                                if bt == "gate" and (b - 2) in spill_ms:
                                    s.wait_ge(sxp[b % 2], spill_ms[b - 2])
                            s.wait_ge(pe, mm_ms[(b, tcc)])
                            p = pM[tcc % 2]
                            c0_, c1_ = tcc * TC, (tcc + 1) * TC
                            if bt == "gate":
                                a = s.activation(ostg[b % 2][:, c0_:c1_], p[:, :], AF.Silu)
                            elif bt == "dt":
                                bar(s, ac, s.activation(sp1[:, c0_:c1_], p[0:64, :],
                                                        AF.Abs, bias=k_dtb[:, :]))
                                bar(s, ac, s.activation(sp1[:, c0_:c1_], sp1[:, c0_:c1_],
                                                        AF.Exp, scale=-1.0))
                                s.activation(sp1[:, c0_:c1_], sp1[:, c0_:c1_], AF.Ln, bias=1.0)
                                a = s.activation(sp2[:, c0_:c1_], p[0:64, :], AF.Relu,
                                                 bias=k_dtb[:, :])
                            else:
                                if tcc == 0:
                                    s.copy(xbcr[:, 0:3], pHh[:, 0:3])
                                a = s.copy(xbcr[:, 3 + c0_:3 + c1_], p[:, :])
                            ctr.inc(a, ac, 1)
                            evac_ms[(b, tcc)] = ctr.v(ac)

                if bt != "dt":
                    @blk.scalar
                    def _(s, b=b, bt=bt):
                        if bt in ("x", "B", "C"):
                            if b in (1, 2) and dtmath_dv[0] is not None:
                                s.wait_ge(dv, dtmath_dv[0])
                        if bt == "gate" and (b - 2) in spill_ms:
                            s.wait_ge(sxp[b % 2], spill_ms[b - 2])
                        for tcc in range(NTC):
                            if bt in ("x", "B", "C"):
                                if tcc == 0 and (b - 1) in convh0_ms:
                                    s.wait_ge(convh0_ms[b - 1][0],
                                              convh0_ms[b - 1][1])
                                if tcc == NTC // 2 and (b - 1) in conv_ms:
                                    s.wait_ge(conv_ms[b - 1][0], conv_ms[b - 1][1])
                            s.wait_ge(pe, mm_ms[(b, tcc)])
                            p = pM[tcc % 2]
                            c0_, c1_ = tcc * TC, (tcc + 1) * TC
                            if bt == "gate":
                                a = s.activation(ostg[b % 2][:, c0_:c1_], p[:, :], AF.Silu)
                            else:
                                if tcc == 0:
                                    s.copy(xbcr[:, 0:3], pHh[:, 0:3])
                                a = s.copy(xbcr[:, 3 + c0_:3 + c1_], p[:, :])
                            ctr.inc(a, ac, 1)
                            assert ctr.v(ac) == evac_ms[(b, tcc)], (b, tcc, ctr.v(ac))

                if b + 2 < NBLK:
                    @blk.gpsimd
                    def _(g, b=b):
                        g.wait_ge(pe, mm_ms[(b, NTC - 1)])
                        ctr.inc(g.dma_start(
                            out=wbuf[b % 2][:, :].rearrange("p (kt m) -> p kt m", m=128),
                            in_=wproj[b + 2].transpose([1, 0, 2])), swp[b % 2], 16)
                        wld[b + 2] = ctr.v(swp[b % 2])

                if bt in ("x", "B", "C"):
                    use_pool = False
                    eng = blk.gpsimd if use_pool else blk.vector
                    sem = pl if use_pool else dv

                    TH = T // 2
                    for hf_ in range(2):
                        @eng
                        def _(e, b=b, hf_=hf_):
                            e.wait_ge(ac, evac_ms[(b, (NTC // 2) * (hf_ + 1) - 1)])
                            if hf_ == 0 and (b - 1) in silu_ms:
                                e.wait_ge(ac, silu_ms[b - 1])
                            o0 = hf_ * TH
                            bar(e, sem, e.tensor_scalar(
                                accb[:, o0:o0 + TH], xbcr[:, 3 + o0:3 + o0 + TH],
                                k_convw[:, (b - 1) * KC + 3:(b - 1) * KC + 4],
                                k_convb[:, b - 1:b], ALU.mult, ALU.add))
                            for k in range(3):
                                a = e.scalar_tensor_tensor(
                                    accb[:, o0:o0 + TH], xbcr[:, k + o0:k + o0 + TH],
                                    k_convw[:, (b - 1) * KC + k:(b - 1) * KC + k + 1],
                                    accb[:, o0:o0 + TH], ALU.mult, ALU.add)
                                if k < 2:
                                    bar(e, sem, a)
                            ctr.inc(a, sem, 1)
                        if hf_ == 0:
                            convh0_ms[b] = (sem, ctr.v(sem))
                    conv_ms[b] = (sem, ctr.v(sem))

                    @blk.scalar
                    def _(s, b=b):
                        s.wait_ge(conv_ms[b][0], conv_ms[b][1])
                        if (b - 2) in spill_ms:
                            s.wait_ge(sxp[b % 2], spill_ms[b - 2])
                        a = s.activation(ostg[b % 2][:, :], accb[:, :], AF.Silu)
                        ctr.inc(a, ac, 1)
                    silu_ms[b] = ctr.v(ac)

                    @blk.sync
                    def _(s, b=b):
                        s.wait_ge(ac, silu_ms[b])
                        ctr.inc(s.dma_start(out=sc_xbc[(b - 1) * 128:b * 128, :],
                                            in_=ostg[b % 2][:, :]), sxp[b % 2], 16)
                    spill_ms[b] = ctr.v(sxp[b % 2])
                elif bt == "gate":
                    gi = b - (1 + NPI + 2 * NG)

                    @blk.sync
                    def _(s, b=b, gi=gi):
                        s.wait_ge(ac, evac_ms[(b, NTC - 1)])
                        ctr.inc(s.dma_start(out=sc_gate[gi * 128:(gi + 1) * 128, :],
                                            in_=ostg[b % 2][:, :]), sxp[b % 2], 16)
                    spill_ms[b] = ctr.v(sxp[b % 2])
                else:
                    @blk.vector
                    def _(v):
                        v.wait_ge(ac, evac_ms[(0, NTC - 1)])
                        bar(v, dv, v.tensor_tensor(p_dt[:, :], sp1[:, 0:T], sp2[:, 0:T],
                                                   ALU.add))
                        v.tensor_copy(p_cdtp[:, 0:1], k_zcol[0:64, :])
                        a = v.tensor_tensor_scan(
                            p_cdtp[:, 1:T + 1], p_dt[:, :],
                            k_zcol[0:64, :].broadcast_to([64, T]), 0.0, ALU.add, ALU.add)
                        ctr.inc(a, dv, 1)
                    c1 = ctr.v(dv)

                    @blk.scalar
                    def _(s, c1=c1):
                        s.wait_ge(dv, c1)
                        a = s.activation(p_expA[:, :], k_alog[:, :], AF.Exp)
                        ctr.inc(a, ac, 1)
                    c2 = ctr.v(ac)

                    @blk.vector
                    def _(v, c2=c2):
                        v.wait_ge(ac, c2)
                        ce = p_cdtp[:, 128:T + 1:128]
                        cs = p_cdtp[:, 0:T:128]
                        bar(v, dv, v.tensor_tensor(p_ach[:, :], ce, cs, ALU.subtract))
                        v.tensor_scalar(p_ach[:, :], p_ach[:, :], p_expA[:, :], -1.0,
                                        ALU.mult, ALU.mult)
                        a = v.tensor_scalar(p_delt[:, :], p_cdtp[:, T:T + 1],
                                            p_expA[:, :], -1.0, ALU.mult, ALU.mult)
                        ctr.inc(a, dv, 1)
                    c3 = ctr.v(dv)

                    @blk.scalar
                    def _(s, c3=c3):
                        s.wait_ge(dv, c3)
                        s.activation(p_ach[:, :], p_ach[:, :], AF.Exp)
                        a = s.activation(p_delt[:, :], p_delt[:, :], AF.Exp)
                        ctr.inc(a, ac, 1)
                    c4 = ctr.v(ac)
                    dtmath_dv[0] = c3
                    dtmath_ac = c4

                    dtmath_dv[0] = c1

            p1_sxp = (ctr.v(sxp[0]), ctr.v(sxp[1]))
            p1_pe = ctr.v(pe)
        arA_ctx.close()

        stage_barrier()
        with ExitStack() as sc_:
            SE = sc_.enter_context
            Xa = SE(nc.sbuf_tensor("Xa", [128, NPI * TC], BF16))
            Ba = SE(nc.sbuf_tensor("Ba", [128, NG * TC], BF16))
            XTa = SE(nc.sbuf_tensor("XTa", [128, 4 * INTER], BF16))
            BTa = SE(nc.sbuf_tensor("BTa", [128, 4 * GN], BF16))
            XwTa = SE(nc.sbuf_tensor("XwTa", [128, 4 * INTER], BF16))
            wcha = SE(nc.sbuf_tensor("wcha", [64, 4 * 128], F32))
            wTba = SE(nc.sbuf_tensor("wTba", [128, 4 * 64], BF16))
            lam = SE(nc.sbuf_tensor("lam", [128, INTER], F32))
            pTa = [SE(nc.psum_tensor(f"pTa{i}", [128, 1024], BF16)) for i in range(2)]
            pWa = SE(nc.psum_tensor("pWa", [128, 512], F32))
            pLa = [SE(nc.psum_tensor(f"pLa{i}", [128, TC], F32)) for i in range(4)]

            sa_tr_cons = None
            sa_lam = None
            for scc in range(8):
                @blk.sync
                def _(s, scc=scc):
                    if scc == 0:
                        s.wait_ge(sxp[0], p1_sxp[0])
                        s.wait_ge(sxp[1], p1_sxp[1])
                    else:
                        s.wait_ge(pe, sa_trd)
                    rr = sc_xbc.rearrange("(pt p) t -> p pt t", p=128)
                    sl = slice(scc * TC, (scc + 1) * TC)
                    ctr.inc(s.dma_start(out=Xa.rearrange("p (pt t) -> p pt t", t=TC),
                                        in_=rr[:, 0:NPI, sl]), sx, 16)
                    ctr.inc(s.dma_start(out=Ba.rearrange("p (pt t) -> p pt t", t=TC),
                                        in_=rr[:, NPI:NPI + NG, sl]), sx, 16)
                lda = ctr.v(sx)

                @blk.vector
                def _(v, scc=scc):
                    if scc == 0:
                        v.wait_ge(ac, dtmath_ac)
                    for c in range(4):
                        cg = scc * 4 + c
                        a = v.tensor_scalar(wcha[:, c * 128:(c + 1) * 128],
                                            p_cdtp[:, cg * 128 + 1:cg * 128 + 129],
                                            p_cdtp[:, T:T + 1], p_expA[:, :],
                                            ALU.subtract, ALU.mult)
                    ctr.inc(a, dv, 1)
                w1 = ctr.v(dv)

                @blk.scalar
                def _(s, w1=w1):
                    s.wait_ge(dv, w1)
                    a = s.activation(wcha[:, 0:512], wcha[:, 0:512], AF.Exp)
                    ctr.inc(a, ac, 1)
                w2 = ctr.v(ac)

                @blk.vector
                def _(v, scc=scc, w2=w2):
                    v.wait_ge(ac, w2)
                    for c in range(4):
                        cg = scc * 4 + c
                        a = v.tensor_tensor(wcha[:, c * 128:(c + 1) * 128],
                                            wcha[:, c * 128:(c + 1) * 128],
                                            p_dt[:, cg * 128:(cg + 1) * 128], ALU.mult)
                    ctr.inc(a, dv, 1)
                w3 = ctr.v(dv)

                tiles = []
                for c in range(4):
                    for pt in range(NPI):
                        tiles.append(("X", c, pt))
                    for g_ in range(NG):
                        tiles.append(("B", c, g_))
                waves = [tiles[i:i + 8] for i in range(0, len(tiles), 8)]
                pTa_ev = pTa_ev if scc > 0 else {0: None, 1: None}
                last_wave_ev = None
                for wv, wtiles in enumerate(waves):
                    @blk.tensor
                    def _(t, wv=wv, wtiles=wtiles, scc=scc, lda=lda, w3=w3):
                        if wv == 0:
                            t.wait_ge(sx, lda)
                            t.wait_ge(dv, w3)
                            if sa_tr_cons is not None:
                                t.wait_ge(dv, sa_tr_cons)
                        if pTa_ev[wv % 2] is not None:
                            t.wait_ge(dv, pTa_ev[wv % 2])
                        p = pTa[wv % 2]
                        for i, (kind, c, j) in enumerate(wtiles):
                            src = (Xa[:, j * TC + c * 128: j * TC + c * 128 + 128]
                                   if kind == "X" else
                                   Ba[:, j * TC + c * 128: j * TC + c * 128 + 128])
                            mm = t.transpose(p[:, i * 128:(i + 1) * 128], src, k_id_bf[:, :])
                        ctr.inc(mm, pe, 1)
                    twd = ctr.v(pe)

                    @blk.vector
                    def _(v, wv=wv, wtiles=wtiles, twd=twd):
                        v.wait_ge(pe, twd)
                        p = pTa[wv % 2]
                        for i, (kind, c, j) in enumerate(wtiles):
                            dst = (XTa[:, c * INTER + j * 128: c * INTER + (j + 1) * 128]
                                   if kind == "X" else
                                   BTa[:, c * GN + j * 128: c * GN + (j + 1) * 128])
                            a = v.tensor_copy(dst, p[:, i * 128:(i + 1) * 128])
                        ctr.inc(a, dv, 1)
                    pTa_ev[wv % 2] = ctr.v(dv)
                    last_wave_ev = ctr.v(dv)
                sa_trd = ctr.v(pe)

                @blk.tensor
                def _(t, w3=w3):
                    t.wait_ge(dv, w3)
                    for c in range(4):
                        mm = t.transpose(pWa[:, c * 64:(c + 1) * 64],
                                         wcha[:, c * 128:(c + 1) * 128],
                                         k_id_f32[0:64, 0:64])
                    ctr.inc(mm, pe, 1)
                wtd = ctr.v(pe)

                @blk.vector
                def _(v, wtd=wtd):
                    v.wait_ge(pe, wtd)
                    a = v.tensor_copy(wTba[:, :], pWa[:, 0:256])
                    ctr.inc(a, dv, 1)
                wte = ctr.v(dv)

                @blk.vector
                def _(v, wte=wte, lw=last_wave_ev):
                    v.wait_ge(dv, lw)
                    v.wait_ge(dv, wte)
                    for c in range(4):
                        a = v.tensor_tensor(
                            XwTa[:, c * INTER:(c + 1) * INTER].rearrange(
                                "p (h d) -> p h d", d=64),
                            XTa[:, c * INTER:(c + 1) * INTER].rearrange(
                                "p (h d) -> p h d", d=64),
                            wTba[:, c * 64:(c + 1) * 64].unsqueeze(2).broadcast_to(
                                [128, 64, 64]),
                            ALU.mult)
                    ctr.inc(a, dv, 1)
                xwd = ctr.v(dv)

                for half in range(2):
                    @blk.tensor
                    def _(t, half=half, xwd=xwd, scc=scc):
                        t.wait_ge(dv, xwd)
                        if sa_lam is not None:
                            t.wait_ge(dv, sa_lam)
                        for gi in range(4):
                            g_ = half * 4 + gi
                            for c in range(4):
                                mm = t.matmul(
                                    pLa[gi][:, :],
                                    BTa[:, c * GN + g_ * 128: c * GN + (g_ + 1) * 128],
                                    XwTa[:, c * INTER + g_ * 512: c * INTER + g_ * 512 + 512],
                                    start=(c == 0), stop=(c == 3))
                        ctr.inc(mm, pe, 1)
                    cbd = ctr.v(pe)

                    @blk.vector
                    def _(v, half=half, cbd=cbd, scc=scc, prev=sa_lam):
                        v.wait_ge(pe, cbd)
                        if prev is not None:
                            v.wait_ge(dv, prev)
                        for gi in range(4):
                            g_ = half * 4 + gi
                            if scc == 0:
                                a = v.tensor_copy(lam[:, g_ * 512:(g_ + 1) * 512],
                                                  pLa[gi][:, :])
                            else:
                                a = v.tensor_tensor(lam[:, g_ * 512:(g_ + 1) * 512],
                                                    lam[:, g_ * 512:(g_ + 1) * 512],
                                                    pLa[gi][:, :], ALU.add)
                        ctr.inc(a, dv, 1)
                    sa_lam = ctr.v(dv)
                sa_tr_cons = xwd
                sa_tr_cons = sa_lam

            @blk.scalar
            def _(s):
                s.wait_ge(dv, sa_lam)
                s.wait_ge(ac, dtmath_ac)
                ctr.inc(s.dma_start(out=cc_in[0:128, 0:T], in_=lam[:, :]), st, 16)
                ctr.inc(s.dma_start(out=cc_in[:, T:T + 1], in_=k_zcol[:, :],
                                    allow_slow_non_contiguous=True), st, 16)
                s.wait_ge(st, ctr.v(st))
                ctr.inc(s.dma_start(out=cc_in[0:64, T:T + 1], in_=p_delt[:, :],
                                    allow_slow_non_contiguous=True), st, 16)
            payd = ctr.v(st)

            @blk.gpsimd
            def _(g):
                g.wait_ge(st, payd)
                a = g.collective_compute(
                    "AllGather", ALU.bypass,
                    replica_groups=[[0, 1, 2, 3], [4, 5, 6, 7]],
                    ins=[cc_in.opt()], outs=[cc_out.opt()])
                ctr.inc(a, cc, 1)
            ccd = ctr.v(cc)

        stage_barrier()
        with ExitStack() as sc_:
            SE = sc_.enter_context
            arA = SE(nc.sbuf_tensor("arB", [128, NKD * T], BF16))
            wb2 = [SE(nc.sbuf_tensor(f"wb2{i}", [128, NKD * 128], BF16)) for i in range(2)]
            ws2 = [SE(nc.sbuf_tensor(f"ws2{i}", [128, NKD * 128], BF16)) for i in range(2)]
            hst = [SE(nc.sbuf_tensor(f"hst{i}", [128, T], BF16)) for i in range(2)]
            pF = [SE(nc.psum_tensor(f"pF{i}", [128, TC], F32)) for i in range(2)]

            @blk.sync
            def _(s):
                s.wait_ge(pe, p1_pe)
                s.wait_ge(st, p0_st)
                for kt in range(NKD):
                    ctr.inc(s.dma_start(out=arA[:, kt * T:(kt + 1) * T],
                                        in_=sc_xnT[kt * 128:(kt + 1) * 128, :]), sx, 16)
            f1xn = ctr.v(sx)

            f1_wld = {}
            f1_sgn = {}
            f1_mm = {}
            f1_ev = {}
            f1_sp = {}

            @blk.gpsimd
            def _(g):
                for hb in range(2):
                    ctr.inc(g.dma_start(
                        out=wb2[hb][:, :].rearrange("p (kt m) -> p kt m", m=128),
                        in_=wup[hb].transpose([1, 0, 2])), swp[hb % 2], 16)
                    f1_wld[hb] = ctr.v(swp[hb % 2])

            for hb in range(NPH):
                @blk.scalar
                def _(s, hb=hb):
                    s.wait_ge(swp[hb % 2], f1_wld[hb])
                    if hb >= 2:
                        s.wait_ge(pe, f1_mm[(hb - 2, NTC - 1)])
                    a = s.activation(ws2[hb % 2][:, :], wb2[hb % 2][:, :], AF.Sign)
                    ctr.inc(a, ac, 1)
                f1_sgn[hb] = ctr.v(ac)

                if hb + 2 < NPH:
                    @blk.gpsimd
                    def _(g, hb=hb):
                        g.wait_ge(ac, f1_sgn[hb])
                        ctr.inc(g.dma_start(
                            out=wb2[hb % 2][:, :].rearrange("p (kt m) -> p kt m", m=128),
                            in_=wup[hb + 2].transpose([1, 0, 2])), swp[hb % 2], 16)
                        f1_wld[hb + 2] = ctr.v(swp[hb % 2])

                ac_base_f1 = ctr.v(ac)
                for tcc in range(NTC):
                    f1_ev[hb * NTC + tcc] = ac_base_f1 + tcc + 1

                for tcc in range(NTC):
                    @blk.tensor
                    def _(t, hb=hb, tcc=tcc):
                        if tcc == 0:
                            if hb == 0:
                                t.wait_ge(sx, f1xn)
                            t.wait_ge(ac, f1_sgn[hb])
                        it = hb * NTC + tcc
                        if it >= 2:
                            t.wait_ge(ac, f1_ev[it - 2])
                        for kt in range(NKD):
                            mm = t.matmul(pF[tcc % 2][:, :],
                                          ws2[hb % 2][:, kt * 128:(kt + 1) * 128],
                                          arA[:, kt * T + tcc * TC: kt * T + (tcc + 1) * TC],
                                          start=(kt == 0), stop=(kt == NKD - 1))
                        ctr.inc(mm, pe, 1)
                        f1_mm[(hb, tcc)] = ctr.v(pe)

                @blk.scalar
                def _(s, hb=hb):
                    if hb >= 2:
                        s.wait_ge(sxp[hb % 2], f1_sp[hb - 2])
                    for tcc in range(NTC):
                        s.wait_ge(pe, f1_mm[(hb, tcc)])
                        a = s.activation(hst[hb % 2][:, tcc * TC:(tcc + 1) * TC],
                                         pF[tcc % 2][:, :], AF.Relu,
                                         scale=k_upsc[:, hb:hb + 1])
                        ctr.inc(a, ac, 1)
                        assert ctr.v(ac) == f1_ev[hb * NTC + tcc]

                @blk.sync
                def _(s, hb=hb):
                    s.wait_ge(ac, f1_ev[hb * NTC + NTC - 1])
                    ctr.inc(s.dma_start(out=sc_hid[hb * 128:(hb + 1) * 128, :],
                                        in_=hst[hb % 2][:, :]), sxp[hb % 2], 16)
                f1_sp[hb] = ctr.v(sxp[hb % 2])
            f1_pe = ctr.v(pe)
            f1_sxp = (ctr.v(sxp[0]), ctr.v(sxp[1]))

        stage_barrier()
        with ExitStack() as sc_:
            SE = sc_.enter_context
            TC2 = 256
            NTC2 = 16
            GRP = [(0, 6), (6, 6), (12, 4)]
            dnw = SE(nc.sbuf_tensor("dnw", [128, 6 * NPH * 128], BF16))
            hb2 = [SE(nc.sbuf_tensor(f"hb2{i}", [128, NPH * TC2], BF16)) for i in range(2)]
            gstg = [SE(nc.sbuf_tensor(f"gstg{i}", [128, TC2], BF16)) for i in range(2)]
            pF2 = [SE(nc.psum_tensor(f"pF2{i}", [128, TC2], F32)) for i in range(2)]

            @blk.vector
            def _(v):
                v.wait_ge(sx, c0)
                a = v.tensor_scalar(p_dsco[:, :], k_dnsc[:, :], k_oscr[:, :], None, ALU.mult)
                ctr.inc(a, dv, 1)
            dscd = ctr.v(dv)

            f2_mm = {}
            f2_ev = {}
            f2_hl = {}
            f2_stms = {}
            for G, (goff, gn) in enumerate(GRP):
                @blk.gpsimd
                def _(g, G=G, goff=goff, gn=gn):
                    if G == 0:
                        g.wait_ge(pe, f1_pe)
                    else:
                        g.wait_ge(pe, f2_mm[(G - 1, NTC2 - 1, GRP[G - 1][1] - 1)])
                    for mt in range(gn):
                        ctr.inc(g.dma_start(
                            out=dnw[:, mt * 8192:(mt + 1) * 8192].rearrange(
                                "p (kt m) -> p kt m", m=128),
                            in_=wdn[goff + mt].transpose([1, 0, 2])), sw, 16)
                ldh = ctr.v(sw)

                @blk.scalar
                def _(s, ldh=ldh, gn=gn):
                    s.wait_ge(sw, ldh)
                    for q in range(2 * gn):
                        a = s.activation(dnw[:, q * 4096:(q + 1) * 4096],
                                         dnw[:, q * 4096:(q + 1) * 4096], AF.Sign)
                    ctr.inc(a, ac, 1)
                sgn2 = ctr.v(ac)

                for tcc in range(NTC2):
                    @blk.sync
                    def _(s, G=G, tcc=tcc, gn=gn):
                        if G == 0 and tcc == 0:
                            s.wait_ge(sxp[0], f1_sxp[0])
                            s.wait_ge(sxp[1], f1_sxp[1])
                        if tcc >= 2:
                            s.wait_ge(pe, f2_mm[(G, tcc - 2, gn - 1)])
                        elif G > 0 and tcc < 2:
                            s.wait_ge(pe, f2_mm[(G - 1, NTC2 - 2 + tcc, GRP[G - 1][1] - 1)])
                        src = sc_hid.rearrange("(pt p) t -> p pt t", p=128)[
                            :, :, tcc * TC2:(tcc + 1) * TC2]
                        ctr.inc(s.dma_start(
                            out=hb2[tcc % 2].rearrange("p (pt t) -> p pt t", t=TC2),
                            in_=src), hlp[tcc % 2], 16)
                    f2_hl[(G, tcc)] = ctr.v(hlp[tcc % 2])

                    ac_base_f2 = ctr.v(ac)
                    for mt in range(gn):
                        f2_ev[(G, tcc, mt)] = ac_base_f2 + mt + 1

                    for mt in range(gn):
                        @blk.tensor
                        def _(t, G=G, tcc=tcc, mt=mt, sgn2=sgn2, gn=gn):
                            if mt == 0:
                                if tcc == 0:
                                    t.wait_ge(ac, sgn2)
                                t.wait_ge(hlp[tcc % 2], f2_hl[(G, tcc)])
                            it = tcc * gn + mt
                            if it >= 2:
                                k = (tcc, mt - 2) if mt >= 2 else (tcc - 1, gn - 2 + mt)
                                t.wait_ge(ac, f2_ev[(G,) + k])
                            for kt in range(NPH):
                                mm = t.matmul(pF2[mt % 2][:, :],
                                              dnw[:, mt * 8192 + kt * 128:
                                                  mt * 8192 + (kt + 1) * 128],
                                              hb2[tcc % 2][:, kt * TC2:(kt + 1) * TC2],
                                              start=(kt == 0), stop=(kt == NPH - 1))
                            ctr.inc(mm, pe, 1)
                            f2_mm[(G, tcc, mt)] = ctr.v(pe)

                    @blk.scalar
                    def _(s, G=G, goff=goff, tcc=tcc, gn=gn):
                        if G == 0 and tcc == 0:
                            s.wait_ge(dv, dscd)
                        for mt in range(gn):
                            s.wait_ge(pe, f2_mm[(G, tcc, mt)])
                            prev = ((G, tcc, mt - 2) if mt >= 2 else
                                    (G, tcc - 1, gn - 2 + mt) if tcc >= 1 else
                                    (G - 1, NTC2 - 1, GRP[G - 1][1] - 2 + mt) if G >= 1
                                    else None)
                            if prev is not None and prev in f2_stms:
                                s.wait_ge(gsp[mt % 2], f2_stms[prev])
                            a = s.activation(gstg[mt % 2][:, :], pF2[mt % 2][:, :],
                                             AF.Silu,
                                             scale=p_dsco[:, goff + mt:goff + mt + 1])
                            ctr.inc(a, ac, 1)
                            assert ctr.v(ac) == f2_ev[(G, tcc, mt)]
                            s.wait_ge(ac, f2_ev[(G, tcc, mt)])
                            ctr.inc(s.dma_start(
                                out=sc_g[(goff + mt) * 128:(goff + mt + 1) * 128,
                                         tcc * TC2:(tcc + 1) * TC2],
                                in_=gstg[mt % 2][:, :]), gsp[mt % 2], 16)
                            f2_stms[(G, tcc, mt)] = ctr.v(gsp[mt % 2])
            f2_gsp = (ctr.v(gsp[0]), ctr.v(gsp[1]))
            f2_pe = ctr.v(pe)
        stage_barrier()
        with ExitStack() as sc_:
            SE = sc_.enter_context
            Xb = SE(nc.sbuf_tensor("Xb", [128, NPI * TC], BF16))
            Bbc = [SE(nc.sbuf_tensor(f"Bbc{i}", [128, GN], BF16)) for i in range(2)]
            Cbc = [SE(nc.sbuf_tensor(f"Cbc{i}", [128, GN], BF16)) for i in range(2)]
            S_ = SE(nc.sbuf_tensor("S_", [128, INTER], F32))
            stp = [SE(nc.sbuf_tensor(f"stp{i}", [128, 512], F32)) for i in range(2)]
            XT2 = SE(nc.sbuf_tensor("XT2", [128, INTER], BF16))
            BT2 = SE(nc.sbuf_tensor("BT2", [128, GN], BF16))
            sml = SE(nc.sbuf_tensor("sml", [64, 512], F32))
            smT = SE(nc.sbuf_tensor("smT", [128, 256], F32))
            wT2 = SE(nc.sbuf_tensor("wT2", [128, 64], BF16))
            alr = SE(nc.sbuf_tensor("alr", [128, 64], F32))
            Eb = SE(nc.sbuf_tensor("Eb", [128, NH * 128], BF16))
            Gb = SE(nc.sbuf_tensor("Gb", [128, NG * 128], BF16))
            Mp = SE(nc.sbuf_tensor("Mp", [128, NH * 128], BF16))
            XwT2 = SE(nc.sbuf_tensor("XwT2", [128, INTER], BF16))
            scout = SE(nc.sbuf_tensor("scout", [128, NPI * TC], BF16))
            drow = SE(nc.sbuf_tensor("drow", [1, 3 * 64], F32))
            crow = SE(nc.sbuf_tensor("crow", [1, 6 * 64], F32))
            crep = SE(nc.sbuf_tensor("crep", [128, 3 * 64], F32))
            rmsr = SE(nc.sbuf_tensor("rmsr", [1, TC * 2], F32))
            rmsb = SE(nc.sbuf_tensor("rmsb", [1, TC * 2], BF16))
            rstdb = SE(nc.sbuf_tensor("rstdb", [128, TC], BF16))
            wb3 = [SE(nc.sbuf_tensor(f"wb3{i}", [128, 16 * 128], BF16)) for i in range(2)]
            gabp = [SE(nc.sbuf_tensor(f"gabp{i}", [128, TC], BF16)) for i in range(2)]
            gfbp = [SE(nc.sbuf_tensor(f"gfbp{i}", [128, TC], BF16)) for i in range(2)]
            xfbp = [SE(nc.sbuf_tensor(f"xfbp{i}", [128, TC], F32)) for i in range(2)]
            sstg = [SE(nc.sbuf_tensor(f"sstg{i}", [128, TC], BF16)) for i in range(2)]
            ostg2 = [SE(nc.sbuf_tensor(f"ostg2{i}", [128, TC], F32)) for i in range(2)]
            S_bf = SE(nc.sbuf_tensor("S_bf", [128, INTER], BF16))
            pA = [SE(nc.psum_tensor(f"pA{i}", [128, 512], F32)) for i in range(2)]
            pB = [SE(nc.psum_tensor(f"pB{i}", [128, 512], F32)) for i in range(2)]
            pCx = SE(nc.psum_tensor("pCx", [128, 512], F32))
            pE2 = SE(nc.psum_tensor("pE2", [128, 512], F32))
            pT2 = [SE(nc.psum_tensor(f"pT2{i}", [128, 1024], BF16)) for i in range(2)]
            Ych = Eb[:, 0:INTER]

            @blk.sync
            def _(s):
                s.wait_ge(cc, ccd)
                for k in range(3):
                    ctr.inc(s.dma_start(
                        out=drow[0:1, k * 64:(k + 1) * 64],
                        in_=cc_out[k * 128:k * 128 + 64, T:T + 1].transpose([1, 0]),
                        allow_slow_non_contiguous=True), sdr, 16)
            drd = ctr.v(sdr)

            @blk.vector
            def _(v):
                v.wait_ge(sdr, drd)
                ch = k_chain
                t01 = crow[0:1, 192:256]
                t02 = crow[0:1, 256:320]
                t12 = crow[0:1, 320:384]
                v.scalar_tensor_tensor(t01, drow[0:1, 64:128], ch[0:1, 3:4],
                                       ch[0:1, 4:5].broadcast_to([1, 64]), ALU.mult, ALU.add)
                v.scalar_tensor_tensor(t02, drow[0:1, 128:192], ch[0:1, 5:6],
                                       ch[0:1, 6:7].broadcast_to([1, 64]), ALU.mult, ALU.add)
                bar(v, dv, v.scalar_tensor_tensor(t12, drow[0:1, 128:192], ch[0:1, 7:8],
                                                  ch[0:1, 8:9].broadcast_to([1, 64]),
                                                  ALU.mult, ALU.add))
                bar(v, dv, v.tensor_tensor(crow[0:1, 0:64], t01, t02, ALU.mult))
                v.tensor_single_scalar(crow[0:1, 0:64], crow[0:1, 0:64], ch[0:1, 0:1],
                                       ALU.mult)
                v.tensor_single_scalar(crow[0:1, 64:128], t12, ch[0:1, 1:2], ALU.mult)
                a = v.tensor_single_scalar(crow[0:1, 128:192], k_oner32[0:1, 0:64],
                                           ch[0:1, 2:3], ALU.mult)
                ctr.inc(a, dv, 1)
            cfd = ctr.v(dv)

            @blk.tensor
            def _(t):
                t.wait_ge(dv, cfd)
                for k in range(3):
                    mm = t.matmul(pE2[:, k * 64:(k + 1) * 64], k_oner32[:, :],
                                  crow[0:1, k * 64:(k + 1) * 64], start=True, stop=True)
                ctr.inc(mm, pe, 1)
            crm = ctr.v(pe)

            @blk.scalar
            def _(s):
                s.wait_ge(pe, crm)
                a = s.copy(crep[:, :], pE2[:, 0:192])
                ctr.inc(a, ac, 1)
            crd = ctr.v(ac)

            for pc in range(8):
                for k in range(3):
                    lp_guard = ctr.v(dv)

                    @blk.sync
                    def _(s, pc=pc, k=k, lp_guard=lp_guard):
                        if pc == 0 and k == 0:
                            s.wait_ge(ac, crd)
                        else:
                            s.wait_ge(dv, lp_guard)
                        ctr.inc(s.dma_start(
                            out=stp[1][:, :],
                            in_=cc_out[k * 128:(k + 1) * 128, pc * 512:(pc + 1) * 512]),
                            slp, 16)
                    lpd = ctr.v(slp)

                    @blk.vector
                    def _(v, pc=pc, k=k, lpd=lpd):
                        v.wait_ge(slp, lpd)
                        cap = crep[:, k * 64:(k + 1) * 64]
                        capg = cap[:, pc * 8:(pc + 1) * 8].unsqueeze(2).broadcast_to(
                            [128, 8, 64])
                        if k == 0:
                            a = v.tensor_tensor(
                                S_[:, pc * 512:(pc + 1) * 512].rearrange(
                                    "p (hh d) -> p hh d", d=64),
                                stp[1][:, :].rearrange("p (hh d) -> p hh d", d=64),
                                capg, ALU.mult)
                        else:
                            bar(v, dv, v.tensor_tensor(
                                stp[0][:, :].rearrange("p (hh d) -> p hh d", d=64),
                                stp[1][:, :].rearrange("p (hh d) -> p hh d", d=64),
                                capg, ALU.mult))
                            a = v.tensor_tensor(S_[:, pc * 512:(pc + 1) * 512],
                                                S_[:, pc * 512:(pc + 1) * 512],
                                                stp[0][:, :], ALU.add)
                        ctr.inc(a, dv, 1)
            @blk.vector
            def _(v, prev=ctr.v(dv)):
                v.wait_ge(dv, prev)
                a = v.tensor_copy(S_bf[:, :], S_[:, :])
                ctr.inc(a, dv, 1)
            sind = ctr.v(dv)

            sb_free = {}
            gev = {}
            eexp_ms = {}
            ostm_prev = None
            for scc in range(8):
                @blk.sync
                def _(s, scc=scc):
                    if scc == 0:
                        s.wait_ge(gsp[0], f2_gsp[0])
                        s.wait_ge(gsp[1], f2_gsp[1])
                        s.wait_ge(dv, sind)
                    if scc >= 1:
                        s.wait_ge(dv, sb_free[scc - 1])
                    rr = sc_xbc.rearrange("(pt p) t -> p pt t", p=128)
                    sl = slice(scc * TC, (scc + 1) * TC)
                    ctr.inc(s.dma_start(out=Xb.rearrange("p (pt t) -> p pt t", t=TC),
                                        in_=rr[:, 0:NPI, sl]), sxb, 16)
                sbl = ctr.v(sxb)

                for c in range(4):
                    cg = scc * 4 + c
                    t0 = c * 128

                    @blk.sync
                    def _(s, scc=scc, cg=cg, t0=t0):
                        if cg >= 2:
                            s.wait_ge(pe, bc_free[cg - 2])
                        rr = sc_xbc.rearrange("(pt p) t -> p pt t", p=128)
                        csl = slice(scc * TC + t0, scc * TC + t0 + 128)
                        ctr.inc(s.dma_start(
                            out=Bbc[cg % 2].rearrange("p (g t) -> p g t", t=128),
                            in_=rr[:, NPI:NPI + NG, csl]), bcp[cg % 2], 16)
                        ctr.inc(s.dma_start(
                            out=Cbc[cg % 2].rearrange("p (g t) -> p g t", t=128),
                            in_=rr[:, NPI + NG:NPX, csl]), bcp[cg % 2], 16)
                    bcl = ctr.v(bcp[cg % 2])
                    bc_free = bc_free if cg > 0 else {}

                    @blk.vector
                    def _(v, cg=cg, scc=scc):
                        if cg == 0:
                            v.wait_ge(dv, sind)
                        v.tensor_scalar(sml[:, 0:128],
                                        p_cdtp[:, cg * 128 + 1:cg * 128 + 129],
                                        p_cdtp[:, (cg + 1) * 128:(cg + 1) * 128 + 1],
                                        p_expA[:, :], ALU.subtract, ALU.mult)
                        v.tensor_scalar(sml[:, 128:256],
                                        p_cdtp[:, cg * 128 + 1:cg * 128 + 129],
                                        p_cdtp[:, cg * 128:cg * 128 + 1],
                                        p_expA[:, :], ALU.subtract, ALU.mult)
                        a = v.tensor_scalar(sml[:, 256:384],
                                            p_cdtp[:, cg * 128 + 1:cg * 128 + 129],
                                            p_expA[:, :], -1.0, ALU.mult, ALU.mult)
                        ctr.inc(a, dv, 1)
                    sm1 = ctr.v(dv)

                    @blk.scalar
                    def _(s, cg=cg, sm1=sm1):
                        s.wait_ge(dv, sm1)
                        s.activation(sml[:, 0:128], sml[:, 0:128], AF.Exp)
                        s.activation(sml[:, 128:256], sml[:, 128:256], AF.Exp, scale=-1.0)
                        a = s.activation(sml[:, 384:512], p_dt[:, cg * 128:(cg + 1) * 128],
                                         AF.Ln)
                        ctr.inc(a, ac, 1)
                    sm2 = ctr.v(ac)

                    @blk.vector
                    def _(v, cg=cg, sm2=sm2):
                        v.wait_ge(ac, sm2)
                        v.tensor_tensor(sml[:, 0:128], sml[:, 0:128],
                                        p_dt[:, cg * 128:(cg + 1) * 128], ALU.mult)
                        a = v.tensor_tensor(sml[:, 384:512], sml[:, 384:512],
                                            sml[:, 256:384], ALU.subtract)
                        ctr.inc(a, dv, 1)
                    sm3 = ctr.v(dv)

                    pt2_ev = pt2_ev if cg > 0 else {0: None, 1: None}
                    for wv in range(5):
                        @blk.tensor
                        def _(t, wv=wv, c=c, cg=cg, sbl=sbl, sm3=sm3, bcl=bcl):
                            if wv == 0:
                                t.wait_ge(sxb, sbl)
                                t.wait_ge(bcp[cg % 2], bcl)
                                if cg > 0:
                                    t.wait_ge(dv, xt_cons)
                            if pt2_ev[wv % 2] is not None:
                                t.wait_ge(dv, pt2_ev[wv % 2])
                            p = pT2[wv % 2]
                            for i in range(8):
                                n = wv * 8 + i
                                if n < 32:
                                    src = Xb[:, n * TC + t0: n * TC + t0 + 128]
                                else:
                                    src = Bbc[cg % 2][:, (n - 32) * 128:(n - 31) * 128]
                                mm = t.transpose(p[:, i * 128:(i + 1) * 128], src,
                                                 k_id_bf[:, :])
                            ctr.inc(mm, pe, 1)
                        twd = ctr.v(pe)

                        @blk.vector
                        def _(v, wv=wv, twd=twd):
                            v.wait_ge(pe, twd)
                            p = pT2[wv % 2]
                            for i in range(8):
                                n = wv * 8 + i
                                dst = (XT2[:, n * 128:(n + 1) * 128] if n < 32
                                       else BT2[:, (n - 32) * 128:(n - 31) * 128])
                                a = v.tensor_copy(dst, p[:, i * 128:(i + 1) * 128])
                            ctr.inc(a, dv, 1)
                        pt2_ev[wv % 2] = ctr.v(dv)
                        if wv == 4:
                            xbt_done = ctr.v(dv)

                    @blk.tensor
                    def _(t, cg=cg, sm3=sm3):
                        t.wait_ge(dv, sm3)
                        if cg > 0:
                            t.wait_ge(dv, gev[(cg - 1, 7)])
                        id64 = k_id_f32[0:64, 0:64]
                        t.transpose(pE2[:, 0:64], sml[:, 0:128], id64)
                        t.transpose(pE2[:, 64:128], sml[:, 128:256], id64)
                        t.transpose(pE2[:, 128:192], sml[:, 256:384], id64)
                        mm = t.transpose(pE2[:, 192:256], sml[:, 384:512], id64)
                        ctr.inc(mm, pe, 1)
                    smtd = ctr.v(pe)

                    acb = ctr.v(ac)
                    smev = acb + 1
                    for blkh in range(16):
                        eexp_ms[(cg, blkh)] = acb + 2 + blkh
                    gevac = acb + 18
                    ald = acb + 19

                    ebd_list = []

                    @blk.tensor
                    def _(t, cg=cg, smev=smev):
                        t.wait_ge(ac, smev)
                        if cg > 0:
                            t.wait_ge(dv, gev[(cg - 1, 7)])
                            t.wait_ge(dv, gev[(cg - 1, 6)])
                        for blkh in range(16):
                            if blkh >= 2:
                                t.wait_ge(ac, eexp_ms[(cg, blkh - 2)])
                            p = pA[blkh % 2]
                            for hh in range(4):
                                h = blkh * 4 + hh
                                t.matmul(p[:, hh * 128:(hh + 1) * 128],
                                         smT[:, 128 + h:129 + h].broadcast_to([128, 128]),
                                         k_id_f32[:, :], is_transpose=True,
                                         start=(hh == 0), stop=False)
                            mm = t.matmul(p[:, :], k_negbigI[:, :],
                                          k_triL[:, :].unsqueeze(1).broadcast_to(
                                              [128, 4, 128]),
                                          start=False, stop=True)
                            ctr.inc(mm, pe, 1)
                            ebd_list.append(ctr.v(pe))

                    @blk.tensor
                    def _(t, c=c, t0=t0, cg=cg):
                        if cg > 0:
                            t.wait_ge(dv, gev[(cg - 1, 7)])
                        for g_ in range(NG):
                            p = pB[g_ // 4]
                            mm = t.matmul(p[:, (g_ % 4) * 128:(g_ % 4) * 128 + 128],
                                          Bbc[cg % 2][:, g_ * 128:(g_ + 1) * 128],
                                          Cbc[cg % 2][:, g_ * 128:(g_ + 1) * 128],
                                          start=True, stop=True)
                        ctr.inc(mm, pe, 1)
                    gmd = ctr.v(pe)

                    @blk.vector
                    def _(v, smev=smev):
                        v.wait_ge(ac, smev)
                        a = v.tensor_copy(wT2[:, :], smT[:, 0:64])
                        ctr.inc(a, dv, 1)
                    wtc = ctr.v(dv)

                    @blk.vector
                    def _(v, gevac=gevac, cg=cg, wtc=wtc, xbt_done=xbt_done):
                        v.wait_ge(ac, gevac)
                        v.wait_ge(ac, eexp_ms[(cg, 15)])
                        v.wait_ge(dv, xbt_done)
                        v.wait_ge(dv, wtc)
                        v.tensor_tensor(
                            Mp[:, :].rearrange("p (g hh i) -> p g hh i", g=NG, hh=8),
                            Eb[:, :].rearrange("p (g hh i) -> p g hh i", g=NG, hh=8),
                            Gb[:, :].rearrange("p (g i) -> p g i", g=NG).unsqueeze(2)
                            .broadcast_to([128, NG, 8, 128]), ALU.mult)
                        a = v.tensor_tensor(
                            XwT2[:, :].rearrange("p (h d) -> p h d", d=64),
                            XT2[:, :].rearrange("p (h d) -> p h d", d=64),
                            wT2[:, :].unsqueeze(2).broadcast_to([128, 64, 64]), ALU.mult)
                        ctr.inc(a, dv, 1)
                    mpd = ctr.v(dv)

                    @blk.tensor
                    def _(t, cg=cg, mpd=mpd):
                        t.wait_ge(dv, mpd)
                        t.wait_ge(ac, dtmath_ac)
                        mm = t.transpose(pE2[:, 256:320],
                                         p_ach[:, cg:cg + 1].broadcast_to([64, 128]),
                                         k_id_f32[0:64, 0:64])
                        ctr.inc(mm, pe, 1)
                    alm = ctr.v(pe)

                    @blk.scalar
                    def _(s, cg=cg, smtd=smtd, gmd=gmd, alm=alm, ebd_list=ebd_list):
                        s.wait_ge(pe, smtd)
                        ctr.inc(s.copy(smT[:, :], pE2[:, 0:256]), ac, 1)
                        for blkh in range(16):
                            s.wait_ge(pe, ebd_list[blkh])
                            p = pA[blkh % 2]
                            for hh in range(4):
                                h = blkh * 4 + hh
                                a = s.activation(Eb[:, h * 128:(h + 1) * 128],
                                                 p[:, hh * 128:(hh + 1) * 128], AF.Exp,
                                                 bias=smT[:, 192 + h:193 + h])
                            ctr.inc(a, ac, 1)
                        s.wait_ge(pe, gmd)
                        s.copy(Gb[:, 0:512], pB[0][:, :])
                        ctr.inc(s.copy(Gb[:, 512:1024], pB[1][:, :]), ac, 1)
                        s.wait_ge(pe, alm)
                        ctr.inc(s.copy(alr[:, :], pE2[:, 256:320]), ac, 1)
                        assert ctr.v(ac) == cg * 0 + ald, (ctr.v(ac), ald)

                    for g_ in range(NG):
                        pCb = pCx if g_ % 2 == 0 else pE2

                        @blk.tensor
                        def _(t, g_=g_, cg=cg, t0=t0, mpd=mpd, ald=ald):
                            if g_ == 0:
                                t.wait_ge(dv, mpd)
                            if g_ == 1:
                                t.wait_ge(ac, ald)
                            if g_ >= 2:
                                t.wait_ge(dv, gev[(cg, g_ - 2)])
                            par = g_ % 2
                            for hh in range(8):
                                h = g_ * 8 + hh
                                t.matmul(pA[par][:, hh * 64:(hh + 1) * 64],
                                         Mp[:, h * 128:(h + 1) * 128],
                                         XT2[:, h * 64:(h + 1) * 64],
                                         start=True, stop=True)
                            t.matmul(pB[par][:, :],
                                     Cbc[cg % 2][:, g_ * 128:(g_ + 1) * 128],
                                     S_bf[:, g_ * 512:(g_ + 1) * 512], start=True, stop=True)
                            mm = t.matmul((pCx if g_ % 2 == 0 else pE2)[:, :],
                                          BT2[:, g_ * 128:(g_ + 1) * 128],
                                          XwT2[:, g_ * 512:(g_ + 1) * 512],
                                          start=True, stop=True)
                            ctr.inc(mm, pe, 1)
                        gmm = ctr.v(pe)

                        @blk.gpsimd
                        def _(g, g_=g_, gmm=gmm, ald=ald):
                            g.wait_ge(pe, gmm)
                            g.wait_ge(ac, ald)
                            a = g.tensor_tensor(
                                stp[g_ % 2][:, :].rearrange("p (hh d) -> p hh d", d=64),
                                S_[:, g_ * 512:(g_ + 1) * 512].rearrange(
                                    "p (hh d) -> p hh d", d=64),
                                alr[:, g_ * 8:(g_ + 1) * 8].unsqueeze(2).broadcast_to(
                                    [128, 8, 64]), ALU.mult)
                            ctr.inc(a, pl, 1)
                        spd = ctr.v(pl)

                        @blk.vector
                        def _(v, g_=g_, cg=cg, gmm=gmm, spd=spd):
                            v.wait_ge(pe, gmm)
                            par = g_ % 2
                            bar(v, dv, v.tensor_tensor(
                                Ych[:, g_ * 512:(g_ + 1) * 512].rearrange(
                                    "p (hh d) -> p hh d", d=64),
                                pB[par][:, :].rearrange("p (hh d) -> p hh d", d=64),
                                smT[:, 64 + g_ * 8:64 + (g_ + 1) * 8].unsqueeze(2)
                                .broadcast_to([128, 8, 64]), ALU.mult))
                            v.tensor_tensor(Ych[:, g_ * 512:(g_ + 1) * 512],
                                            pA[par][:, :], Ych[:, g_ * 512:(g_ + 1) * 512],
                                            ALU.add)
                            v.wait_ge(pl, spd)
                            bar(v, dv, v.tensor_tensor(S_[:, g_ * 512:(g_ + 1) * 512],
                                                       stp[g_ % 2][:, :],
                                                       (pCx if g_ % 2 == 0 else pE2)[:, :],
                                                       ALU.add))
                            a = v.tensor_copy(S_bf[:, g_ * 512:(g_ + 1) * 512],
                                              S_[:, g_ * 512:(g_ + 1) * 512])
                            ctr.inc(a, dv, 1)
                        gev[(cg, g_)] = ctr.v(dv)
                    xt_cons = gev[(cg, 7)]
                    bc_free[cg] = ctr.v(pe)

                    for wv in range(4):
                        @blk.tensor
                        def _(t, wv=wv, cg=cg):
                            if wv == 0:
                                t.wait_ge(dv, gev[(cg, 7)])
                            if pt2_ev[wv % 2] is not None:
                                t.wait_ge(dv, pt2_ev[wv % 2])
                            p = pT2[wv % 2]
                            for i in range(8):
                                pt = wv * 8 + i
                                mm = t.transpose(p[:, i * 128:(i + 1) * 128],
                                                 Ych[:, pt * 128:(pt + 1) * 128],
                                                 k_id_bf[:, :])
                            ctr.inc(mm, pe, 1)
                        ytd = ctr.v(pe)

                        @blk.vector
                        def _(v, wv=wv, c=c, ytd=ytd):
                            v.wait_ge(pe, ytd)
                            p = pT2[wv % 2]
                            for i in range(8):
                                pt = wv * 8 + i
                                a = v.tensor_copy(
                                    scout[:, pt * TC + c * 128: pt * TC + c * 128 + 128],
                                    p[:, i * 128:(i + 1) * 128])
                            ctr.inc(a, dv, 1)
                        pt2_ev[wv % 2] = ctr.v(dv)
                ychd = ctr.v(dv)

                gp_dv = []
                for pt in range(NPI):
                    @blk.sync
                    def _(s, pt=pt, scc=scc):
                        if pt >= 2:
                            s.wait_ge(dv, gp_dv[pt - 2])
                        ctr.inc(s.dma_start(
                            out=gabp[pt % 2][:, :],
                            in_=sc_gate[pt * 128:(pt + 1) * 128, scc * TC:(scc + 1) * TC]),
                            gbp[pt % 2], 16)
                    gld = ctr.v(gbp[pt % 2])

                    @blk.vector
                    def _(v, pt=pt, gld=gld, ychd=ychd, sbl=sbl):
                        if pt == 0:
                            v.wait_ge(dv, ychd)
                            v.wait_ge(sxb, sbl)
                        v.wait_ge(gbp[pt % 2], gld)
                        bar(v, dv, v.scalar_tensor_tensor(stp[pt % 2][:, :],
                                                          Xb[:, pt * TC:(pt + 1) * TC],
                                                          k_dexp[:, pt:pt + 1],
                                                          scout[:, pt * TC:(pt + 1) * TC],
                                                          ALU.mult, ALU.add))
                        a = v.tensor_tensor(scout[:, pt * TC:(pt + 1) * TC],
                                            stp[pt % 2][:, :], gabp[pt % 2][:, :],
                                            ALU.mult)
                        ctr.inc(a, dv, 1)
                    gp_dv.append(ctr.v(dv))
                gpd = ctr.v(dv)

                acb2 = ctr.v(ac)
                peb2 = ctr.v(pe)
                sq_ms = {pt: peb2 + pt + 1 for pt in range(NPI)}
                sqa_pl = {pt: acb2 + pt + 1 for pt in range(NPI)}

                @blk.scalar
                def _(s, gpd=gpd):
                    s.wait_ge(dv, gpd)
                    for pt in range(NPI):
                        if pt >= 4:
                            s.wait_ge(pe, sq_ms[pt - 4])
                        a = s.activation(XwT2[:, (pt % 4) * 512:(pt % 4 + 1) * 512],
                                         scout[:, pt * TC:(pt + 1) * TC], AF.Square)
                        ctr.inc(a, ac, 1)
                        assert ctr.v(ac) == sqa_pl[pt]

                @blk.tensor
                def _(t, scc=scc):
                    if scc > 0:
                        t.wait_ge(ac, rbe_prev)
                    for pt in range(NPI):
                        t.wait_ge(ac, sqa_pl[pt])
                        mm = t.matmul(pB[0][0:1, 0:TC], k_onec[:, :],
                                      XwT2[:, (pt % 4) * 512:(pt % 4 + 1) * 512],
                                      start=(pt == 0), stop=(pt == NPI - 1))
                        ctr.inc(mm, pe, 1)
                        assert ctr.v(pe) == sq_ms[pt]
                ssqd = ctr.v(pe)

                @blk.vector
                def _(v, ssqd=ssqd):
                    v.wait_ge(pe, ssqd)
                    a = v.tensor_scalar_mul(rmsr[0:1, 0:TC], pB[0][0:1, 0:TC], 1.0 / INTER)
                    ctr.inc(a, dv, 1)
                r1 = ctr.v(dv)

                @blk.scalar
                def _(s, r1=r1):
                    s.wait_ge(dv, r1)
                    bar(s, ac, s.activation(rmsr[0:1, TC:2 * TC], rmsr[0:1, 0:TC],
                                            AF.Ln, bias=EPS))
                    a = s.activation(rmsr[0:1, 0:TC], rmsr[0:1, TC:2 * TC], AF.Exp,
                                     scale=-0.5)
                    ctr.inc(a, ac, 1)
                r2 = ctr.v(ac)

                @blk.vector
                def _(v, r2=r2):
                    v.wait_ge(ac, r2)
                    bar(v, dv, v.tensor_copy(rmsb[0:1, 0:TC], rmsr[0:1, 0:TC]))
                    bar(v, dv, v.tensor_copy(rmsr[0:1, TC:2 * TC], rmsb[0:1, 0:TC]))
                    a = v.tensor_tensor(rmsb[0:1, TC:2 * TC], rmsr[0:1, 0:TC],
                                        rmsr[0:1, TC:2 * TC], ALU.subtract)
                    ctr.inc(a, dv, 1)
                r3 = ctr.v(dv)

                @blk.tensor
                def _(t, r3=r3):
                    t.wait_ge(dv, r3)
                    t.matmul(pB[1][:, 0:TC], k_oner[:, :], rmsb[0:1, 0:TC],
                             start=True, stop=False)
                    mm = t.matmul(pB[1][:, 0:TC], k_oner[:, :], rmsb[0:1, TC:2 * TC],
                                  start=False, stop=True)
                    ctr.inc(mm, pe, 1)
                r4 = ctr.v(pe)

                @blk.scalar
                def _(s, r4=r4):
                    s.wait_ge(pe, r4)
                    a = s.copy(rstdb[:, :], pB[1][:, 0:TC])
                    ctr.inc(a, ac, 1)
                rbe = ctr.v(ac)
                rbe_prev = rbe

                @blk.vector
                def _(v, rbe=rbe):
                    v.wait_ge(ac, rbe)
                    for pt in range(NPI):
                        a = v.scalar_tensor_tensor(Xb[:, pt * TC:(pt + 1) * TC],
                                                   rstdb[:, :], k_nw[:, pt:pt + 1],
                                                   scout[:, pt * TC:(pt + 1) * TC],
                                                   ALU.mult, ALU.mult)
                    ctr.inc(a, dv, 1)
                nrm = ctr.v(dv)

                wo_ld = {}
                wo_mm = {}
                wo_mmh = {}
                wo_ev = {}
                out_dv = {}

                @blk.gpsimd
                def _(g, scc=scc):
                    g.wait_ge(pe, f2_pe)
                    for hf in range(2):
                        if scc > 0:
                            g.wait_ge(pe, wo_mmh_prev[(15, hf)])
                        ctr.inc(g.dma_start(
                            out=wb3[hf][:, :].rearrange("p (kt m) -> p kt m", m=128),
                            in_=wout[0, hf * 16:(hf + 1) * 16].transpose([1, 0, 2])),
                            swp[hf], 16)
                        wo_ld[(0, hf)] = ctr.v(swp[hf])

                acb3 = ctr.v(ac)
                dvb3 = ctr.v(dv)
                fnb = [ctr.v(fnp[0]), ctr.v(fnp[1])]
                osb = [ctr.v(osp[0]), ctr.v(osp[1])]
                fld_pl = {}
                for mt in range(NPD):
                    wo_ev[mt] = acb3 + mt + 1
                    out_dv[mt] = dvb3 + 2 * (mt + 1)
                    fld_pl[mt] = fnb[mt % 2] + 32 * (mt // 2 + 1)
                ostm_new = {mt: osb[mt % 2] + 16 * (mt // 2 + 1) for mt in range(NPD)}

                for mt in range(NPD):
                    for hf in range(2):
                        @blk.tensor
                        def _(t, mt=mt, hf=hf, nrm=nrm, scc=scc):
                            if mt == 0 and hf == 0:
                                t.wait_ge(dv, nrm)
                            t.wait_ge(swp[hf], wo_ld[(mt, hf)])
                            if hf == 0 and mt >= 2:
                                t.wait_ge(ac, wo_ev[mt - 2])
                            for k2 in range(16):
                                kt = hf * 16 + k2
                                mm = t.matmul(pA[mt % 2][:, :],
                                              wb3[hf][:, k2 * 128:(k2 + 1) * 128],
                                              Xb[:, kt * TC:(kt + 1) * TC],
                                              start=(kt == 0), stop=(kt == NPI - 1))
                            ctr.inc(mm, pe, 1)
                            wo_mmh[(mt, hf)] = ctr.v(pe)
                        wo_mm[mt] = ctr.v(pe)

                        nxt = (mt + 1, hf)
                        if nxt[0] < NPD:
                            @blk.gpsimd
                            def _(g, mt=mt, hf=hf, nxt=nxt):
                                g.wait_ge(pe, wo_mmh[(mt, hf)])
                                ctr.inc(g.dma_start(
                                    out=wb3[nxt[1]][:, :].rearrange(
                                        "p (kt m) -> p kt m", m=128),
                                    in_=wout[nxt[0], nxt[1] * 16:(nxt[1] + 1) * 16].transpose([1, 0, 2])),
                                    swp[nxt[1]], 16)
                                wo_ld[nxt] = ctr.v(swp[nxt[1]])

                @blk.sync
                def _(s, scc=scc, nrm=nrm):
                    for mt in range(NPD):
                        if mt >= 2:
                            s.wait_ge(dv, out_dv[mt - 2])
                        elif scc > 0:
                            s.wait_ge(dv, nrm)
                        ctr.inc(s.dma_start(
                            out=gfbp[mt % 2][:, :],
                            in_=sc_g[mt * 128:(mt + 1) * 128, scc * TC:(scc + 1) * TC]),
                            fnp[mt % 2], 16)
                        ctr.inc(s.dma_start(
                            out=xfbp[mt % 2][:, :],
                            in_=xT.rearrange("(pt p) t -> pt p t", p=128)[
                                mt, :, scc * TC:(scc + 1) * TC]), fnp[mt % 2], 16)
                        assert ctr.v(fnp[mt % 2]) == fld_pl[mt]

                @blk.scalar
                def _(s, scc=scc):
                    for mt in range(NPD):
                        s.wait_ge(pe, wo_mm[mt])
                        if mt >= 2:
                            s.wait_ge(dv, out_dv[mt - 2])
                        a = s.copy(sstg[mt % 2][:, :], pA[mt % 2][:, :])
                        ctr.inc(a, ac, 1)
                        assert ctr.v(ac) == wo_ev[mt]
                        if mt >= 2:
                            ctr.inc(s.dma_start(
                                out=outT[(mt - 2) * 128:(mt - 1) * 128,
                                         scc * TC:(scc + 1) * TC],
                                in_=ostg2[mt % 2][:, :]), osp[mt % 2], 16)
                            assert ctr.v(osp[mt % 2]) == ostm_new[mt - 2]
                    for mt in (NPD - 2, NPD - 1):
                        s.wait_ge(dv, out_dv[mt])
                        ctr.inc(s.dma_start(
                            out=outT[mt * 128:(mt + 1) * 128, scc * TC:(scc + 1) * TC],
                            in_=ostg2[mt % 2][:, :]), osp[mt % 2], 16)
                        assert ctr.v(osp[mt % 2]) == ostm_new[mt]

                @blk.vector
                def _(v, scc=scc, ostm_prev=ostm_prev):
                    for mt in range(NPD):
                        v.wait_ge(ac, wo_ev[mt])
                        v.wait_ge(fnp[mt % 2], fld_pl[mt])
                        if mt >= 2:
                            v.wait_ge(osp[mt % 2], ostm_new[mt - 2])
                        elif scc > 0:
                            v.wait_ge(osp[mt % 2], ostm_prev[14 + mt])
                        bar(v, dv, v.tensor_tensor(sstg[mt % 2][:, :], sstg[mt % 2][:, :],
                                                   gfbp[mt % 2][:, :], ALU.mult))
                        a = v.tensor_tensor(ostg2[mt % 2][:, :], xfbp[mt % 2][:, :],
                                            sstg[mt % 2][:, :], ALU.add)
                        ctr.inc(a, dv, 1)
                        assert ctr.v(dv) == out_dv[mt]
                ostm_prev = ostm_new
                wo_mmh_prev = wo_mmh
                sb_free[scc] = out_dv[NPD - 1]

            fin0, fin1 = ctr.v(osp[0]), ctr.v(osp[1])

            @blk.scalar
            def _(s):
                s.wait_ge(osp[0], fin0)
                s.wait_ge(osp[1], fin1)

    return nc


def _prep_weights(i):
    w = {}
    Wdt = np.zeros((D, 128), np.float32)
    Wdt[:, 0:64] = i["L0_attn_Wdt"]
    blocks = [Wdt]
    for m in range(NPI):
        blocks.append(i["L0_attn_Wx"][:, m * 128:(m + 1) * 128])
    for m in range(NG):
        blocks.append(i["L0_attn_WB"][:, m * 128:(m + 1) * 128])
    for m in range(NG):
        blocks.append(i["L0_attn_WC"][:, m * 128:(m + 1) * 128])
    for m in range(NPI):
        blocks.append(i["L0_attn_Wz"][:, m * 128:(m + 1) * 128])
    w["wproj"] = np.ascontiguousarray(
        np.stack([b.reshape(NKD, 128, 128) for b in blocks]))
    w["wout"] = np.ascontiguousarray(
        i["L0_attn_Wout"].reshape(NPI, 128, NPD, 128).transpose(2, 0, 1, 3))
    w["wup"] = np.ascontiguousarray(
        i["L0_ffn_up_weight"].reshape(NPH, 128, NKD, 128).transpose(0, 2, 3, 1))
    w["wdn"] = np.ascontiguousarray(
        i["L0_ffn_down_weight"].reshape(NPD, 128, NPH, 128).transpose(0, 2, 3, 1))
    w["convw"] = np.ascontiguousarray(
        i["L0_attn_conv_w"].reshape(NPX, 128, KC).transpose(1, 0, 2).reshape(128, NPX * KC))
    w["convb"] = np.ascontiguousarray(i["L0_attn_conv_b"].reshape(NPX, 128).T)
    w["dtb"] = np.ascontiguousarray(i["L0_attn_dt_bias"].reshape(64, 1))
    w["alog"] = np.ascontiguousarray(i["L0_attn_A_log"].reshape(64, 1))
    w["dexp"] = np.ascontiguousarray(np.repeat(i["L0_attn_D"], HD).reshape(NPI, 128).T)
    w["nw"] = np.ascontiguousarray(i["L0_attn_norm_w"].reshape(NPI, 128).T)
    w["upsc"] = np.ascontiguousarray(i["L0_ffn_up_scale"].reshape(NPH, 128).T)
    w["dnsc"] = np.ascontiguousarray(i["L0_ffn_down_scale"].reshape(NPD, 128).T)
    w["oscr"] = np.ascontiguousarray(
        np.broadcast_to(i["L0_ffn_output_scale"].reshape(1, 1), (128, 1)))
    w["id_bf"] = np.eye(128, dtype=BF)
    w["id_f32"] = np.eye(128, dtype=np.float32)
    w["onec_bf"] = np.ones((128, 1), BF)
    w["oner_bf"] = np.ones((1, 128), BF)
    w["oner_f32"] = np.ones((1, 128), np.float32)
    w["negbigI"] = (NEGBIG * np.eye(128)).astype(BF)
    w["triL"] = np.tril(np.ones((128, 128)), -1).astype(BF)
    w["zcol"] = np.zeros((128, 1), np.float32)
    w["epscol"] = np.full((128, 1), EPS, np.float32)
    return w


def kernel(**inputs):
    inputs = {k: np.asarray(v, dtype=np.float32) for k, v in inputs.items()}
    x = inputs["L0_attn_hidden_states"]
    w = _prep_weights(inputs)

    nc = build_program()
    in_maps = []
    for c in range(8):
        b, s = c // 4, c % 4
        xTc = np.ascontiguousarray(x[b, s * T:(s + 1) * T, :].T)
        if s == 0:
            halo = np.zeros((D, 3), np.float32)
        else:
            halo = np.ascontiguousarray(x[b, s * T - 3:s * T, :].T)
        cm = [1.0 if k < s else 0.0 for k in range(3)]
        cf01 = 1.0 if s >= 2 else 0.0
        cf02 = 1.0 if s >= 3 else 0.0
        cf12 = 1.0 if s >= 3 else 0.0
        chain = np.array([[cm[0], cm[1], cm[2], cf01, 1 - cf01,
                           cf02, 1 - cf02, cf12, 1 - cf12]], np.float32)
        m = dict(w)
        m["xT"] = xTc
        m["xTh"] = halo
        m["chain"] = chain
        in_maps.append(m)

    res = run_bass_kernel_spmd(nc, in_maps, core_ids=list(range(8)))
    out = np.empty((2, 16384, D), np.float32)
    for c in range(8):
        b, s = c // 4, c % 4
        out[b, s * T:(s + 1) * T, :] = res.results[c]["outT"].T
    return out


# version 09
# speedup vs baseline: 1.0613x; 1.0074x over previous
import numpy as np
import ml_dtypes
import concourse.bass as bass
import concourse.mybir as mybir
from concourse.bass_utils import run_bass_kernel_spmd
from contextlib import ExitStack

F32 = mybir.dt.float32
BF16 = mybir.dt.bfloat16
AF = mybir.ActivationFunctionType
ALU = mybir.AluOpType
BF = ml_dtypes.bfloat16

D = 2048
INTER = 4096
NH, HD = 64, 64
NG, NS = 8, 128
GN = NG * NS
CONV = INTER + 2 * GN
KC = 4
DH = 8192
T = 4096
Q = 128
NCH = 32
TC = 512
NTC = 8
NKD = 16
NPI = 32
NPX = 48
NPD = 16
NPH = 64
EPS = 1e-5
NEGBIG = -1e30
NBLK = 1 + NPI + NG + NG + NPI


class Ctr:
    def __init__(self):
        self.c = {}

    def inc(self, instr, sem, n=1):
        instr.then_inc(sem, n)
        self.c[sem.name] = self.c.get(sem.name, 0) + n
        return self.c[sem.name]

    def v(self, sem):
        return self.c.get(sem.name, 0)


def build_program(debug_outs=()):
    nc = bass.Bass("TRN2", target_bir_lowering=False)

    def din(name, shape, dt=F32):
        return nc.dram_tensor(name, shape, dt, kind="ExternalInput").ap()

    def dscr(name, shape, dt):
        kind = "ExternalOutput" if name in debug_outs else "Internal"
        return nc.dram_tensor(name, shape, dt, kind=kind).ap()

    xT = din("xT", [D, T])
    xTh = din("xTh", [D, 3])
    wproj = din("wproj", [NBLK, NKD, 128, 128])
    wout = din("wout", [NPD, NPI, 128, 128])
    wup = din("wup", [NPH, NKD, 128, 128])
    wdn = din("wdn", [NPD, NPH, 128, 128])
    convw = din("convw", [128, NPX * KC])
    convb = din("convb", [128, NPX])
    dtb = din("dtb", [64, 1])
    alog = din("alog", [64, 1])
    dexp = din("dexp", [128, NPI])
    nw_i = din("nw", [128, NPI])
    upsc = din("upsc", [128, NPH])
    dnsc = din("dnsc", [128, NPD])
    oscr = din("oscr", [128, 1])
    id_bf = din("id_bf", [128, 128], BF16)
    id_f32 = din("id_f32", [128, 128])
    onec_bf = din("onec_bf", [128, 1], BF16)
    oner_bf = din("oner_bf", [1, 128], BF16)
    oner_f32 = din("oner_f32", [1, 128])
    negbigI = din("negbigI", [128, 128], BF16)
    triL = din("triL", [128, 128], BF16)
    zcol = din("zcol", [128, 1])
    epscol = din("epscol", [128, 1])
    chain = din("chain", [1, 9])

    outT = nc.dram_tensor("outT", [D, T], F32, kind="ExternalOutput").ap()
    sc_xnT = dscr("sc_xnT", [D, T], BF16)
    sc_gate = dscr("sc_gate", [INTER, T], BF16)
    sc_xbc = dscr("sc_xbc", [CONV, T], BF16)
    sc_hid = dscr("sc_hid", [DH, T], BF16)
    sc_g = dscr("sc_g", [D, T], BF16)
    cc_in = nc.dram_tensor("cc_in", [128, T + 1], F32).ap()
    cc_out = nc.dram_tensor("cc_out", [4 * 128, T + 1], F32).ap()
    d_flat = nc.dram_tensor("d_flat", [64, 32], F32).ap()

    ctr = Ctr()

    with ExitStack() as ctx:
        E = ctx.enter_context

        sx = E(nc.semaphore("sx"))
        sw = E(nc.semaphore("sw"))
        st = E(nc.semaphore("st"))
        pe = E(nc.semaphore("pe"))
        dv = E(nc.semaphore("dv"))
        ac = E(nc.semaphore("ac"))
        pl = E(nc.semaphore("pl"))
        cc = E(nc.semaphore("cc"))
        swp = [E(nc.semaphore(f"swp{i}")) for i in range(2)]
        sxp = [E(nc.semaphore(f"sxp{i}")) for i in range(2)]
        hlp = [E(nc.semaphore(f"hlp{i}")) for i in range(2)]
        gsp = [E(nc.semaphore(f"gsp{i}")) for i in range(2)]
        osp = [E(nc.semaphore(f"osp{i}")) for i in range(2)]
        gbp = [E(nc.semaphore(f"gbp{i}")) for i in range(2)]
        fnp = [E(nc.semaphore(f"fnp{i}")) for i in range(2)]
        sxb = E(nc.semaphore("sxb"))
        bcp = [E(nc.semaphore(f"bcp{i}")) for i in range(2)]
        slp = E(nc.semaphore("slp"))
        sdr = E(nc.semaphore("sdr"))

        k_id_bf = E(nc.sbuf_tensor("k_id_bf", [128, 128], BF16))
        k_id_f32 = E(nc.sbuf_tensor("k_id_f32", [128, 128], F32))
        k_onec = E(nc.sbuf_tensor("k_onec", [128, 1], BF16))
        k_oner = E(nc.sbuf_tensor("k_oner", [1, 128], BF16))
        k_oner32 = E(nc.sbuf_tensor("k_oner32", [1, 128], F32))
        k_negbigI = E(nc.sbuf_tensor("k_negbigI", [128, 128], BF16))
        k_triL = E(nc.sbuf_tensor("k_triL", [128, 128], BF16))
        k_zcol = E(nc.sbuf_tensor("k_zcol", [128, 1], F32))
        k_eps = E(nc.sbuf_tensor("k_eps", [128, 1], F32))
        k_convw = E(nc.sbuf_tensor("k_convw", [128, NPX * KC], F32))
        k_convb = E(nc.sbuf_tensor("k_convb", [128, NPX], F32))
        k_dtb = E(nc.sbuf_tensor("k_dtb", [64, 1], F32))
        k_alog = E(nc.sbuf_tensor("k_alog", [64, 1], F32))
        k_dexp = E(nc.sbuf_tensor("k_dexp", [128, NPI], F32))
        k_nw = E(nc.sbuf_tensor("k_nw", [128, NPI], F32))
        k_upsc = E(nc.sbuf_tensor("k_upsc", [128, NPH], F32))
        k_dnsc = E(nc.sbuf_tensor("k_dnsc", [128, NPD], F32))
        k_oscr = E(nc.sbuf_tensor("k_oscr", [128, 1], F32))
        k_chain = E(nc.sbuf_tensor("k_chain", [1, 9], F32))
        p_dtc = E(nc.sbuf_tensor("p_dtc", [128, T + 8], F32))
        p_dt = p_dtc[0:64, 0:T]
        p_cdtp = p_dtc[64:128, 0:T + 8]
        p_ach = E(nc.sbuf_tensor("p_ach", [64, 32], F32))
        p_expA = E(nc.sbuf_tensor("p_expA", [64, 1], F32))
        p_delt = E(nc.sbuf_tensor("p_delt", [64, 1], F32))
        p_dsco = E(nc.sbuf_tensor("p_dsco", [128, NPD], F32))
        xnTh = E(nc.sbuf_tensor("xnTh", [128, NKD * 3], BF16))

        nc.const_aps.register(F32, EPS, k_eps[:, :])
        blk = E(nc.Block())

        def bar(e, sem, instr):
            e.wait_ge(sem, ctr.inc(instr, sem, 1))

        def stage_barrier():
            allsems = [sx, sw, st, pe, dv, ac, pl] + swp + sxp + hlp + gsp + osp \
                + gbp + fnp + [sxb, slp, sdr] + bcp
            snap = [(s_, ctr.v(s_)) for s_ in allsems if ctr.v(s_) > 0]
            for eng in (blk.sync, blk.gpsimd, blk.tensor, blk.vector, blk.scalar):
                @eng
                def _(e, snap=snap):
                    for s_, v_ in snap:
                        e.wait_ge(s_, v_)

        @blk.sync
        def _(s):
            for dst, src in [
                (k_id_bf[:, :], id_bf), (k_id_f32[:, :], id_f32),
                (k_onec[:, :], onec_bf), (k_oner[:, :], oner_bf),
                (k_oner32[:, :], oner_f32),
                (k_negbigI[:, :], negbigI), (k_triL[:, :], triL),
                (k_zcol[:, :], zcol),
                (k_eps[:, :], epscol),
                (k_convw[:, :], convw),
                (k_convb[:, :], convb),
                (k_dtb[:, :], dtb), (k_alog[:, :], alog),
                (k_dexp[:, :], dexp),
                (k_nw[:, :], nw_i),
                (k_upsc[:, :], upsc),
                (k_dnsc[:, :], dnsc),
                (k_oscr[:, :], oscr), (k_chain[:, :], chain),
            ]:
                ctr.inc(s.dma_start(out=dst, in_=src), sx, 16)
        c0 = ctr.v(sx)

        arA_ctx = ExitStack()
        arA = arA_ctx.enter_context(nc.sbuf_tensor("arA", [128, NKD * T], BF16))
        TCP0 = 256
        NP0 = T // TCP0
        with ExitStack() as sc_:
            SE = sc_.enter_context
            xbf = SE(nc.sbuf_tensor("xbf", [128, NKD * TCP0], BF16))
            xsq = SE(nc.sbuf_tensor("xsq", [128, NKD * TCP0], BF16))
            rows = SE(nc.sbuf_tensor("rows", [1, TCP0 * 6], F32))
            rowb = SE(nc.sbuf_tensor("rowb", [1, TCP0 * 4], BF16))
            rh32 = SE(nc.sbuf_tensor("rh32", [1, TCP0 * 2], F32))
            mrb = SE(nc.sbuf_tensor("mrb", [128, TCP0 * 2], BF16))
            pS = [SE(nc.psum_tensor(f"pS{i}", [128, TCP0], F32)) for i in range(4)]

            prev_done = {"dv": None, "pe_stats": None, "ac_bce": None}

            def p0_chunk(tcols, src_ap, dst_fn, first):
                @blk.gpsimd
                def _(g):
                    if first:
                        g.wait_ge(sx, c0)
                    if prev_done["dv"] is not None:
                        g.wait_ge(dv, prev_done["dv"])
                    ctr.inc(g.dma_start(
                        out=xbf[:, 0:NKD * tcols].rearrange("p (kt t) -> p kt t", t=tcols),
                        in_=src_ap), sw, 16)
                ld = ctr.v(sw)

                @blk.scalar
                def _(s):
                    s.wait_ge(sw, ld)
                    for kt in range(NKD):
                        a = s.activation(xsq[:, kt * tcols:(kt + 1) * tcols],
                                         xbf[:, kt * tcols:(kt + 1) * tcols], AF.Square)
                    ctr.inc(a, ac, 1)
                sqd = ctr.v(ac)

                @blk.tensor
                def _(t):
                    t.wait_ge(ac, sqd)
                    if prev_done["dv"] is not None:
                        t.wait_ge(dv, prev_done["dv"])
                    for kt in range(NKD):
                        t.matmul(pS[0][0:1, 0:tcols], k_onec[:, :],
                                 xbf[:, kt * tcols:(kt + 1) * tcols],
                                 start=(kt == 0), stop=(kt == NKD - 1))
                    for kt in range(NKD):
                        m = t.matmul(pS[1][0:1, 0:tcols], k_onec[:, :],
                                     xsq[:, kt * tcols:(kt + 1) * tcols],
                                     start=(kt == 0), stop=(kt == NKD - 1))
                    ctr.inc(m, pe, 1)
                std = ctr.v(pe)

                @blk.vector
                def _(v):
                    v.wait_ge(pe, std)
                    TP = TCP0
                    m_ = rows[0:1, 0:tcols]
                    e2 = rows[0:1, TP:TP + tcols]
                    var = rows[0:1, 2 * TP:2 * TP + tcols]
                    v.tensor_scalar_mul(m_, pS[0][0:1, 0:tcols], 1.0 / D)
                    bar(v, dv, v.tensor_scalar_mul(e2, pS[1][0:1, 0:tcols], 1.0 / D))
                    bar(v, dv, v.tensor_tensor(var, m_, m_, ALU.mult))
                    a = v.tensor_tensor(var, e2, var, ALU.subtract)
                    ctr.inc(a, dv, 1)
                vd = ctr.v(dv)

                @blk.scalar
                def _(s):
                    s.wait_ge(dv, vd)
                    TP = TCP0
                    bar(s, ac, s.activation(rows[0:1, 3 * TP:3 * TP + tcols],
                                            rows[0:1, 2 * TP:2 * TP + tcols], AF.Ln,
                                            bias=EPS))
                    a = s.activation(rows[0:1, 4 * TP:4 * TP + tcols],
                                     rows[0:1, 3 * TP:3 * TP + tcols], AF.Exp, scale=-0.5)
                    ctr.inc(a, ac, 1)
                sq2 = ctr.v(ac)

                @blk.vector
                def _(v):
                    v.wait_ge(ac, sq2)
                    TP = TCP0
                    rstd = rows[0:1, 4 * TP:4 * TP + tcols]
                    mr = rows[0:1, 5 * TP:5 * TP + tcols]
                    mh, ml = rowb[0:1, 0:tcols], rowb[0:1, TP:TP + tcols]
                    rh, rl = rowb[0:1, 2 * TP:2 * TP + tcols], rowb[0:1, 3 * TP:3 * TP + tcols]
                    v.tensor_tensor(mr, rows[0:1, 0:tcols], rstd, ALU.mult)
                    bar(v, dv, v.tensor_copy(rh, rstd))
                    v.tensor_copy(mh, mr)
                    bar(v, dv, v.tensor_copy(rh32[0:1, TP:TP + tcols], rh))
                    v.tensor_copy(rh32[0:1, 0:tcols], mh)
                    bar(v, dv, v.tensor_tensor(rl, rstd, rh32[0:1, TP:TP + tcols], ALU.subtract))
                    a = v.tensor_tensor(ml, mr, rh32[0:1, 0:tcols], ALU.subtract)
                    ctr.inc(a, dv, 1)
                rd = ctr.v(dv)

                @blk.tensor
                def _(t):
                    t.wait_ge(dv, rd)
                    TP = TCP0
                    if prev_done["ac_bce"] is not None:
                        t.wait_ge(ac, prev_done["ac_bce"])
                    t.matmul(pS[2][:, 0:tcols], k_oner[:, :], rowb[0:1, 0:tcols],
                             start=True, stop=False)
                    t.matmul(pS[2][:, 0:tcols], k_oner[:, :], rowb[0:1, TP:TP + tcols],
                             start=False, stop=True)
                    t.matmul(pS[3][:, 0:tcols], k_oner[:, :], rowb[0:1, 2 * TP:2 * TP + tcols],
                             start=True, stop=False)
                    m = t.matmul(pS[3][:, 0:tcols], k_oner[:, :], rowb[0:1, 3 * TP:3 * TP + tcols],
                                 start=False, stop=True)
                    ctr.inc(m, pe, 1)
                bcd = ctr.v(pe)

                @blk.scalar
                def _(s):
                    s.wait_ge(pe, bcd)
                    s.copy(mrb[:, 0:tcols], pS[2][:, 0:tcols])
                    a = s.copy(mrb[:, TCP0:TCP0 + tcols], pS[3][:, 0:tcols])
                    ctr.inc(a, ac, 1)
                prev_done["ac_bce"] = ctr.v(ac)
                bce = ctr.v(ac)

                @blk.vector
                def _(v):
                    v.wait_ge(ac, bce)
                    for kt in range(NKD):
                        xs = xbf[:, kt * tcols:(kt + 1) * tcols]
                        a = v.tensor_tensor(xsq[:, kt * tcols:(kt + 1) * tcols], xs,
                                            mrb[:, TCP0:TCP0 + tcols], ALU.mult)
                    bar(v, dv, a)
                    for kt in range(NKD):
                        a = v.tensor_tensor(dst_fn(kt),
                                            xsq[:, kt * tcols:(kt + 1) * tcols],
                                            mrb[:, 0:tcols], ALU.subtract)
                    ctr.inc(a, dv, 1)
                prev_done["dv"] = ctr.v(dv)
                return ctr.v(dv)

            src_h = xTh.rearrange("(kt p) t -> p kt t", p=128)
            p0_chunk(3, src_h, lambda kt: xnTh[:, kt * 3:(kt + 1) * 3], True)

            for tcc in range(NP0):
                src = xT.rearrange("(kt p) t -> p kt t", p=128)[:, :, tcc * TCP0:(tcc + 1) * TCP0]
                done = p0_chunk(
                    TCP0, src,
                    lambda kt, tcc=tcc: arA[:, kt * T + tcc * TCP0: kt * T + (tcc + 1) * TCP0],
                    False)

                @blk.scalar
                def _(s, tcc=tcc, done=done):
                    s.wait_ge(dv, done)
                    dst = sc_xnT.rearrange("(kt p) t -> p kt t", p=128)[
                        :, :, tcc * TCP0:(tcc + 1) * TCP0]
                    src2 = arA.rearrange("p (kt t) -> p kt t", t=T)[
                        :, :, tcc * TCP0:(tcc + 1) * TCP0]
                    ctr.inc(s.dma_start(out=dst, in_=src2), st, 16)
            p0_st = ctr.v(st)
            p0_dv = ctr.v(dv)

        with ExitStack() as sc_:
            SE = sc_.enter_context
            wbuf = [SE(nc.sbuf_tensor(f"wbuf{i}", [128, NKD * 128], BF16)) for i in range(2)]
            xbcr = SE(nc.sbuf_tensor("xbcr", [128, T + 3], F32))
            accb = SE(nc.sbuf_tensor("accb", [128, T], F32))
            ostg = [SE(nc.sbuf_tensor(f"ostg{i}", [128, T], BF16)) for i in range(2)]
            pM = [SE(nc.psum_tensor(f"pM{i}", [128, TC], F32)) for i in range(2)]
            pHh = SE(nc.psum_tensor("pHh", [128, 8], F32))
            sp1 = accb[0:64, :]
            sp2 = xbcr[0:64, 0:T]

            def btype(b):
                if b == 0:
                    return "dt"
                if b <= NPI:
                    return "x"
                if b <= NPI + NG:
                    return "B"
                if b <= NPI + 2 * NG:
                    return "C"
                return "gate"

            wld = {}
            mm_ms = {}
            evac_ms = {}
            accinit_ms = {}
            conv_ms = {}
            convh0_ms = {}
            silu_ms = {}
            spill_ms = {}
            dtmath_dv = [None]

            @blk.gpsimd
            def _(g):
                g.wait_ge(dv, p0_dv)
                for b in range(2):
                    ctr.inc(g.dma_start(
                        out=wbuf[b][:, :].rearrange("p (kt m) -> p kt m", m=128),
                        in_=wproj[b].transpose([1, 0, 2])), swp[b % 2], 16)
                    wld[b] = ctr.v(swp[b % 2])

            for b in range(NBLK):
                bt = btype(b)

                if bt in ("x", "B", "C"):
                    @blk.tensor
                    def _(t, b=b):
                        t.wait_ge(swp[b % 2], wld[b])
                        if (b - 1, 0) in evac_ms:
                            t.wait_ge(ac, evac_ms[(b - 1, 0)])
                        for kt in range(NKD):
                            t.matmul(pHh[:, 0:3], wbuf[b % 2][:, kt * 128:(kt + 1) * 128],
                                     xnTh[:, kt * 3:(kt + 1) * 3],
                                     start=(kt == 0), stop=(kt == NKD - 1))

                ac_base = ctr.v(ac)
                dv_base = ctr.v(dv)
                for tcc in range(NTC):
                    if bt == "dt":
                        evac_ms[(b, tcc)] = ac_base + 3 * (tcc + 1)
                    else:
                        evac_ms[(b, tcc)] = ac_base + tcc + 1
                for tcc in range(NTC):
                    @blk.tensor
                    def _(t, b=b, tcc=tcc):
                        if tcc == 0:
                            t.wait_ge(swp[b % 2], wld[b])
                        it = b * NTC + tcc
                        if it >= 2:
                            kk = ((b, tcc - 2) if tcc >= 2
                                  else (b - 1, NTC - 2 + tcc))
                            t.wait_ge(ac, evac_ms[kk])
                            if kk in accinit_ms:
                                t.wait_ge(dv, accinit_ms[kk])
                        for kt in range(NKD):
                            mm = t.matmul(pM[tcc % 2][:, :],
                                          wbuf[b % 2][:, kt * 128:(kt + 1) * 128],
                                          arA[:, kt * T + tcc * TC: kt * T + (tcc + 1) * TC],
                                          start=(kt == 0), stop=(kt == NKD - 1))
                        ctr.inc(mm, pe, 1)
                        mm_ms[(b, tcc)] = ctr.v(pe)
                    if bt == "dt":
                        @blk.scalar
                        def _(s, b=b, bt=bt, tcc=tcc):
                            if tcc == 0:
                                if bt in ("x", "B", "C"):
                                    if b in (1, 2) and dtmath_dv[0] is not None:
                                        s.wait_ge(dv, dtmath_dv[0])
                                    if (b - 1) in silu_ms:
                                        s.wait_ge(ac, silu_ms[b - 1])
                                    if (b - 2) in spill_ms:
                                        s.wait_ge(sxp[b % 2], spill_ms[b - 2])
                                if bt == "gate" and (b - 2) in spill_ms:
                                    s.wait_ge(sxp[b % 2], spill_ms[b - 2])
                            s.wait_ge(pe, mm_ms[(b, tcc)])
                            p = pM[tcc % 2]
                            c0_, c1_ = tcc * TC, (tcc + 1) * TC
                            if bt == "gate":
                                a = s.activation(ostg[b % 2][:, c0_:c1_], p[:, :], AF.Silu)
                            elif bt == "dt":
                                bar(s, ac, s.activation(sp1[:, c0_:c1_], p[0:64, :],
                                                        AF.Abs, bias=k_dtb[:, :]))
                                bar(s, ac, s.activation(sp1[:, c0_:c1_], sp1[:, c0_:c1_],
                                                        AF.Exp, scale=-1.0))
                                s.activation(sp1[:, c0_:c1_], sp1[:, c0_:c1_], AF.Ln, bias=1.0)
                                a = s.activation(sp2[:, c0_:c1_], p[0:64, :], AF.Relu,
                                                 bias=k_dtb[:, :])
                            else:
                                if tcc == 0:
                                    s.copy(xbcr[:, 0:3], pHh[:, 0:3])
                                a = s.copy(xbcr[:, 3 + c0_:3 + c1_], p[:, :])
                            ctr.inc(a, ac, 1)
                            evac_ms[(b, tcc)] = ctr.v(ac)

                if bt != "dt":
                    @blk.scalar
                    def _(s, b=b, bt=bt):
                        if bt in ("x", "B", "C"):
                            if b in (1, 2) and dtmath_dv[0] is not None:
                                s.wait_ge(dv, dtmath_dv[0])
                        if bt == "gate" and (b - 2) in spill_ms:
                            s.wait_ge(sxp[b % 2], spill_ms[b - 2])
                        for tcc in range(NTC):
                            if bt in ("x", "B", "C"):
                                if tcc == 0 and (b - 1) in convh0_ms:
                                    s.wait_ge(convh0_ms[b - 1][0],
                                              convh0_ms[b - 1][1])
                                if tcc == NTC // 2 and (b - 1) in conv_ms:
                                    s.wait_ge(conv_ms[b - 1][0], conv_ms[b - 1][1])
                            s.wait_ge(pe, mm_ms[(b, tcc)])
                            p = pM[tcc % 2]
                            c0_, c1_ = tcc * TC, (tcc + 1) * TC
                            if bt == "gate":
                                a = s.activation(ostg[b % 2][:, c0_:c1_], p[:, :], AF.Silu)
                            else:
                                if tcc == 0:
                                    s.copy(xbcr[:, 0:3], pHh[:, 0:3])
                                a = s.copy(xbcr[:, 3 + c0_:3 + c1_], p[:, :])
                            ctr.inc(a, ac, 1)
                            assert ctr.v(ac) == evac_ms[(b, tcc)], (b, tcc, ctr.v(ac))

                if b + 2 < NBLK:
                    @blk.gpsimd
                    def _(g, b=b):
                        g.wait_ge(pe, mm_ms[(b, NTC - 1)])
                        ctr.inc(g.dma_start(
                            out=wbuf[b % 2][:, :].rearrange("p (kt m) -> p kt m", m=128),
                            in_=wproj[b + 2].transpose([1, 0, 2])), swp[b % 2], 16)
                        wld[b + 2] = ctr.v(swp[b % 2])

                if bt in ("x", "B", "C"):
                    use_pool = False
                    eng = blk.gpsimd if use_pool else blk.vector
                    sem = pl if use_pool else dv

                    TH = T // 2
                    for hf_ in range(2):
                        @eng
                        def _(e, b=b, hf_=hf_):
                            e.wait_ge(ac, evac_ms[(b, (NTC // 2) * (hf_ + 1) - 1)])
                            if hf_ == 0 and (b - 1) in silu_ms:
                                e.wait_ge(ac, silu_ms[b - 1])
                            o0 = hf_ * TH
                            bar(e, sem, e.tensor_scalar(
                                accb[:, o0:o0 + TH], xbcr[:, 3 + o0:3 + o0 + TH],
                                k_convw[:, (b - 1) * KC + 3:(b - 1) * KC + 4],
                                k_convb[:, b - 1:b], ALU.mult, ALU.add))
                            for k in range(3):
                                a = e.scalar_tensor_tensor(
                                    accb[:, o0:o0 + TH], xbcr[:, k + o0:k + o0 + TH],
                                    k_convw[:, (b - 1) * KC + k:(b - 1) * KC + k + 1],
                                    accb[:, o0:o0 + TH], ALU.mult, ALU.add)
                                if k < 2:
                                    bar(e, sem, a)
                            ctr.inc(a, sem, 1)
                        if hf_ == 0:
                            convh0_ms[b] = (sem, ctr.v(sem))
                    conv_ms[b] = (sem, ctr.v(sem))

                    @blk.scalar
                    def _(s, b=b):
                        s.wait_ge(conv_ms[b][0], conv_ms[b][1])
                        if (b - 2) in spill_ms:
                            s.wait_ge(sxp[b % 2], spill_ms[b - 2])
                        a = s.activation(ostg[b % 2][:, :], accb[:, :], AF.Silu)
                        ctr.inc(a, ac, 1)
                    silu_ms[b] = ctr.v(ac)

                    @blk.sync
                    def _(s, b=b):
                        s.wait_ge(ac, silu_ms[b])
                        ctr.inc(s.dma_start(out=sc_xbc[(b - 1) * 128:b * 128, :],
                                            in_=ostg[b % 2][:, :]), sxp[b % 2], 16)
                    spill_ms[b] = ctr.v(sxp[b % 2])
                elif bt == "gate":
                    gi = b - (1 + NPI + 2 * NG)

                    @blk.sync
                    def _(s, b=b, gi=gi):
                        s.wait_ge(ac, evac_ms[(b, NTC - 1)])
                        ctr.inc(s.dma_start(out=sc_gate[gi * 128:(gi + 1) * 128, :],
                                            in_=ostg[b % 2][:, :]), sxp[b % 2], 16)
                    spill_ms[b] = ctr.v(sxp[b % 2])
                else:
                    @blk.vector
                    def _(v):
                        v.wait_ge(ac, evac_ms[(0, NTC - 1)])
                        bar(v, dv, v.tensor_tensor(p_dt[:, :], sp1[:, 0:T], sp2[:, 0:T],
                                                   ALU.add))
                        v.tensor_copy(p_cdtp[:, 0:1], k_zcol[0:64, :])
                        a = v.tensor_tensor_scan(
                            p_cdtp[:, 1:T + 1], p_dt[:, :],
                            k_zcol[0:64, :].broadcast_to([64, T]), 0.0, ALU.add, ALU.add)
                        ctr.inc(a, dv, 1)
                    c1 = ctr.v(dv)

                    @blk.scalar
                    def _(s, c1=c1):
                        s.wait_ge(dv, c1)
                        a = s.activation(p_expA[:, :], k_alog[:, :], AF.Exp)
                        ctr.inc(a, ac, 1)
                    c2 = ctr.v(ac)

                    @blk.vector
                    def _(v, c2=c2):
                        v.wait_ge(ac, c2)
                        ce = p_cdtp[:, 128:T + 1:128]
                        cs = p_cdtp[:, 0:T:128]
                        bar(v, dv, v.tensor_tensor(p_ach[:, :], ce, cs, ALU.subtract))
                        v.tensor_scalar(p_ach[:, :], p_ach[:, :], p_expA[:, :], -1.0,
                                        ALU.mult, ALU.mult)
                        a = v.tensor_scalar(p_delt[:, :], p_cdtp[:, T:T + 1],
                                            p_expA[:, :], -1.0, ALU.mult, ALU.mult)
                        ctr.inc(a, dv, 1)
                    c3 = ctr.v(dv)

                    @blk.scalar
                    def _(s, c3=c3):
                        s.wait_ge(dv, c3)
                        s.activation(p_ach[:, :], p_ach[:, :], AF.Exp)
                        a = s.activation(p_delt[:, :], p_delt[:, :], AF.Exp)
                        ctr.inc(a, ac, 1)
                    c4 = ctr.v(ac)
                    dtmath_dv[0] = c3
                    dtmath_ac = c4

                    dtmath_dv[0] = c1

            p1_sxp = (ctr.v(sxp[0]), ctr.v(sxp[1]))
            p1_pe = ctr.v(pe)
        arA_ctx.close()

        stage_barrier()
        with ExitStack() as sc_:
            SE = sc_.enter_context
            Xa = SE(nc.sbuf_tensor("Xa", [128, NPI * TC], BF16))
            Ba = SE(nc.sbuf_tensor("Ba", [128, NG * TC], BF16))
            XTa = SE(nc.sbuf_tensor("XTa", [128, 4 * INTER], BF16))
            BTa = SE(nc.sbuf_tensor("BTa", [128, 4 * GN], BF16))
            XwTa = SE(nc.sbuf_tensor("XwTa", [128, 4 * INTER], BF16))
            wcha = SE(nc.sbuf_tensor("wcha", [64, 4 * 128], F32))
            wTba = SE(nc.sbuf_tensor("wTba", [128, 4 * 64], BF16))
            lam = SE(nc.sbuf_tensor("lam", [128, INTER], F32))
            pTa = [SE(nc.psum_tensor(f"pTa{i}", [128, 1024], BF16)) for i in range(2)]
            pWa = SE(nc.psum_tensor("pWa", [128, 512], F32))
            pLa = [SE(nc.psum_tensor(f"pLa{i}", [128, TC], F32)) for i in range(4)]

            sa_tr_cons = None
            sa_lam = None
            for scc in range(8):
                @blk.sync
                def _(s, scc=scc):
                    if scc == 0:
                        s.wait_ge(sxp[0], p1_sxp[0])
                        s.wait_ge(sxp[1], p1_sxp[1])
                    else:
                        s.wait_ge(pe, sa_trd)
                    rr = sc_xbc.rearrange("(pt p) t -> p pt t", p=128)
                    sl = slice(scc * TC, (scc + 1) * TC)
                    ctr.inc(s.dma_start(out=Xa.rearrange("p (pt t) -> p pt t", t=TC),
                                        in_=rr[:, 0:NPI, sl]), sx, 16)
                    ctr.inc(s.dma_start(out=Ba.rearrange("p (pt t) -> p pt t", t=TC),
                                        in_=rr[:, NPI:NPI + NG, sl]), sx, 16)
                lda = ctr.v(sx)

                @blk.vector
                def _(v, scc=scc):
                    if scc == 0:
                        v.wait_ge(ac, dtmath_ac)
                    for c in range(4):
                        cg = scc * 4 + c
                        a = v.tensor_scalar(wcha[:, c * 128:(c + 1) * 128],
                                            p_cdtp[:, cg * 128 + 1:cg * 128 + 129],
                                            p_cdtp[:, T:T + 1], p_expA[:, :],
                                            ALU.subtract, ALU.mult)
                    ctr.inc(a, dv, 1)
                w1 = ctr.v(dv)

                @blk.scalar
                def _(s, w1=w1):
                    s.wait_ge(dv, w1)
                    a = s.activation(wcha[:, 0:512], wcha[:, 0:512], AF.Exp)
                    ctr.inc(a, ac, 1)
                w2 = ctr.v(ac)

                @blk.vector
                def _(v, scc=scc, w2=w2):
                    v.wait_ge(ac, w2)
                    for c in range(4):
                        cg = scc * 4 + c
                        a = v.tensor_tensor(wcha[:, c * 128:(c + 1) * 128],
                                            wcha[:, c * 128:(c + 1) * 128],
                                            p_dt[:, cg * 128:(cg + 1) * 128], ALU.mult)
                    ctr.inc(a, dv, 1)
                w3 = ctr.v(dv)

                tiles = []
                for c in range(4):
                    for pt in range(NPI):
                        tiles.append(("X", c, pt))
                    for g_ in range(NG):
                        tiles.append(("B", c, g_))
                waves = [tiles[i:i + 8] for i in range(0, len(tiles), 8)]
                pTa_ev = pTa_ev if scc > 0 else {0: None, 1: None}
                last_wave_ev = None
                for wv, wtiles in enumerate(waves):
                    @blk.tensor
                    def _(t, wv=wv, wtiles=wtiles, scc=scc, lda=lda, w3=w3):
                        if wv == 0:
                            t.wait_ge(sx, lda)
                            t.wait_ge(dv, w3)
                            if sa_tr_cons is not None:
                                t.wait_ge(dv, sa_tr_cons)
                        if pTa_ev[wv % 2] is not None:
                            t.wait_ge(dv, pTa_ev[wv % 2])
                        p = pTa[wv % 2]
                        for i, (kind, c, j) in enumerate(wtiles):
                            src = (Xa[:, j * TC + c * 128: j * TC + c * 128 + 128]
                                   if kind == "X" else
                                   Ba[:, j * TC + c * 128: j * TC + c * 128 + 128])
                            mm = t.transpose(p[:, i * 128:(i + 1) * 128], src, k_id_bf[:, :])
                        ctr.inc(mm, pe, 1)
                    twd = ctr.v(pe)

                    @blk.vector
                    def _(v, wv=wv, wtiles=wtiles, twd=twd):
                        v.wait_ge(pe, twd)
                        p = pTa[wv % 2]
                        for i, (kind, c, j) in enumerate(wtiles):
                            dst = (XTa[:, c * INTER + j * 128: c * INTER + (j + 1) * 128]
                                   if kind == "X" else
                                   BTa[:, c * GN + j * 128: c * GN + (j + 1) * 128])
                            a = v.tensor_copy(dst, p[:, i * 128:(i + 1) * 128])
                        ctr.inc(a, dv, 1)
                    pTa_ev[wv % 2] = ctr.v(dv)
                    last_wave_ev = ctr.v(dv)
                sa_trd = ctr.v(pe)

                @blk.tensor
                def _(t, w3=w3):
                    t.wait_ge(dv, w3)
                    for c in range(4):
                        mm = t.transpose(pWa[:, c * 64:(c + 1) * 64],
                                         wcha[:, c * 128:(c + 1) * 128],
                                         k_id_f32[0:64, 0:64])
                    ctr.inc(mm, pe, 1)
                wtd = ctr.v(pe)

                @blk.vector
                def _(v, wtd=wtd):
                    v.wait_ge(pe, wtd)
                    a = v.tensor_copy(wTba[:, :], pWa[:, 0:256])
                    ctr.inc(a, dv, 1)
                wte = ctr.v(dv)

                @blk.vector
                def _(v, wte=wte, lw=last_wave_ev):
                    v.wait_ge(dv, lw)
                    v.wait_ge(dv, wte)
                    for c in range(4):
                        a = v.tensor_tensor(
                            XwTa[:, c * INTER:(c + 1) * INTER].rearrange(
                                "p (h d) -> p h d", d=64),
                            XTa[:, c * INTER:(c + 1) * INTER].rearrange(
                                "p (h d) -> p h d", d=64),
                            wTba[:, c * 64:(c + 1) * 64].unsqueeze(2).broadcast_to(
                                [128, 64, 64]),
                            ALU.mult)
                    ctr.inc(a, dv, 1)
                xwd = ctr.v(dv)

                for half in range(2):
                    @blk.tensor
                    def _(t, half=half, xwd=xwd, scc=scc):
                        t.wait_ge(dv, xwd)
                        if sa_lam is not None:
                            t.wait_ge(dv, sa_lam)
                        for gi in range(4):
                            g_ = half * 4 + gi
                            for c in range(4):
                                mm = t.matmul(
                                    pLa[gi][:, :],
                                    BTa[:, c * GN + g_ * 128: c * GN + (g_ + 1) * 128],
                                    XwTa[:, c * INTER + g_ * 512: c * INTER + g_ * 512 + 512],
                                    start=(c == 0), stop=(c == 3))
                        ctr.inc(mm, pe, 1)
                    cbd = ctr.v(pe)

                    @blk.vector
                    def _(v, half=half, cbd=cbd, scc=scc, prev=sa_lam):
                        v.wait_ge(pe, cbd)
                        if prev is not None:
                            v.wait_ge(dv, prev)
                        for gi in range(4):
                            g_ = half * 4 + gi
                            if scc == 0:
                                a = v.tensor_copy(lam[:, g_ * 512:(g_ + 1) * 512],
                                                  pLa[gi][:, :])
                            else:
                                a = v.tensor_tensor(lam[:, g_ * 512:(g_ + 1) * 512],
                                                    lam[:, g_ * 512:(g_ + 1) * 512],
                                                    pLa[gi][:, :], ALU.add)
                        ctr.inc(a, dv, 1)
                    sa_lam = ctr.v(dv)
                sa_tr_cons = xwd
                sa_tr_cons = sa_lam

            @blk.scalar
            def _(s):
                s.wait_ge(dv, sa_lam)
                s.wait_ge(ac, dtmath_ac)
                ctr.inc(s.dma_start(out=cc_in[0:128, 0:T], in_=lam[:, :]), st, 16)
                ctr.inc(s.dma_start(out=cc_in[:, T:T + 1], in_=k_zcol[:, :],
                                    allow_slow_non_contiguous=True), st, 16)
                s.wait_ge(st, ctr.v(st))
                ctr.inc(s.dma_start(out=cc_in[0:64, T:T + 1], in_=p_delt[:, :],
                                    allow_slow_non_contiguous=True), st, 16)
            payd = ctr.v(st)

            @blk.gpsimd
            def _(g):
                g.wait_ge(st, payd)
                a = g.collective_compute(
                    "AllGather", ALU.bypass,
                    replica_groups=[[0, 1, 2, 3], [4, 5, 6, 7]],
                    ins=[cc_in.opt()], outs=[cc_out.opt()])
                ctr.inc(a, cc, 1)
            ccd = ctr.v(cc)

        stage_barrier()
        with ExitStack() as sc_:
            SE = sc_.enter_context
            arA = SE(nc.sbuf_tensor("arB", [128, NKD * T], BF16))
            wb2 = [SE(nc.sbuf_tensor(f"wb2{i}", [128, NKD * 128], BF16)) for i in range(2)]
            ws2 = [SE(nc.sbuf_tensor(f"ws2{i}", [128, NKD * 128], BF16)) for i in range(2)]
            hst = [SE(nc.sbuf_tensor(f"hst{i}", [128, T], BF16)) for i in range(2)]
            pF = [SE(nc.psum_tensor(f"pF{i}", [128, TC], F32)) for i in range(2)]

            @blk.sync
            def _(s):
                s.wait_ge(pe, p1_pe)
                s.wait_ge(st, p0_st)
                for kt in range(NKD):
                    ctr.inc(s.dma_start(out=arA[:, kt * T:(kt + 1) * T],
                                        in_=sc_xnT[kt * 128:(kt + 1) * 128, :]), sx, 16)
            f1xn = ctr.v(sx)

            f1_wld = {}
            f1_sgn = {}
            f1_mm = {}
            f1_ev = {}
            f1_sp = {}

            @blk.gpsimd
            def _(g):
                for hb in range(2):
                    ctr.inc(g.dma_start(
                        out=wb2[hb][:, :].rearrange("p (kt m) -> p kt m", m=128),
                        in_=wup[hb].transpose([1, 0, 2])), swp[hb % 2], 16)
                    f1_wld[hb] = ctr.v(swp[hb % 2])

            for hb in range(NPH):
                @blk.scalar
                def _(s, hb=hb):
                    s.wait_ge(swp[hb % 2], f1_wld[hb])
                    if hb >= 2:
                        s.wait_ge(pe, f1_mm[(hb - 2, NTC - 1)])
                    a = s.activation(ws2[hb % 2][:, :], wb2[hb % 2][:, :], AF.Sign)
                    ctr.inc(a, ac, 1)
                f1_sgn[hb] = ctr.v(ac)

                if hb + 2 < NPH:
                    @blk.gpsimd
                    def _(g, hb=hb):
                        g.wait_ge(ac, f1_sgn[hb])
                        ctr.inc(g.dma_start(
                            out=wb2[hb % 2][:, :].rearrange("p (kt m) -> p kt m", m=128),
                            in_=wup[hb + 2].transpose([1, 0, 2])), swp[hb % 2], 16)
                        f1_wld[hb + 2] = ctr.v(swp[hb % 2])

                ac_base_f1 = ctr.v(ac)
                for tcc in range(NTC):
                    f1_ev[hb * NTC + tcc] = ac_base_f1 + tcc + 1

                for tcc in range(NTC):
                    @blk.tensor
                    def _(t, hb=hb, tcc=tcc):
                        if tcc == 0:
                            if hb == 0:
                                t.wait_ge(sx, f1xn)
                            t.wait_ge(ac, f1_sgn[hb])
                        it = hb * NTC + tcc
                        if it >= 2:
                            t.wait_ge(ac, f1_ev[it - 2])
                        for kt in range(NKD):
                            mm = t.matmul(pF[tcc % 2][:, :],
                                          ws2[hb % 2][:, kt * 128:(kt + 1) * 128],
                                          arA[:, kt * T + tcc * TC: kt * T + (tcc + 1) * TC],
                                          start=(kt == 0), stop=(kt == NKD - 1))
                        ctr.inc(mm, pe, 1)
                        f1_mm[(hb, tcc)] = ctr.v(pe)

                @blk.scalar
                def _(s, hb=hb):
                    if hb >= 2:
                        s.wait_ge(sxp[hb % 2], f1_sp[hb - 2])
                    for tcc in range(NTC):
                        s.wait_ge(pe, f1_mm[(hb, tcc)])
                        a = s.activation(hst[hb % 2][:, tcc * TC:(tcc + 1) * TC],
                                         pF[tcc % 2][:, :], AF.Relu,
                                         scale=k_upsc[:, hb:hb + 1])
                        ctr.inc(a, ac, 1)
                        assert ctr.v(ac) == f1_ev[hb * NTC + tcc]

                @blk.sync
                def _(s, hb=hb):
                    s.wait_ge(ac, f1_ev[hb * NTC + NTC - 1])
                    ctr.inc(s.dma_start(out=sc_hid[hb * 128:(hb + 1) * 128, :],
                                        in_=hst[hb % 2][:, :]), sxp[hb % 2], 16)
                f1_sp[hb] = ctr.v(sxp[hb % 2])
            f1_pe = ctr.v(pe)
            f1_sxp = (ctr.v(sxp[0]), ctr.v(sxp[1]))

        stage_barrier()
        with ExitStack() as sc_:
            SE = sc_.enter_context
            TC2 = 256
            NTC2 = 16
            GRP = [(0, 6), (6, 6), (12, 4)]
            dnw = SE(nc.sbuf_tensor("dnw", [128, 6 * NPH * 128], BF16))
            hb2 = [SE(nc.sbuf_tensor(f"hb2{i}", [128, NPH * TC2], BF16)) for i in range(2)]
            gstg = [SE(nc.sbuf_tensor(f"gstg{i}", [128, TC2], BF16)) for i in range(2)]
            pF2 = [SE(nc.psum_tensor(f"pF2{i}", [128, TC2], F32)) for i in range(2)]

            @blk.vector
            def _(v):
                v.wait_ge(sx, c0)
                a = v.tensor_scalar(p_dsco[:, :], k_dnsc[:, :], k_oscr[:, :], None, ALU.mult)
                ctr.inc(a, dv, 1)
            dscd = ctr.v(dv)

            f2_mm = {}
            f2_ev = {}
            f2_hl = {}
            f2_stms = {}
            sgn_ms = {}
            for G, (goff, gn) in enumerate(GRP):
                ldh_mt = {}
                for mt in range(gn):
                    @blk.gpsimd
                    def _(g, G=G, goff=goff, mt=mt):
                        if G == 0:
                            if mt == 0:
                                g.wait_ge(pe, f1_pe)
                        else:
                            gp = GRP[G - 1][1]
                            g.wait_ge(pe, f2_mm[(G - 1, NTC2 - 1, min(mt, gp - 1))])
                        if mt > 0:
                            g.wait_ge(sw, ldh_mt[mt - 1])
                        ctr.inc(g.dma_start(
                            out=dnw[:, mt * 8192:(mt + 1) * 8192].rearrange(
                                "p (kt m) -> p kt m", m=128),
                            in_=wdn[goff + mt].transpose([1, 0, 2])), sw, 16)
                        ldh_mt[mt] = ctr.v(sw)

                @blk.scalar
                def _(s, G=G, gn=gn):
                    for mt in range(gn):
                        s.wait_ge(sw, ldh_mt[mt])
                        for q in range(2):
                            a = s.activation(
                                dnw[:, mt * 8192 + q * 4096:
                                    mt * 8192 + (q + 1) * 4096],
                                dnw[:, mt * 8192 + q * 4096:
                                    mt * 8192 + (q + 1) * 4096], AF.Sign)
                        ctr.inc(a, ac, 1)
                        sgn_ms[(G, mt)] = ctr.v(ac)

                for tcc in range(NTC2):
                    @blk.sync
                    def _(s, G=G, tcc=tcc, gn=gn):
                        if G == 0 and tcc == 0:
                            s.wait_ge(sxp[0], f1_sxp[0])
                            s.wait_ge(sxp[1], f1_sxp[1])
                        if tcc >= 2:
                            s.wait_ge(pe, f2_mm[(G, tcc - 2, gn - 1)])
                        elif G > 0 and tcc < 2:
                            s.wait_ge(pe, f2_mm[(G - 1, NTC2 - 2 + tcc, GRP[G - 1][1] - 1)])
                        src = sc_hid.rearrange("(pt p) t -> p pt t", p=128)[
                            :, :, tcc * TC2:(tcc + 1) * TC2]
                        ctr.inc(s.dma_start(
                            out=hb2[tcc % 2].rearrange("p (pt t) -> p pt t", t=TC2),
                            in_=src), hlp[tcc % 2], 16)
                    f2_hl[(G, tcc)] = ctr.v(hlp[tcc % 2])

                    ac_base_f2 = ctr.v(ac)
                    for mt in range(gn):
                        f2_ev[(G, tcc, mt)] = ac_base_f2 + mt + 1

                    for mt in range(gn):
                        @blk.tensor
                        def _(t, G=G, tcc=tcc, mt=mt, gn=gn):
                            if tcc == 0:
                                t.wait_ge(ac, sgn_ms[(G, mt)])
                            if mt == 0:
                                t.wait_ge(hlp[tcc % 2], f2_hl[(G, tcc)])
                            it = tcc * gn + mt
                            if it >= 2:
                                k = (tcc, mt - 2) if mt >= 2 else (tcc - 1, gn - 2 + mt)
                                t.wait_ge(ac, f2_ev[(G,) + k])
                            for kt in range(NPH):
                                mm = t.matmul(pF2[mt % 2][:, :],
                                              dnw[:, mt * 8192 + kt * 128:
                                                  mt * 8192 + (kt + 1) * 128],
                                              hb2[tcc % 2][:, kt * TC2:(kt + 1) * TC2],
                                              start=(kt == 0), stop=(kt == NPH - 1))
                            ctr.inc(mm, pe, 1)
                            f2_mm[(G, tcc, mt)] = ctr.v(pe)

                    @blk.scalar
                    def _(s, G=G, goff=goff, tcc=tcc, gn=gn):
                        if G == 0 and tcc == 0:
                            s.wait_ge(dv, dscd)
                        for mt in range(gn):
                            s.wait_ge(pe, f2_mm[(G, tcc, mt)])
                            prev = ((G, tcc, mt - 2) if mt >= 2 else
                                    (G, tcc - 1, gn - 2 + mt) if tcc >= 1 else
                                    (G - 1, NTC2 - 1, GRP[G - 1][1] - 2 + mt) if G >= 1
                                    else None)
                            if prev is not None and prev in f2_stms:
                                s.wait_ge(gsp[mt % 2], f2_stms[prev])
                            a = s.activation(gstg[mt % 2][:, :], pF2[mt % 2][:, :],
                                             AF.Silu,
                                             scale=p_dsco[:, goff + mt:goff + mt + 1])
                            ctr.inc(a, ac, 1)
                            assert ctr.v(ac) == f2_ev[(G, tcc, mt)]
                            s.wait_ge(ac, f2_ev[(G, tcc, mt)])
                            ctr.inc(s.dma_start(
                                out=sc_g[(goff + mt) * 128:(goff + mt + 1) * 128,
                                         tcc * TC2:(tcc + 1) * TC2],
                                in_=gstg[mt % 2][:, :]), gsp[mt % 2], 16)
                            f2_stms[(G, tcc, mt)] = ctr.v(gsp[mt % 2])
            f2_gsp = (ctr.v(gsp[0]), ctr.v(gsp[1]))
            f2_pe = ctr.v(pe)
        stage_barrier()
        with ExitStack() as sc_:
            SE = sc_.enter_context
            Xb = SE(nc.sbuf_tensor("Xb", [128, NPI * TC], BF16))
            Bbc = [SE(nc.sbuf_tensor(f"Bbc{i}", [128, GN], BF16)) for i in range(2)]
            Cbc = [SE(nc.sbuf_tensor(f"Cbc{i}", [128, GN], BF16)) for i in range(2)]
            S_ = SE(nc.sbuf_tensor("S_", [128, INTER], F32))
            stp = [SE(nc.sbuf_tensor(f"stp{i}", [128, 512], F32)) for i in range(2)]
            XT2 = SE(nc.sbuf_tensor("XT2", [128, INTER], BF16))
            BT2 = SE(nc.sbuf_tensor("BT2", [128, GN], BF16))
            sml = SE(nc.sbuf_tensor("sml", [64, 512], F32))
            smT = SE(nc.sbuf_tensor("smT", [128, 256], F32))
            wT2 = SE(nc.sbuf_tensor("wT2", [128, 64], BF16))
            alr = SE(nc.sbuf_tensor("alr", [128, 64], F32))
            Eb = SE(nc.sbuf_tensor("Eb", [128, NH * 128], BF16))
            Gb = SE(nc.sbuf_tensor("Gb", [128, NG * 128], BF16))
            Mp = SE(nc.sbuf_tensor("Mp", [128, NH * 128], BF16))
            XwT2 = SE(nc.sbuf_tensor("XwT2", [128, INTER], BF16))
            scout = SE(nc.sbuf_tensor("scout", [128, NPI * TC], BF16))
            drow = SE(nc.sbuf_tensor("drow", [1, 3 * 64], F32))
            crow = SE(nc.sbuf_tensor("crow", [1, 6 * 64], F32))
            crep = SE(nc.sbuf_tensor("crep", [128, 3 * 64], F32))
            rmsr = SE(nc.sbuf_tensor("rmsr", [1, TC * 2], F32))
            rmsb = SE(nc.sbuf_tensor("rmsb", [1, TC * 2], BF16))
            rstdb = SE(nc.sbuf_tensor("rstdb", [128, TC], BF16))
            wb3 = [SE(nc.sbuf_tensor(f"wb3{i}", [128, 16 * 128], BF16)) for i in range(2)]
            gabp = [SE(nc.sbuf_tensor(f"gabp{i}", [128, TC], BF16)) for i in range(2)]
            gfbp = [SE(nc.sbuf_tensor(f"gfbp{i}", [128, TC], BF16)) for i in range(2)]
            xfbp = [SE(nc.sbuf_tensor(f"xfbp{i}", [128, TC], F32)) for i in range(2)]
            sstg = [SE(nc.sbuf_tensor(f"sstg{i}", [128, TC], BF16)) for i in range(2)]
            ostg2 = [SE(nc.sbuf_tensor(f"ostg2{i}", [128, TC], F32)) for i in range(2)]
            S_bf = SE(nc.sbuf_tensor("S_bf", [128, INTER], BF16))
            pA = [SE(nc.psum_tensor(f"pA{i}", [128, 512], F32)) for i in range(2)]
            pB = [SE(nc.psum_tensor(f"pB{i}", [128, 512], F32)) for i in range(2)]
            pCx = SE(nc.psum_tensor("pCx", [128, 512], F32))
            pE2 = SE(nc.psum_tensor("pE2", [128, 512], F32))
            pT2 = [SE(nc.psum_tensor(f"pT2{i}", [128, 1024], BF16)) for i in range(2)]
            Ych = Eb[:, 0:INTER]

            @blk.sync
            def _(s):
                s.wait_ge(cc, ccd)
                for k in range(3):
                    ctr.inc(s.dma_start(
                        out=drow[0:1, k * 64:(k + 1) * 64],
                        in_=cc_out[k * 128:k * 128 + 64, T:T + 1].transpose([1, 0]),
                        allow_slow_non_contiguous=True), sdr, 16)
            drd = ctr.v(sdr)

            @blk.vector
            def _(v):
                v.wait_ge(sdr, drd)
                ch = k_chain
                t01 = crow[0:1, 192:256]
                t02 = crow[0:1, 256:320]
                t12 = crow[0:1, 320:384]
                v.scalar_tensor_tensor(t01, drow[0:1, 64:128], ch[0:1, 3:4],
                                       ch[0:1, 4:5].broadcast_to([1, 64]), ALU.mult, ALU.add)
                v.scalar_tensor_tensor(t02, drow[0:1, 128:192], ch[0:1, 5:6],
                                       ch[0:1, 6:7].broadcast_to([1, 64]), ALU.mult, ALU.add)
                bar(v, dv, v.scalar_tensor_tensor(t12, drow[0:1, 128:192], ch[0:1, 7:8],
                                                  ch[0:1, 8:9].broadcast_to([1, 64]),
                                                  ALU.mult, ALU.add))
                bar(v, dv, v.tensor_tensor(crow[0:1, 0:64], t01, t02, ALU.mult))
                v.tensor_single_scalar(crow[0:1, 0:64], crow[0:1, 0:64], ch[0:1, 0:1],
                                       ALU.mult)
                v.tensor_single_scalar(crow[0:1, 64:128], t12, ch[0:1, 1:2], ALU.mult)
                a = v.tensor_single_scalar(crow[0:1, 128:192], k_oner32[0:1, 0:64],
                                           ch[0:1, 2:3], ALU.mult)
                ctr.inc(a, dv, 1)
            cfd = ctr.v(dv)

            @blk.tensor
            def _(t):
                t.wait_ge(dv, cfd)
                for k in range(3):
                    mm = t.matmul(pE2[:, k * 64:(k + 1) * 64], k_oner32[:, :],
                                  crow[0:1, k * 64:(k + 1) * 64], start=True, stop=True)
                ctr.inc(mm, pe, 1)
            crm = ctr.v(pe)

            @blk.scalar
            def _(s):
                s.wait_ge(pe, crm)
                a = s.copy(crep[:, :], pE2[:, 0:192])
                ctr.inc(a, ac, 1)
            crd = ctr.v(ac)

            for pc in range(8):
                for k in range(3):
                    lp_guard = ctr.v(dv)

                    @blk.sync
                    def _(s, pc=pc, k=k, lp_guard=lp_guard):
                        if pc == 0 and k == 0:
                            s.wait_ge(ac, crd)
                        else:
                            s.wait_ge(dv, lp_guard)
                        ctr.inc(s.dma_start(
                            out=stp[1][:, :],
                            in_=cc_out[k * 128:(k + 1) * 128, pc * 512:(pc + 1) * 512]),
                            slp, 16)
                    lpd = ctr.v(slp)

                    @blk.vector
                    def _(v, pc=pc, k=k, lpd=lpd):
                        v.wait_ge(slp, lpd)
                        cap = crep[:, k * 64:(k + 1) * 64]
                        capg = cap[:, pc * 8:(pc + 1) * 8].unsqueeze(2).broadcast_to(
                            [128, 8, 64])
                        if k == 0:
                            a = v.tensor_tensor(
                                S_[:, pc * 512:(pc + 1) * 512].rearrange(
                                    "p (hh d) -> p hh d", d=64),
                                stp[1][:, :].rearrange("p (hh d) -> p hh d", d=64),
                                capg, ALU.mult)
                        else:
                            bar(v, dv, v.tensor_tensor(
                                stp[0][:, :].rearrange("p (hh d) -> p hh d", d=64),
                                stp[1][:, :].rearrange("p (hh d) -> p hh d", d=64),
                                capg, ALU.mult))
                            a = v.tensor_tensor(S_[:, pc * 512:(pc + 1) * 512],
                                                S_[:, pc * 512:(pc + 1) * 512],
                                                stp[0][:, :], ALU.add)
                        ctr.inc(a, dv, 1)
            @blk.vector
            def _(v, prev=ctr.v(dv)):
                v.wait_ge(dv, prev)
                a = v.tensor_copy(S_bf[:, :], S_[:, :])
                ctr.inc(a, dv, 1)
            sind = ctr.v(dv)

            sb_free = {}
            gev = {}
            eexp_ms = {}
            ostm_prev = None
            for scc in range(8):
                @blk.sync
                def _(s, scc=scc):
                    if scc == 0:
                        s.wait_ge(gsp[0], f2_gsp[0])
                        s.wait_ge(gsp[1], f2_gsp[1])
                        s.wait_ge(dv, sind)
                    if scc >= 1:
                        s.wait_ge(dv, sb_free[scc - 1])
                    rr = sc_xbc.rearrange("(pt p) t -> p pt t", p=128)
                    sl = slice(scc * TC, (scc + 1) * TC)
                    ctr.inc(s.dma_start(out=Xb.rearrange("p (pt t) -> p pt t", t=TC),
                                        in_=rr[:, 0:NPI, sl]), sxb, 16)
                sbl = ctr.v(sxb)

                for c in range(4):
                    cg = scc * 4 + c
                    t0 = c * 128

                    @blk.sync
                    def _(s, scc=scc, cg=cg, t0=t0):
                        if cg >= 2:
                            s.wait_ge(pe, bc_free[cg - 2])
                        rr = sc_xbc.rearrange("(pt p) t -> p pt t", p=128)
                        csl = slice(scc * TC + t0, scc * TC + t0 + 128)
                        ctr.inc(s.dma_start(
                            out=Bbc[cg % 2].rearrange("p (g t) -> p g t", t=128),
                            in_=rr[:, NPI:NPI + NG, csl]), bcp[cg % 2], 16)
                        ctr.inc(s.dma_start(
                            out=Cbc[cg % 2].rearrange("p (g t) -> p g t", t=128),
                            in_=rr[:, NPI + NG:NPX, csl]), bcp[cg % 2], 16)
                    bcl = ctr.v(bcp[cg % 2])
                    bc_free = bc_free if cg > 0 else {}

                    @blk.vector
                    def _(v, cg=cg, scc=scc):
                        if cg == 0:
                            v.wait_ge(dv, sind)
                        v.tensor_scalar(sml[:, 0:128],
                                        p_cdtp[:, cg * 128 + 1:cg * 128 + 129],
                                        p_cdtp[:, (cg + 1) * 128:(cg + 1) * 128 + 1],
                                        p_expA[:, :], ALU.subtract, ALU.mult)
                        v.tensor_scalar(sml[:, 128:256],
                                        p_cdtp[:, cg * 128 + 1:cg * 128 + 129],
                                        p_cdtp[:, cg * 128:cg * 128 + 1],
                                        p_expA[:, :], ALU.subtract, ALU.mult)
                        a = v.tensor_scalar(sml[:, 256:384],
                                            p_cdtp[:, cg * 128 + 1:cg * 128 + 129],
                                            p_expA[:, :], -1.0, ALU.mult, ALU.mult)
                        ctr.inc(a, dv, 1)
                    sm1 = ctr.v(dv)

                    @blk.scalar
                    def _(s, cg=cg, sm1=sm1):
                        s.wait_ge(dv, sm1)
                        s.activation(sml[:, 0:128], sml[:, 0:128], AF.Exp)
                        s.activation(sml[:, 128:256], sml[:, 128:256], AF.Exp, scale=-1.0)
                        a = s.activation(sml[:, 384:512], p_dt[:, cg * 128:(cg + 1) * 128],
                                         AF.Ln)
                        ctr.inc(a, ac, 1)
                    sm2 = ctr.v(ac)

                    @blk.vector
                    def _(v, cg=cg, sm2=sm2):
                        v.wait_ge(ac, sm2)
                        v.tensor_tensor(sml[:, 0:128], sml[:, 0:128],
                                        p_dt[:, cg * 128:(cg + 1) * 128], ALU.mult)
                        a = v.tensor_tensor(sml[:, 384:512], sml[:, 384:512],
                                            sml[:, 256:384], ALU.subtract)
                        ctr.inc(a, dv, 1)
                    sm3 = ctr.v(dv)

                    pt2_ev = pt2_ev if cg > 0 else {0: None, 1: None}
                    for wv in range(5):
                        @blk.tensor
                        def _(t, wv=wv, c=c, cg=cg, sbl=sbl, sm3=sm3, bcl=bcl):
                            if wv == 0:
                                t.wait_ge(sxb, sbl)
                                t.wait_ge(bcp[cg % 2], bcl)
                                if cg > 0:
                                    t.wait_ge(dv, xt_cons)
                            if pt2_ev[wv % 2] is not None:
                                t.wait_ge(dv, pt2_ev[wv % 2])
                            p = pT2[wv % 2]
                            for i in range(8):
                                n = wv * 8 + i
                                if n < 32:
                                    src = Xb[:, n * TC + t0: n * TC + t0 + 128]
                                else:
                                    src = Bbc[cg % 2][:, (n - 32) * 128:(n - 31) * 128]
                                mm = t.transpose(p[:, i * 128:(i + 1) * 128], src,
                                                 k_id_bf[:, :])
                            ctr.inc(mm, pe, 1)
                        twd = ctr.v(pe)

                        @blk.vector
                        def _(v, wv=wv, twd=twd):
                            v.wait_ge(pe, twd)
                            p = pT2[wv % 2]
                            for i in range(8):
                                n = wv * 8 + i
                                dst = (XT2[:, n * 128:(n + 1) * 128] if n < 32
                                       else BT2[:, (n - 32) * 128:(n - 31) * 128])
                                a = v.tensor_copy(dst, p[:, i * 128:(i + 1) * 128])
                            ctr.inc(a, dv, 1)
                        pt2_ev[wv % 2] = ctr.v(dv)
                        if wv == 4:
                            xbt_done = ctr.v(dv)

                    @blk.tensor
                    def _(t, cg=cg, sm3=sm3):
                        t.wait_ge(dv, sm3)
                        if cg > 0:
                            t.wait_ge(dv, gev[(cg - 1, 7)])
                        id64 = k_id_f32[0:64, 0:64]
                        t.transpose(pE2[:, 0:64], sml[:, 0:128], id64)
                        t.transpose(pE2[:, 64:128], sml[:, 128:256], id64)
                        t.transpose(pE2[:, 128:192], sml[:, 256:384], id64)
                        mm = t.transpose(pE2[:, 192:256], sml[:, 384:512], id64)
                        ctr.inc(mm, pe, 1)
                    smtd = ctr.v(pe)

                    acb = ctr.v(ac)
                    smev = acb + 1
                    for blkh in range(16):
                        eexp_ms[(cg, blkh)] = acb + 2 + blkh
                    gevac = acb + 18
                    ald = acb + 19

                    ebd_list = []

                    @blk.tensor
                    def _(t, cg=cg, smev=smev):
                        t.wait_ge(ac, smev)
                        if cg > 0:
                            t.wait_ge(dv, gev[(cg - 1, 7)])
                            t.wait_ge(dv, gev[(cg - 1, 6)])
                        for blkh in range(16):
                            if blkh >= 2:
                                t.wait_ge(ac, eexp_ms[(cg, blkh - 2)])
                            p = pA[blkh % 2]
                            for hh in range(4):
                                h = blkh * 4 + hh
                                t.matmul(p[:, hh * 128:(hh + 1) * 128],
                                         smT[:, 128 + h:129 + h].broadcast_to([128, 128]),
                                         k_id_f32[:, :], is_transpose=True,
                                         start=(hh == 0), stop=False)
                            mm = t.matmul(p[:, :], k_negbigI[:, :],
                                          k_triL[:, :].unsqueeze(1).broadcast_to(
                                              [128, 4, 128]),
                                          start=False, stop=True)
                            ctr.inc(mm, pe, 1)
                            ebd_list.append(ctr.v(pe))

                    @blk.tensor
                    def _(t, c=c, t0=t0, cg=cg):
                        if cg > 0:
                            t.wait_ge(dv, gev[(cg - 1, 7)])
                        for g_ in range(NG):
                            p = pB[g_ // 4]
                            mm = t.matmul(p[:, (g_ % 4) * 128:(g_ % 4) * 128 + 128],
                                          Bbc[cg % 2][:, g_ * 128:(g_ + 1) * 128],
                                          Cbc[cg % 2][:, g_ * 128:(g_ + 1) * 128],
                                          start=True, stop=True)
                        ctr.inc(mm, pe, 1)
                    gmd = ctr.v(pe)

                    @blk.vector
                    def _(v, smev=smev):
                        v.wait_ge(ac, smev)
                        a = v.tensor_copy(wT2[:, :], smT[:, 0:64])
                        ctr.inc(a, dv, 1)
                    wtc = ctr.v(dv)

                    @blk.vector
                    def _(v, gevac=gevac, cg=cg, wtc=wtc, xbt_done=xbt_done):
                        v.wait_ge(ac, gevac)
                        v.wait_ge(ac, eexp_ms[(cg, 15)])
                        v.wait_ge(dv, xbt_done)
                        v.wait_ge(dv, wtc)
                        v.tensor_tensor(
                            Mp[:, :].rearrange("p (g hh i) -> p g hh i", g=NG, hh=8),
                            Eb[:, :].rearrange("p (g hh i) -> p g hh i", g=NG, hh=8),
                            Gb[:, :].rearrange("p (g i) -> p g i", g=NG).unsqueeze(2)
                            .broadcast_to([128, NG, 8, 128]), ALU.mult)
                        a = v.tensor_tensor(
                            XwT2[:, :].rearrange("p (h d) -> p h d", d=64),
                            XT2[:, :].rearrange("p (h d) -> p h d", d=64),
                            wT2[:, :].unsqueeze(2).broadcast_to([128, 64, 64]), ALU.mult)
                        ctr.inc(a, dv, 1)
                    mpd = ctr.v(dv)

                    @blk.tensor
                    def _(t, cg=cg, mpd=mpd):
                        t.wait_ge(dv, mpd)
                        t.wait_ge(ac, dtmath_ac)
                        mm = t.transpose(pE2[:, 256:320],
                                         p_ach[:, cg:cg + 1].broadcast_to([64, 128]),
                                         k_id_f32[0:64, 0:64])
                        ctr.inc(mm, pe, 1)
                    alm = ctr.v(pe)

                    @blk.scalar
                    def _(s, cg=cg, smtd=smtd, gmd=gmd, alm=alm, ebd_list=ebd_list):
                        s.wait_ge(pe, smtd)
                        ctr.inc(s.copy(smT[:, :], pE2[:, 0:256]), ac, 1)
                        for blkh in range(16):
                            s.wait_ge(pe, ebd_list[blkh])
                            p = pA[blkh % 2]
                            for hh in range(4):
                                h = blkh * 4 + hh
                                a = s.activation(Eb[:, h * 128:(h + 1) * 128],
                                                 p[:, hh * 128:(hh + 1) * 128], AF.Exp,
                                                 bias=smT[:, 192 + h:193 + h])
                            ctr.inc(a, ac, 1)
                        s.wait_ge(pe, gmd)
                        s.copy(Gb[:, 0:512], pB[0][:, :])
                        ctr.inc(s.copy(Gb[:, 512:1024], pB[1][:, :]), ac, 1)
                        s.wait_ge(pe, alm)
                        ctr.inc(s.copy(alr[:, :], pE2[:, 256:320]), ac, 1)
                        assert ctr.v(ac) == cg * 0 + ald, (ctr.v(ac), ald)

                    for g_ in range(NG):
                        pCb = pCx if g_ % 2 == 0 else pE2

                        @blk.tensor
                        def _(t, g_=g_, cg=cg, t0=t0, mpd=mpd, ald=ald):
                            if g_ == 0:
                                t.wait_ge(dv, mpd)
                            if g_ == 1:
                                t.wait_ge(ac, ald)
                            if g_ >= 2:
                                t.wait_ge(dv, gev[(cg, g_ - 2)])
                            par = g_ % 2
                            for hh in range(8):
                                h = g_ * 8 + hh
                                t.matmul(pA[par][:, hh * 64:(hh + 1) * 64],
                                         Mp[:, h * 128:(h + 1) * 128],
                                         XT2[:, h * 64:(h + 1) * 64],
                                         start=True, stop=True)
                            t.matmul(pB[par][:, :],
                                     Cbc[cg % 2][:, g_ * 128:(g_ + 1) * 128],
                                     S_bf[:, g_ * 512:(g_ + 1) * 512], start=True, stop=True)
                            mm = t.matmul((pCx if g_ % 2 == 0 else pE2)[:, :],
                                          BT2[:, g_ * 128:(g_ + 1) * 128],
                                          XwT2[:, g_ * 512:(g_ + 1) * 512],
                                          start=True, stop=True)
                            ctr.inc(mm, pe, 1)
                        gmm = ctr.v(pe)

                        @blk.gpsimd
                        def _(g, g_=g_, gmm=gmm, ald=ald):
                            g.wait_ge(pe, gmm)
                            g.wait_ge(ac, ald)
                            a = g.tensor_tensor(
                                stp[g_ % 2][:, :].rearrange("p (hh d) -> p hh d", d=64),
                                S_[:, g_ * 512:(g_ + 1) * 512].rearrange(
                                    "p (hh d) -> p hh d", d=64),
                                alr[:, g_ * 8:(g_ + 1) * 8].unsqueeze(2).broadcast_to(
                                    [128, 8, 64]), ALU.mult)
                            ctr.inc(a, pl, 1)
                        spd = ctr.v(pl)

                        @blk.vector
                        def _(v, g_=g_, cg=cg, gmm=gmm, spd=spd):
                            v.wait_ge(pe, gmm)
                            par = g_ % 2
                            bar(v, dv, v.tensor_tensor(
                                Ych[:, g_ * 512:(g_ + 1) * 512].rearrange(
                                    "p (hh d) -> p hh d", d=64),
                                pB[par][:, :].rearrange("p (hh d) -> p hh d", d=64),
                                smT[:, 64 + g_ * 8:64 + (g_ + 1) * 8].unsqueeze(2)
                                .broadcast_to([128, 8, 64]), ALU.mult))
                            v.tensor_tensor(Ych[:, g_ * 512:(g_ + 1) * 512],
                                            pA[par][:, :], Ych[:, g_ * 512:(g_ + 1) * 512],
                                            ALU.add)
                            v.wait_ge(pl, spd)
                            bar(v, dv, v.tensor_tensor(S_[:, g_ * 512:(g_ + 1) * 512],
                                                       stp[g_ % 2][:, :],
                                                       (pCx if g_ % 2 == 0 else pE2)[:, :],
                                                       ALU.add))
                            a = v.tensor_copy(S_bf[:, g_ * 512:(g_ + 1) * 512],
                                              S_[:, g_ * 512:(g_ + 1) * 512])
                            ctr.inc(a, dv, 1)
                        gev[(cg, g_)] = ctr.v(dv)
                    xt_cons = gev[(cg, 7)]
                    bc_free[cg] = ctr.v(pe)

                    for wv in range(4):
                        @blk.tensor
                        def _(t, wv=wv, cg=cg):
                            if wv == 0:
                                t.wait_ge(dv, gev[(cg, 7)])
                            if pt2_ev[wv % 2] is not None:
                                t.wait_ge(dv, pt2_ev[wv % 2])
                            p = pT2[wv % 2]
                            for i in range(8):
                                pt = wv * 8 + i
                                mm = t.transpose(p[:, i * 128:(i + 1) * 128],
                                                 Ych[:, pt * 128:(pt + 1) * 128],
                                                 k_id_bf[:, :])
                            ctr.inc(mm, pe, 1)
                        ytd = ctr.v(pe)

                        @blk.vector
                        def _(v, wv=wv, c=c, ytd=ytd):
                            v.wait_ge(pe, ytd)
                            p = pT2[wv % 2]
                            for i in range(8):
                                pt = wv * 8 + i
                                a = v.tensor_copy(
                                    scout[:, pt * TC + c * 128: pt * TC + c * 128 + 128],
                                    p[:, i * 128:(i + 1) * 128])
                            ctr.inc(a, dv, 1)
                        pt2_ev[wv % 2] = ctr.v(dv)
                ychd = ctr.v(dv)

                gp_dv = []
                for pt in range(NPI):
                    @blk.sync
                    def _(s, pt=pt, scc=scc):
                        if pt >= 2:
                            s.wait_ge(dv, gp_dv[pt - 2])
                        ctr.inc(s.dma_start(
                            out=gabp[pt % 2][:, :],
                            in_=sc_gate[pt * 128:(pt + 1) * 128, scc * TC:(scc + 1) * TC]),
                            gbp[pt % 2], 16)
                    gld = ctr.v(gbp[pt % 2])

                    @blk.vector
                    def _(v, pt=pt, gld=gld, ychd=ychd, sbl=sbl):
                        if pt == 0:
                            v.wait_ge(dv, ychd)
                            v.wait_ge(sxb, sbl)
                        v.wait_ge(gbp[pt % 2], gld)
                        bar(v, dv, v.scalar_tensor_tensor(stp[pt % 2][:, :],
                                                          Xb[:, pt * TC:(pt + 1) * TC],
                                                          k_dexp[:, pt:pt + 1],
                                                          scout[:, pt * TC:(pt + 1) * TC],
                                                          ALU.mult, ALU.add))
                        a = v.tensor_tensor(scout[:, pt * TC:(pt + 1) * TC],
                                            stp[pt % 2][:, :], gabp[pt % 2][:, :],
                                            ALU.mult)
                        ctr.inc(a, dv, 1)
                    gp_dv.append(ctr.v(dv))
                gpd = ctr.v(dv)

                acb2 = ctr.v(ac)
                peb2 = ctr.v(pe)
                sq_ms = {pt: peb2 + pt + 1 for pt in range(NPI)}
                sqa_pl = {pt: acb2 + pt + 1 for pt in range(NPI)}

                @blk.scalar
                def _(s, gpd=gpd):
                    s.wait_ge(dv, gpd)
                    for pt in range(NPI):
                        if pt >= 4:
                            s.wait_ge(pe, sq_ms[pt - 4])
                        a = s.activation(XwT2[:, (pt % 4) * 512:(pt % 4 + 1) * 512],
                                         scout[:, pt * TC:(pt + 1) * TC], AF.Square)
                        ctr.inc(a, ac, 1)
                        assert ctr.v(ac) == sqa_pl[pt]

                @blk.tensor
                def _(t, scc=scc):
                    if scc > 0:
                        t.wait_ge(ac, rbe_prev)
                    for pt in range(NPI):
                        t.wait_ge(ac, sqa_pl[pt])
                        mm = t.matmul(pB[0][0:1, 0:TC], k_onec[:, :],
                                      XwT2[:, (pt % 4) * 512:(pt % 4 + 1) * 512],
                                      start=(pt == 0), stop=(pt == NPI - 1))
                        ctr.inc(mm, pe, 1)
                        assert ctr.v(pe) == sq_ms[pt]
                ssqd = ctr.v(pe)

                @blk.vector
                def _(v, ssqd=ssqd):
                    v.wait_ge(pe, ssqd)
                    a = v.tensor_scalar_mul(rmsr[0:1, 0:TC], pB[0][0:1, 0:TC], 1.0 / INTER)
                    ctr.inc(a, dv, 1)
                r1 = ctr.v(dv)

                @blk.scalar
                def _(s, r1=r1):
                    s.wait_ge(dv, r1)
                    bar(s, ac, s.activation(rmsr[0:1, TC:2 * TC], rmsr[0:1, 0:TC],
                                            AF.Ln, bias=EPS))
                    a = s.activation(rmsr[0:1, 0:TC], rmsr[0:1, TC:2 * TC], AF.Exp,
                                     scale=-0.5)
                    ctr.inc(a, ac, 1)
                r2 = ctr.v(ac)

                @blk.vector
                def _(v, r2=r2):
                    v.wait_ge(ac, r2)
                    bar(v, dv, v.tensor_copy(rmsb[0:1, 0:TC], rmsr[0:1, 0:TC]))
                    bar(v, dv, v.tensor_copy(rmsr[0:1, TC:2 * TC], rmsb[0:1, 0:TC]))
                    a = v.tensor_tensor(rmsb[0:1, TC:2 * TC], rmsr[0:1, 0:TC],
                                        rmsr[0:1, TC:2 * TC], ALU.subtract)
                    ctr.inc(a, dv, 1)
                r3 = ctr.v(dv)

                @blk.tensor
                def _(t, r3=r3):
                    t.wait_ge(dv, r3)
                    t.matmul(pB[1][:, 0:TC], k_oner[:, :], rmsb[0:1, 0:TC],
                             start=True, stop=False)
                    mm = t.matmul(pB[1][:, 0:TC], k_oner[:, :], rmsb[0:1, TC:2 * TC],
                                  start=False, stop=True)
                    ctr.inc(mm, pe, 1)
                r4 = ctr.v(pe)

                @blk.scalar
                def _(s, r4=r4):
                    s.wait_ge(pe, r4)
                    a = s.copy(rstdb[:, :], pB[1][:, 0:TC])
                    ctr.inc(a, ac, 1)
                rbe = ctr.v(ac)
                rbe_prev = rbe

                @blk.vector
                def _(v, rbe=rbe):
                    v.wait_ge(ac, rbe)
                    for pt in range(NPI):
                        a = v.scalar_tensor_tensor(Xb[:, pt * TC:(pt + 1) * TC],
                                                   rstdb[:, :], k_nw[:, pt:pt + 1],
                                                   scout[:, pt * TC:(pt + 1) * TC],
                                                   ALU.mult, ALU.mult)
                    ctr.inc(a, dv, 1)
                nrm = ctr.v(dv)

                wo_ld = {}
                wo_mm = {}
                wo_mmh = {}
                wo_ev = {}
                out_dv = {}

                @blk.gpsimd
                def _(g, scc=scc):
                    g.wait_ge(pe, f2_pe)
                    for hf in range(2):
                        if scc > 0:
                            g.wait_ge(pe, wo_mmh_prev[(15, hf)])
                        ctr.inc(g.dma_start(
                            out=wb3[hf][:, :].rearrange("p (kt m) -> p kt m", m=128),
                            in_=wout[0, hf * 16:(hf + 1) * 16].transpose([1, 0, 2])),
                            swp[hf], 16)
                        wo_ld[(0, hf)] = ctr.v(swp[hf])

                acb3 = ctr.v(ac)
                dvb3 = ctr.v(dv)
                fnb = [ctr.v(fnp[0]), ctr.v(fnp[1])]
                osb = [ctr.v(osp[0]), ctr.v(osp[1])]
                fld_pl = {}
                for mt in range(NPD):
                    wo_ev[mt] = acb3 + mt + 1
                    out_dv[mt] = dvb3 + 2 * (mt + 1)
                    fld_pl[mt] = fnb[mt % 2] + 32 * (mt // 2 + 1)
                ostm_new = {mt: osb[mt % 2] + 16 * (mt // 2 + 1) for mt in range(NPD)}

                for mt in range(NPD):
                    for hf in range(2):
                        @blk.tensor
                        def _(t, mt=mt, hf=hf, nrm=nrm, scc=scc):
                            if mt == 0 and hf == 0:
                                t.wait_ge(dv, nrm)
                            t.wait_ge(swp[hf], wo_ld[(mt, hf)])
                            if hf == 0 and mt >= 2:
                                t.wait_ge(ac, wo_ev[mt - 2])
                            for k2 in range(16):
                                kt = hf * 16 + k2
                                mm = t.matmul(pA[mt % 2][:, :],
                                              wb3[hf][:, k2 * 128:(k2 + 1) * 128],
                                              Xb[:, kt * TC:(kt + 1) * TC],
                                              start=(kt == 0), stop=(kt == NPI - 1))
                            ctr.inc(mm, pe, 1)
                            wo_mmh[(mt, hf)] = ctr.v(pe)
                        wo_mm[mt] = ctr.v(pe)

                        nxt = (mt + 1, hf)
                        if nxt[0] < NPD:
                            @blk.gpsimd
                            def _(g, mt=mt, hf=hf, nxt=nxt):
                                g.wait_ge(pe, wo_mmh[(mt, hf)])
                                ctr.inc(g.dma_start(
                                    out=wb3[nxt[1]][:, :].rearrange(
                                        "p (kt m) -> p kt m", m=128),
                                    in_=wout[nxt[0], nxt[1] * 16:(nxt[1] + 1) * 16].transpose([1, 0, 2])),
                                    swp[nxt[1]], 16)
                                wo_ld[nxt] = ctr.v(swp[nxt[1]])

                @blk.sync
                def _(s, scc=scc, nrm=nrm):
                    for mt in range(NPD):
                        if mt >= 2:
                            s.wait_ge(dv, out_dv[mt - 2])
                        elif scc > 0:
                            s.wait_ge(dv, nrm)
                        ctr.inc(s.dma_start(
                            out=gfbp[mt % 2][:, :],
                            in_=sc_g[mt * 128:(mt + 1) * 128, scc * TC:(scc + 1) * TC]),
                            fnp[mt % 2], 16)
                        ctr.inc(s.dma_start(
                            out=xfbp[mt % 2][:, :],
                            in_=xT.rearrange("(pt p) t -> pt p t", p=128)[
                                mt, :, scc * TC:(scc + 1) * TC]), fnp[mt % 2], 16)
                        assert ctr.v(fnp[mt % 2]) == fld_pl[mt]

                @blk.scalar
                def _(s, scc=scc):
                    for mt in range(NPD):
                        s.wait_ge(pe, wo_mm[mt])
                        if mt >= 2:
                            s.wait_ge(dv, out_dv[mt - 2])
                        a = s.copy(sstg[mt % 2][:, :], pA[mt % 2][:, :])
                        ctr.inc(a, ac, 1)
                        assert ctr.v(ac) == wo_ev[mt]
                        if mt >= 2:
                            ctr.inc(s.dma_start(
                                out=outT[(mt - 2) * 128:(mt - 1) * 128,
                                         scc * TC:(scc + 1) * TC],
                                in_=ostg2[mt % 2][:, :]), osp[mt % 2], 16)
                            assert ctr.v(osp[mt % 2]) == ostm_new[mt - 2]
                    for mt in (NPD - 2, NPD - 1):
                        s.wait_ge(dv, out_dv[mt])
                        ctr.inc(s.dma_start(
                            out=outT[mt * 128:(mt + 1) * 128, scc * TC:(scc + 1) * TC],
                            in_=ostg2[mt % 2][:, :]), osp[mt % 2], 16)
                        assert ctr.v(osp[mt % 2]) == ostm_new[mt]

                @blk.vector
                def _(v, scc=scc, ostm_prev=ostm_prev):
                    for mt in range(NPD):
                        v.wait_ge(ac, wo_ev[mt])
                        v.wait_ge(fnp[mt % 2], fld_pl[mt])
                        if mt >= 2:
                            v.wait_ge(osp[mt % 2], ostm_new[mt - 2])
                        elif scc > 0:
                            v.wait_ge(osp[mt % 2], ostm_prev[14 + mt])
                        bar(v, dv, v.tensor_tensor(sstg[mt % 2][:, :], sstg[mt % 2][:, :],
                                                   gfbp[mt % 2][:, :], ALU.mult))
                        a = v.tensor_tensor(ostg2[mt % 2][:, :], xfbp[mt % 2][:, :],
                                            sstg[mt % 2][:, :], ALU.add)
                        ctr.inc(a, dv, 1)
                        assert ctr.v(dv) == out_dv[mt]
                ostm_prev = ostm_new
                wo_mmh_prev = wo_mmh
                sb_free[scc] = out_dv[NPD - 1]

            fin0, fin1 = ctr.v(osp[0]), ctr.v(osp[1])

            @blk.scalar
            def _(s):
                s.wait_ge(osp[0], fin0)
                s.wait_ge(osp[1], fin1)

    return nc


def _prep_weights(i):
    w = {}
    Wdt = np.zeros((D, 128), np.float32)
    Wdt[:, 0:64] = i["L0_attn_Wdt"]
    blocks = [Wdt]
    for m in range(NPI):
        blocks.append(i["L0_attn_Wx"][:, m * 128:(m + 1) * 128])
    for m in range(NG):
        blocks.append(i["L0_attn_WB"][:, m * 128:(m + 1) * 128])
    for m in range(NG):
        blocks.append(i["L0_attn_WC"][:, m * 128:(m + 1) * 128])
    for m in range(NPI):
        blocks.append(i["L0_attn_Wz"][:, m * 128:(m + 1) * 128])
    w["wproj"] = np.ascontiguousarray(
        np.stack([b.reshape(NKD, 128, 128) for b in blocks]))
    w["wout"] = np.ascontiguousarray(
        i["L0_attn_Wout"].reshape(NPI, 128, NPD, 128).transpose(2, 0, 1, 3))
    w["wup"] = np.ascontiguousarray(
        i["L0_ffn_up_weight"].reshape(NPH, 128, NKD, 128).transpose(0, 2, 3, 1))
    w["wdn"] = np.ascontiguousarray(
        i["L0_ffn_down_weight"].reshape(NPD, 128, NPH, 128).transpose(0, 2, 3, 1))
    w["convw"] = np.ascontiguousarray(
        i["L0_attn_conv_w"].reshape(NPX, 128, KC).transpose(1, 0, 2).reshape(128, NPX * KC))
    w["convb"] = np.ascontiguousarray(i["L0_attn_conv_b"].reshape(NPX, 128).T)
    w["dtb"] = np.ascontiguousarray(i["L0_attn_dt_bias"].reshape(64, 1))
    w["alog"] = np.ascontiguousarray(i["L0_attn_A_log"].reshape(64, 1))
    w["dexp"] = np.ascontiguousarray(np.repeat(i["L0_attn_D"], HD).reshape(NPI, 128).T)
    w["nw"] = np.ascontiguousarray(i["L0_attn_norm_w"].reshape(NPI, 128).T)
    w["upsc"] = np.ascontiguousarray(i["L0_ffn_up_scale"].reshape(NPH, 128).T)
    w["dnsc"] = np.ascontiguousarray(i["L0_ffn_down_scale"].reshape(NPD, 128).T)
    w["oscr"] = np.ascontiguousarray(
        np.broadcast_to(i["L0_ffn_output_scale"].reshape(1, 1), (128, 1)))
    w["id_bf"] = np.eye(128, dtype=BF)
    w["id_f32"] = np.eye(128, dtype=np.float32)
    w["onec_bf"] = np.ones((128, 1), BF)
    w["oner_bf"] = np.ones((1, 128), BF)
    w["oner_f32"] = np.ones((1, 128), np.float32)
    w["negbigI"] = (NEGBIG * np.eye(128)).astype(BF)
    w["triL"] = np.tril(np.ones((128, 128)), -1).astype(BF)
    w["zcol"] = np.zeros((128, 1), np.float32)
    w["epscol"] = np.full((128, 1), EPS, np.float32)
    return w


def kernel(**inputs):
    inputs = {k: np.asarray(v, dtype=np.float32) for k, v in inputs.items()}
    x = inputs["L0_attn_hidden_states"]
    w = _prep_weights(inputs)

    nc = build_program()
    in_maps = []
    for c in range(8):
        b, s = c // 4, c % 4
        xTc = np.ascontiguousarray(x[b, s * T:(s + 1) * T, :].T)
        if s == 0:
            halo = np.zeros((D, 3), np.float32)
        else:
            halo = np.ascontiguousarray(x[b, s * T - 3:s * T, :].T)
        cm = [1.0 if k < s else 0.0 for k in range(3)]
        cf01 = 1.0 if s >= 2 else 0.0
        cf02 = 1.0 if s >= 3 else 0.0
        cf12 = 1.0 if s >= 3 else 0.0
        chain = np.array([[cm[0], cm[1], cm[2], cf01, 1 - cf01,
                           cf02, 1 - cf02, cf12, 1 - cf12]], np.float32)
        m = dict(w)
        m["xT"] = xTc
        m["xTh"] = halo
        m["chain"] = chain
        in_maps.append(m)

    res = run_bass_kernel_spmd(nc, in_maps, core_ids=list(range(8)))
    out = np.empty((2, 16384, D), np.float32)
    for c in range(8):
        b, s = c // 4, c % 4
        out[b, s * T:(s + 1) * T, :] = res.results[c]["outT"].T
    return out
